# version 02
# baseline (speedup 1.0000x reference)
import numpy as np
import ml_dtypes
import concourse.bass as bass
import concourse.mybir as mybir
from concourse.bass_utils import run_bass_kernel_spmd

F32 = mybir.dt.float32
BF16 = mybir.dt.bfloat16
I32 = mybir.dt.int32
AF = mybir.ActivationFunctionType
ALU = mybir.AluOpType

NCORE = 8
B, T, D = 2, 8192, 1024
TC = 2048
NDT = D // 128
TN = 512
NTN = TC // TN
CH = 128
NCH = TC // CH
MAGIC = float(3 * 2.0 ** 22)

DI = 2048
NH1 = 32
NST = 32
DH1 = 64
C1 = 256
CL = 16
HS = 1024

ENGINES = ("T", "V", "S", "G", "Y")
ENG_ATTR = {"T": "tensor", "V": "vector", "S": "scalar", "G": "gpsimd", "Y": "sync"}


class Prog:

    def __init__(self, nc):
        self.nc = nc
        self.q = {e: [] for e in ENGINES}
        self.cnt = {e: 0 for e in ENGINES}
        self.dma_cnt = {e: 0 for e in ENGINES}
        self.last_write = {}
        self.readers = {}
        self.known = {e: {} for e in ENGINES}
        self.sems = {}

    def _sem(self, kind, e):
        key = (kind, e)
        if key not in self.sems:
            self.sems[key] = self.nc.semaphore(f"sem_{kind}_{e}").__enter__()
        return self.sems[key]

    def _wait(self, e, dep):
        kind, de, val = dep
        if kind == "c" and de == e:
            pass
        k = (kind, de)
        if self.known[e].get(k, 0) >= val:
            return
        self.known[e][k] = val
        sem = self._sem(kind, de)
        self.q[e].append(("wait", sem, val))

    def _deps_for_read(self, e, names):
        for n in names:
            d = self.last_write.get(n)
            if d is not None:
                self._wait(e, d)

    def _deps_for_write(self, e, name):
        d = self.last_write.get(name)
        if d is not None:
            self._wait(e, d)
        for r in self.readers.get(name, ()):
            self._wait(e, r)

    @staticmethod
    def _nm(ap):
        return ap.name if hasattr(ap, "name") else None

    def op(self, e, method, out, ins, *args, _opname=None, **kwargs):
        outs = out if isinstance(out, (list, tuple)) else [out]
        in_names = [self._nm(a) for a in ins if self._nm(a)]
        out_names = [self._nm(a) for a in outs if self._nm(a)]
        self._deps_for_read(e, in_names)
        for on in out_names:
            self._deps_for_write(e, on)
        self.cnt[e] += 1
        cval = self.cnt[e]
        sem = self._sem("c", e)
        self.q[e].append(("op", method, args, kwargs, sem))
        for on in out_names:
            self.last_write[on] = ("c", e, cval)
            self.readers[on] = []
        for n in in_names:
            self.readers.setdefault(n, []).append(("c", e, cval))

    def dma(self, e, out, in_):
        names_in = [self._nm(in_)] if self._nm(in_) else []
        self._deps_for_read(e, names_in)
        on = self._nm(out)
        if on:
            self._deps_for_write(e, on)
        if self.dma_cnt[e]:
            self._wait(e, ("d", e, 16 * self.dma_cnt[e]))
        self.dma_cnt[e] += 1
        dval = 16 * self.dma_cnt[e]
        sem = self._sem("d", e)
        self.q[e].append(("dma", out, in_, sem))
        if on:
            self.last_write[on] = ("d", e, dval)
            self.readers[on] = []
        if names_in:
            self.readers.setdefault(names_in[0], []).append(("d", e, dval))

    def collective(self, in_ap, out_ap, in_name, out_name):
        e = "G"
        self._deps_for_read(e, [in_name])
        self._deps_for_write(e, out_name)
        if self.dma_cnt[e]:
            self._wait(e, ("d", e, 16 * self.dma_cnt[e]))
        self.dma_cnt[e] += 1
        dval = 16 * self.dma_cnt[e]
        sem = self._sem("d", e)
        self.q[e].append(("cc", in_ap, out_ap, sem))
        self.last_write[out_name] = ("d", e, dval)
        self.readers[out_name] = []
        self.readers.setdefault(in_name, []).append(("d", e, dval))

    def barrier_all(self):
        for e in ENGINES:
            for ee in ENGINES:
                if self.cnt[ee]:
                    self._wait(e, ("c", ee, self.cnt[ee]))
                if self.dma_cnt[ee]:
                    self._wait(e, ("d", ee, 16 * self.dma_cnt[ee]))

    def barrier_dma(self, e):
        if self.dma_cnt[e]:
            self._wait(e, ("d", e, 16 * self.dma_cnt[e]))

    def finish(self):
        e = "Y"
        for ee in ENGINES:
            if self.cnt[ee]:
                self._wait(e, ("c", ee, self.cnt[ee]))
            if self.dma_cnt[ee]:
                self._wait(e, ("d", ee, 16 * self.dma_cnt[ee]))

    def build(self):
        nc = self.nc
        with nc.Block() as block:
            for e in ENGINES:
                ops = self.q[e]
                if not ops:
                    continue

                def make(eng_ops):
                    def body(eng):
                        for it in eng_ops:
                            if it[0] == "wait":
                                eng.wait_ge(it[1], it[2])
                            elif it[0] == "op":
                                _, method, args, kwargs, sem = it
                                getattr(eng, method)(*args, **kwargs).then_inc(sem, 1)
                            elif it[0] == "cc":
                                _, in_ap, out_ap, sem = it
                                eng.collective_compute(
                                    "AllGather", mybir.AluOpType.bypass,
                                    replica_groups=[list(range(NCORE))],
                                    ins=[in_ap], outs=[out_ap]).then_inc(sem, 16)
                            else:
                                _, out, in_, sem = it
                                eng.dma_start(out=out, in_=in_).then_inc(sem, 16)
                    return body

                getattr(block, ENG_ATTR[e])(make(ops))



def emit_ln(p, X, H, sb, ps):
    nc = p.nc
    sq = sb["sq"]
    onesf = sb["onesf"]
    onesr = sb["onesr"]
    pstat = ps["P512a"]
    pbig = ps["P1024"]
    lnr = sb["lnrows"]
    mean = lnr[:, 0:TN]
    m2 = lnr[:, TN:2 * TN]
    rb = sb["rb"]
    nb = sb["nb"]
    tmp = sb["lntmp"]

    for tn in range(NTN):
        c0 = tn * TN
        for di in range(NDT):
            p.op("T", "matmul", pstat[0:1, :], [onesf, X[di]],
                 pstat[0:1, :], onesf[:, :], X[di][:, c0:c0 + TN],
                 start=(di == 0), stop=(di == NDT - 1))
        for di in range(NDT):
            sl = sq[:, (di % 2) * TN:(di % 2) * TN + TN]
            p.op("S", "activation", sl, [X[di]], sl, X[di][:, c0:c0 + TN], AF.Square)
            p.op("T", "matmul", pbig, [onesf, sq],
                 pbig[0:1, TN:2 * TN], onesf[:, :], sl,
                 start=(di == 0), stop=(di == NDT - 1))
        p.op("V", "tensor_single_scalar", lnr, [pstat], mean, pstat[0:1, :], 1.0 / D, ALU.mult)
        p.op("V", "tensor_single_scalar", lnr, [pbig], m2, pbig[0:1, TN:2 * TN], 1.0 / D, ALU.mult)
        p.op("V", "tensor_mul", rb, [lnr], rb[0:1, :], mean, mean)
        p.op("V", "tensor_sub", lnr, [lnr, rb], m2, m2, rb[0:1, :])
        p.op("S", "activation", lnr, [lnr], m2, m2, AF.Sqrt, bias=sb["epsr"][0:1, 0:1])
        p.op("V", "reciprocal", rb, [lnr], rb[0:1, :], m2)
        p.op("V", "scalar_tensor_tensor", lnr, [lnr, rb],
             m2, mean, -1.0, rb[0:1, :], ALU.mult, ALU.mult)
        p.op("T", "matmul", pbig[:, 0:TN], [onesr, rb],
             pbig[:, 0:TN], onesr[:, :], rb[0:1, :], start=True, stop=True)
        p.op("T", "matmul", pbig[:, TN:2 * TN], [onesr, lnr],
             pbig[:, TN:2 * TN], onesr[:, :], m2, start=True, stop=True)
        p.op("V", "tensor_copy", rb, [pbig], rb[:, :], pbig[:, 0:TN])
        p.op("V", "tensor_copy", nb, [pbig], nb[:, :], pbig[:, TN:2 * TN])
        for di in range(NDT):
            p.op("V", "tensor_mul", tmp, [X[di], rb], tmp[:, :], X[di][:, c0:c0 + TN], rb[:, :])
            p.op("V", "tensor_add", H[di], [tmp, nb], H[di][:, c0:c0 + TN], tmp[:, :], nb[:, :])


def emit_ln_quant(p, X, H, sb, ps):
    sq = sb["sq"]; onesf = sb["onesf"]; onesr = sb["onesr"]
    pstat = ps["P512a"]; pbig = ps["P1024"]
    lnr = sb["lnrows"]
    mean = lnr[:, 0:TN]
    m2 = lnr[:, TN:2 * TN]
    rb = sb["rb"]; nb = sb["nb"]; tmp = sb["lntmp"]
    scl = sb["scl"]
    for tn in range(NTN):
        c0 = tn * TN
        for di in range(NDT):
            p.op("T", "matmul", pstat[0:1, :], [onesf, X[di]],
                 pstat[0:1, :], onesf[:, :], X[di][:, c0:c0 + TN],
                 start=(di == 0), stop=(di == NDT - 1))
        for di in range(NDT):
            sl = sq[:, (di % 2) * TN:(di % 2) * TN + TN]
            p.op("S", "activation", sl, [X[di]], sl, X[di][:, c0:c0 + TN], AF.Square)
            p.op("T", "matmul", pbig, [onesf, sq],
                 pbig[0:1, TN:2 * TN], onesf[:, :], sl,
                 start=(di == 0), stop=(di == NDT - 1))
        p.op("V", "tensor_single_scalar", lnr, [pstat], mean, pstat[0:1, :], 1.0 / D, ALU.mult)
        p.op("V", "tensor_single_scalar", lnr, [pbig], m2, pbig[0:1, TN:2 * TN], 1.0 / D, ALU.mult)
        p.op("V", "tensor_mul", rb, [lnr], rb[0:1, :], mean, mean)
        p.op("V", "tensor_sub", lnr, [lnr, rb], m2, m2, rb[0:1, :])
        p.op("S", "activation", lnr, [lnr], m2, m2, AF.Sqrt, bias=sb["epsr"][0:1, 0:1])
        p.op("V", "reciprocal", rb, [lnr], rb[0:1, :], m2)
        p.op("V", "scalar_tensor_tensor", lnr, [lnr, rb],
             m2, mean, -1.0, rb[0:1, :], ALU.mult, ALU.mult)
        p.op("T", "matmul", pbig[:, 0:TN], [onesr, rb],
             pbig[:, 0:TN], onesr[:, :], rb[0:1, :], start=True, stop=True)
        p.op("T", "matmul", pbig[:, TN:2 * TN], [onesr, lnr],
             pbig[:, TN:2 * TN], onesr[:, :], m2, start=True, stop=True)
        p.op("V", "tensor_copy", rb, [pbig], rb[:, :], pbig[:, 0:TN])
        p.op("V", "tensor_copy", nb, [pbig], nb[:, :], pbig[:, TN:2 * TN])
        for di in range(NDT):
            p.op("V", "tensor_mul", tmp, [X[di], rb], tmp[:, :], X[di][:, c0:c0 + TN], rb[:, :])
            p.op("V", "tensor_add", tmp, [tmp, nb], tmp[:, :], tmp[:, :], nb[:, :])
            p.op("V", "tensor_scalar", tmp, [tmp, scl], tmp[:, :], tmp[:, :],
                 scl[:, 1:2], 127.0, ALU.mult, ALU.min)
            p.op("V", "tensor_scalar", tmp, [tmp], tmp[:, :], tmp[:, :],
                 -127.0, MAGIC, ALU.max, ALU.add)
            p.op("V", "tensor_single_scalar", H[di], [tmp], H[di][:, c0:c0 + TN],
                 tmp[:, :], MAGIC, ALU.subtract)


def emit_l0ffn(p, X, H, W, DSH, GSM, sb, ps):
    pool_ps = ps["P512a"]
    pooled = sb["pooled"]
    po = ps["P512a"]
    for tn in range(NTN):
        c0 = tn * TN
        for di in range(NDT):
            p.op("T", "matmul", pool_ps[0:64, :], [GSM, H[di]],
                 pool_ps[0:64, :], GSM[:, :], H[di][:, c0:c0 + TN],
                 start=(di == 0), stop=(di == NDT - 1))
        p.op("S", "activation", pooled, [pool_ps],
             pooled[:, c0:c0 + TN], pool_ps[0:64, :], AF.Copy)
    for do in range(NDT):
        for tn in range(NTN):
            c0 = tn * TN
            for di in range(NDT):
                p.op("T", "matmul", po, [W[di], H[di]],
                     po[:, :], W[di][:, do * 128:(do + 1) * 128], H[di][:, c0:c0 + TN],
                     start=(di == 0), stop=False)
            p.op("T", "matmul", po, [DSH, pooled],
                 po[:, :], DSH[:, do * 128:(do + 1) * 128], pooled[:, c0:c0 + TN],
                 start=False, stop=True)
            p.op("V", "scalar_tensor_tensor", X[do], [po, X[do]],
                 X[do][:, c0:c0 + TN], po[:, :], 0.5, X[do][:, c0:c0 + TN],
                 ALU.mult, ALU.add)


def rope_phi(p, QT, W, H, CC, SSX, sb, ps, tag):
    qf = sb["qf"]
    qs = sb["qs"]
    qw = sb["qw"]
    po = ps["P512a"]
    for t in range(NDT):
        for tn in range(NTN):
            c0 = tn * TN
            for di in range(NDT):
                p.op("T", "matmul", po, [W[di], H[di]],
                     po[:, :], W[di][:, t * 128:(t + 1) * 128], H[di][:, c0:c0 + TN],
                     start=(di == 0), stop=(di == NDT - 1))
            p.op("S", "activation", qf, [po], qf[:, c0:c0 + TN], po[:, :], AF.Copy)
        p.op("V", "tensor_mul", qs, [qf, SSX], qs[:, :], qf[:, :], SSX[:, :])
        p.op("S", "activation", qw, [qs], qw[0:32, :], qs[32:64, :], AF.Copy)
        p.op("S", "activation", qw, [qs], qw[32:64, :], qs[0:32, :], AF.Copy)
        p.op("S", "activation", qw, [qs], qw[64:96, :], qs[96:128, :], AF.Copy)
        p.op("S", "activation", qw, [qs], qw[96:128, :], qs[64:96, :], AF.Copy)
        p.op("V", "tensor_mul", qs, [qf, CC], qs[:, :], qf[:, :], CC[:, :])
        p.op("V", "tensor_add", qw, [qs, qw], qw[:, :], qs[:, :], qw[:, :])
        p.op("V", "tensor_single_scalar", qs, [qw], qs[:, :], qw[:, :], 0.0, ALU.min)
        p.op("S", "activation", qf, [qs], qf[:, :], qs[:, :], AF.Exp)
        p.op("V", "tensor_single_scalar", qs, [qw], qs[:, :], qw[:, :], 0.0, ALU.max)
        p.op("V", "tensor_add", QT[t], [qs, qf], QT[t][:, :], qs[:, :], qf[:, :])


def std_alloc(nc, sb, ps):
    def S(name, shape, dt):
        sb[name] = nc.alloc_sbuf_tensor(name, shape, dt)
        return sb[name]
    X = [S(f"X{i}", [128, TC], F32) for i in range(NDT)]
    H = [S(f"H{i}", [128, TC], BF16) for i in range(NDT)]
    WT = [S(f"WT{i}", [128, D], BF16) for i in range(NDT)]
    S("sq", [128, 2 * TN], F32)
    S("onesf", [128, 1], F32)
    S("onesr", [1, 128], F32)
    S("onesbf", [128, 1], BF16)
    S("lnrows", [1, 2 * TN], F32)
    S("epsr", [1, 1], F32)
    S("rb", [128, TN], F32); S("nb", [128, TN], F32); S("lntmp", [128, TN], F32)
    sb["X"] = X; sb["H"] = H; sb["WT"] = WT
    return S


def emit_common_init(p, sb):
    p.op("V", "memset", sb["onesf"], [], sb["onesf"][:, :], 1.0)
    p.op("V", "memset", sb["epsr"], [], sb["epsr"][:, :], 1e-5)
    p.op("V", "memset", sb["onesr"], [], sb["onesr"][:, :], 1.0)
    p.op("V", "memset", sb["onesbf"], [], sb["onesbf"][:, :], 1.0)


def emit_sec1(p, sb, ps, io, fused=False):
    nc = p.nc
    xin = io["x"]; wq = io["wq_t"]; wk = io["wk_t"]; wv = io["wv_t"]
    l0w = io["l0w_t"]; dsh = io["disha"]; ccd = io["cc"]; ssd = io["ssx"]
    gsd = io["gsm"]; trid = io["triu"]
    x1o = io["x1"]; numo = io["num_part"]; deno = io["den_part"]
    kvo = io["kvloc"]; kso = io["ksloc"]
    X = sb["X"]; H = sb["H"]; QP = sb["QP"]; KP = sb["KP"]; WT = sb["WT"]
    DSH = sb["DSH"]; GSM = sb["GSM"]; TRI = sb["TRI"]
    CC = X[3]
    SSX = X[4]
    sb["qf"] = X[0]; sb["qs"] = X[1]; sb["qw"] = X[2]
    vtok = sb["vtok"]; ktok = sb["ktok"]; smh = sb["smh"]
    kvbf = sb["kvbf"]; ksbf = sb["ksbf"]
    nstg = X[5]
    dstg = sb["dstg"]
    kvst = X[6]
    IDN = sb["IDN"]
    pden = ps["P512a"]
    pS = ps["PS"]
    pkv = ps["PKV"]
    pks = ps["PKS"]
    ptr = ps["PTR"]
    idn_h = io["idn_h"]

    for i in range(NDT):
        p.dma("Y", X[i][:, :], xin.ap()[i * 128:(i + 1) * 128, :])
    for i in range(NDT):
        p.dma("G", WT[i][:, :], l0w.ap()[i * 128:(i + 1) * 128, :])
    p.dma("G", DSH[:, :], dsh.ap()[:, :])
    p.dma("G", GSM[:, :], gsd.ap()[:, :])
    p.dma("Y", TRI[:, :], trid.ap()[:, :])
    p.dma("Y", IDN[:, :], idn_h.ap()[:, :])
    zc = sb["zc"]
    zr = sb["zr"]
    p.op("V", "memset", zc, [], zc[:, :], 0.0)
    p.op("V", "memset", zr, [], zr[:, :], 0.0)
    p.op("T", "matmul", pkv, [zc, zr], pkv[:, 0:512], zc[:, 0:64], zr[:, 0:512], start=True, stop=True)
    p.op("T", "matmul", pkv, [zc, zr], pkv[:, 512:1024], zc[:, 0:64], zr[:, 512:1024], start=True, stop=True)
    p.op("T", "matmul", pks, [zc, zr], pks[:, :], zc[:, 0:64], zr[:, 0:16], start=True, stop=True)
    for e in ENGINES:
        p.barrier_dma(e)

    emit_ln(p, X, H, sb, ps)
    emit_l0ffn(p, X, H, WT, DSH, GSM, sb, ps)
    for i in range(NDT):
        p.dma("Y", x1o.ap()[i * 128:(i + 1) * 128, :], X[i][:, :])
    del x1o

    emit_ln(p, X, H, sb, ps)
    p.dma("Y", CC[:, :], ccd.ap()[:, :])
    p.dma("Y", SSX[:, :], ssd.ap()[:, :])
    for i in range(NDT):
        p.dma("G", WT[i][:, :], wq.ap()[i * 128:(i + 1) * 128, :])
    rope_phi(p, QP, WT, H, CC, SSX, sb, ps, "q")
    for i in range(NDT):
        p.dma("G", WT[i][:, :], wk.ap()[i * 128:(i + 1) * 128, :])
    rope_phi(p, KP, WT, H, CC, SSX, sb, ps, "k")
    for i in range(NDT):
        p.dma("G", WT[i][:, :], wv.ap()[i * 128:(i + 1) * 128, :])
    if not fused:
        for i in range(NDT):
            p.dma("Y", io["qphi"].ap()[i * 128:(i + 1) * 128, :], QP[i][:, :])

    pbig = ps["P1024"]
    for j in range(NCH):
        c0 = j * CH
        for half in range(2):
            for di in range(NDT):
                p.op("T", "matmul", pbig, [H[di], WT[di]],
                     pbig[:, half * 512:(half + 1) * 512],
                     H[di][:, c0:c0 + CH], WT[di][:, half * 512:(half + 1) * 512],
                     start=(di == 0), stop=(di == NDT - 1))
            p.op("S", "activation", vtok, [pbig],
                 vtok[:, half * 512:(half + 1) * 512], pbig[:, half * 512:(half + 1) * 512], AF.Copy)
        for t in range(NDT):
            p.op("T", "transpose", ptr, [KP[t], IDN],
                 ptr[:, :], KP[t][:, c0:c0 + CH], IDN[:, :])
            p.op("V", "tensor_copy", ktok, [ptr], ktok[:, t * 128:(t + 1) * 128], ptr[:, :])
        for h in range(16):
            t, hh = h // 2, h % 2
            rows = slice(hh * 64, hh * 64 + 64)
            p.op("T", "matmul", pS, [KP[t], QP[t]],
                 pS[:, :], KP[t][rows, c0:c0 + CH], QP[t][rows, c0:c0 + CH],
                 start=True, stop=True)
            p.op("V", "tensor_mul", smh, [pS, TRI], smh[:, :], pS[:, :], TRI[:, :])
            p.op("T", "matmul", pbig, [smh, vtok],
                 pbig[:, h * 64:(h + 1) * 64], smh[:, :], vtok[:, h * 64:(h + 1) * 64],
                 start=True, stop=(j == 0))
            if j > 0:
                p.op("T", "matmul", pbig, [QP[t], kvbf],
                     pbig[:, h * 64:(h + 1) * 64], QP[t][rows, c0:c0 + CH],
                     kvbf[rows, t * 64:(t + 1) * 64], start=False, stop=True)
            p.op("T", "matmul", pden, [smh, sb["onesbf"]],
                 pden[:, h:h + 1], smh[:, :], sb["onesbf"][:, :],
                 start=True, stop=(j == 0))
            if j > 0:
                p.op("T", "matmul", pden, [QP[t], ksbf],
                     pden[:, h:h + 1], QP[t][rows, c0:c0 + CH], ksbf[rows, t:t + 1],
                     start=False, stop=True)
        p.op("V", "tensor_copy", nstg, [pbig], nstg[:, 0:1024], pbig[:, :])
        p.op("V", "tensor_copy", dstg, [pden], dstg[:, :], pden[:, 0:16])
        p.dma("Y", numo.ap()[c0:c0 + CH, :], nstg[:, 0:1024])
        p.dma("S", deno.ap()[:, j * 16:(j + 1) * 16], dstg[:, :])
        for h in range(16):
            t, hh = h // 2, h % 2
            rows = slice(hh * 64, hh * 64 + 64)
            kcols = slice(t * 128 + hh * 64, t * 128 + hh * 64 + 64)
            p.op("T", "matmul", pkv, [ktok, vtok],
                 pkv[0:64, h * 64:(h + 1) * 64], ktok[:, kcols], vtok[:, h * 64:(h + 1) * 64],
                 start=False, stop=(j == NCH - 1))
            p.op("T", "matmul", pks, [ktok, sb["onesbf"]],
                 pks[0:64, h:h + 1], ktok[:, kcols], sb["onesbf"][:, :],
                 start=False, stop=(j == NCH - 1))
        ev = pkv[0:64, :].rearrange("p (t two d) -> p two t d", two=2, d=64)
        p.op("V", "tensor_copy", kvbf, [pkv], kvbf[0:64, :].rearrange("p (t d) -> p t d", d=64), ev[:, 0])
        p.op("V", "tensor_copy", kvbf, [pkv], kvbf[64:128, :].rearrange("p (t d) -> p t d", d=64), ev[:, 1])
        ek = pks[0:64, :].rearrange("p (t two) -> p two t", two=2)
        p.op("V", "tensor_copy", ksbf, [pks], ksbf[0:64, :], ek[:, 0])
        p.op("V", "tensor_copy", ksbf, [pks], ksbf[64:128, :], ek[:, 1])
    p.op("V", "tensor_copy", kvst, [pkv], kvst[0:64, 0:1024], pkv[0:64, :])
    p.op("V", "tensor_copy", kvst, [pks], kvst[0:64, 1024:1040], pks[0:64, :])
    p.dma("Y", kvo, kvst[0:64, 0:1024])
    p.dma("Y", kso, kvst[0:64, 1024:1040])


def build_launch1(nc):
    io = {}
    for nm, shp, dt in [("x", [D, TC], F32), ("wq_t", [D, D], F32), ("wk_t", [D, D], F32),
                        ("wv_t", [D, D], F32), ("l0w_t", [D, D], F32), ("disha", [64, D], F32),
                        ("cc", [128, TC], F32), ("ssx", [128, TC], F32), ("gsm", [128, 64], F32),
                        ("triu", [128, 128], F32)]:
        io[nm] = nc.dram_tensor(nm, shp, dt, kind="ExternalInput")
    io["x1"] = nc.dram_tensor("x1", [D, TC], F32, kind="ExternalOutput")
    io["qphi"] = nc.dram_tensor("qphi", [D, TC], BF16, kind="ExternalOutput")
    io["num_part"] = nc.dram_tensor("num_part", [TC, D], F32, kind="ExternalOutput")
    io["den_part"] = nc.dram_tensor("den_part", [128, NCH * 16], F32, kind="ExternalOutput")
    io["kvloc"] = nc.dram_tensor("kvloc", [64, 1024], F32, kind="ExternalOutput").ap()[:, :]
    io["ksloc"] = nc.dram_tensor("ksloc", [64, 16], F32, kind="ExternalOutput").ap()[:, :]
    io["idn_h"] = nc.inline_tensor(np.eye(128, dtype=ml_dtypes.bfloat16), "idn_const")
    p = Prog(nc)
    sb = {}
    ps = {}
    S = std_alloc(nc, sb, ps)
    sb["QP"] = [S(f"QP{i}", [128, TC], BF16) for i in range(NDT)]
    sb["KP"] = [S(f"KP{i}", [128, TC], BF16) for i in range(NDT)]
    alloc_sec1_small(nc, S, sb, ps, p)
    emit_common_init(p, sb)
    emit_sec1(p, sb, ps, io)
    p.finish()
    p.build()
    return nc


def alloc_sec1_small(nc, S, sb, ps, p):
    def P(name, shape, dt):
        ps[name] = nc.alloc_psum_tensor(name, shape, dt)
        return ps[name]
    S("DSH", [64, D], BF16)
    S("GSM", [128, 64], BF16)
    S("TRI", [128, 128], F32)
    S("pooled", [64, TC], BF16)
    S("vtok", [128, 1024], BF16)
    S("ktok", [128, 1024], BF16)
    S("smh", [128, 128], BF16)
    S("kvbf", [128, 512], BF16)
    S("ksbf", [128, 8], BF16)
    S("dstg", [128, 16], F32)
    S("IDN", [128, 128], BF16)
    S("zc", [1, 128], BF16)
    S("zr", [1, 1024], BF16)
    if "P1024" not in ps:
        P("P1024", [128, 1024], F32)
        P("P512a", [128, TN], F32)
        P("PS", [128, 128], F32)
        P("PKV", [64, 1024], F32)
        P("PKS", [64, 16], F32)
        P("PTR", [128, 128], BF16)


def emit_sec2(p, sb, ps, io, fused=False):
    nc = p.nc
    x1i = io["x1"]; qpi = io.get("qphi"); numi = io["num_part"]; deni = io["den_part"]
    cmki = io["cmask"]
    woi = io["wo_t"]; l0wi = io["l0w_t"]; dshi = io["disha"]; gsdi = io["gsm"]
    terni = io["tern_t"]; l1bi = io["l1bias_t"]; asci = io["act_scale"]
    wini = io["win_t"]; dwwi = io["dww"]; dwbi = io["dwb"]; dtbi = io["dtb64"]
    x4o = io["x4"]; valo = io["value_bf"]; gato = io["gate_sil"]; dtkdo = io["dtkd"]
    hfo = io["halo_first"]; hlo = io["halo_last"]
    kv_src = io["kv_src"]
    X = sb["X"]; H = sb["H"]; QP = sb["QP"]; WT = sb["WT"]
    YT = H
    DSH = sb["DSH"]; GSM = sb["GSM"]; IDN = sb["IDN"]
    kvacc = sb["kvacc"]; mb = sb["mb"]; kvbf = sb["kvbf"]; ksbf = sb["ksbf"]
    nstg = sb["nstg"]; dstg = sb["dstg"]; drec = sb["drec"]; ybf = sb["ybf"]
    vrp = sb["vrp"]; ca = sb["ca"]; praw = sb["praw"]; gstg = sb["gstg"]
    dwc = sb["dwc"]; dbc = sb["dbc"]; scl = sb["scl"]; bcol = sb["bcol"]; dtb = sb["dtb"]
    pbig = ps["P1024"]
    pden = ps["P512a"]
    ptr = ps["PTR"]
    psc = ps["PSC"]

    if not fused:
        for i in range(NDT):
            p.dma("Y", X[i][:, :], x1i.ap()[i * 128:(i + 1) * 128, :])
            p.dma("Y", QP[i][:, :], qpi.ap()[i * 128:(i + 1) * 128, :])
        p.dma("G", DSH[:, :], dshi.ap()[:, :])
        p.dma("G", GSM[:, :], gsdi.ap()[:, :])
        p.dma("Y", IDN[:, :], io["idn_h"].ap()[:, :])
    else:
        for i in range(NDT):
            p.dma("Y", X[i][:, :], x1i.ap()[i * 128:(i + 1) * 128, :])
    p.dma("Y", sb["scl"][0:1, 0:1], asci.ap()[:, :])
    p.dma("Y", bcol[:, :], l1bi.ap()[:, :])
    p.dma("Y", dtb[:, :], dtbi.ap()[:, :])

    mrow = sb["lnrows"]
    p.dma("Y", mrow[0:1, 0:8], cmki.ap()[:, :])
    p.op("T", "matmul", psc, [sb["onesr"], mrow],
         psc[0:64, :], sb["onesr"][:, 0:64], mrow[0:1, 0:8], start=True, stop=True)
    p.op("V", "tensor_copy", mb, [psc], mb[:, :], psc[0:64, :])
    p.op("V", "memset", kvacc, [], kvacc[:, :], 0.0)
    for cc in range(NCORE):
        kv_ap, ks_ap = kv_src(cc)
        p.dma("Y", nstg[0:64, :], kv_ap)
        p.dma("Y", dstg[0:64, :], ks_ap)
        p.op("V", "scalar_tensor_tensor", kvacc, [nstg, mb, kvacc],
             kvacc[0:64, 0:1024], nstg[0:64, :], mb[:, cc:cc + 1], kvacc[0:64, 0:1024],
             ALU.mult, ALU.add)
        p.op("V", "scalar_tensor_tensor", kvacc, [dstg, mb, kvacc],
             kvacc[0:64, 1024:1040], dstg[0:64, 0:16], mb[:, cc:cc + 1], kvacc[0:64, 1024:1040],
             ALU.mult, ALU.add)
    evv = kvacc[0:64, 0:1024].rearrange("p (t two d) -> p two t d", two=2, d=64)
    p.op("V", "tensor_copy", kvbf, [kvacc], kvbf[0:64, :].rearrange("p (t d) -> p t d", d=64), evv[:, 0])
    p.op("V", "tensor_copy", kvbf, [kvacc], kvbf[64:128, :].rearrange("p (t d) -> p t d", d=64), evv[:, 1])
    ekk = kvacc[0:64, 1024:1040].rearrange("p (t two) -> p two t", two=2)
    p.op("V", "tensor_copy", ksbf, [kvacc], ksbf[0:64, :], ekk[:, 0])
    p.op("V", "tensor_copy", ksbf, [kvacc], ksbf[64:128, :], ekk[:, 1])

    for j in range(NCH):
        c0 = j * CH
        p.dma("Y", nstg[:, :], numi.ap()[c0:c0 + CH, :])
        p.dma("S", dstg[:, :], deni.ap()[:, j * 16:(j + 1) * 16])
        for h in range(16):
            t, hh = h // 2, h % 2
            rows = slice(hh * 64, hh * 64 + 64)
            p.op("T", "matmul", pbig, [QP[t], kvbf],
                 pbig[:, h * 64:(h + 1) * 64], QP[t][rows, c0:c0 + CH],
                 kvbf[rows, t * 64:(t + 1) * 64], start=True, stop=True)
            p.op("T", "matmul", ps["PTRASH"], [QP[t], ksbf],
                 ps["PTRASH"][:, h:h + 1], QP[t][rows, c0:c0 + CH], ksbf[rows, t:t + 1],
                 start=True, stop=True)
        p.op("V", "tensor_add", nstg, [nstg, pbig], nstg[:, :], nstg[:, :], pbig[:, :])
        p.op("V", "tensor_add", dstg, [dstg, ps["PTRASH"]], dstg[:, :], dstg[:, :], ps["PTRASH"][:, 0:16])
        p.op("V", "tensor_single_scalar", dstg, [dstg], dstg[:, :], dstg[:, :], 1e-6, ALU.max)
        p.op("V", "reciprocal", drec, [dstg], drec[:, :], dstg[:, :])
        for h in range(16):
            p.op("V", "tensor_single_scalar", ybf, [nstg, drec],
                 ybf[:, h * 64:(h + 1) * 64], nstg[:, h * 64:(h + 1) * 64],
                 drec[:, h:h + 1], ALU.mult)
        for t in range(NDT):
            p.op("T", "transpose", ptr, [ybf, IDN],
                 ptr[:, :], ybf[:, t * 128:(t + 1) * 128], IDN[:, :])
            p.op("V", "tensor_copy", YT[t], [ptr], YT[t][:, c0:c0 + CH], ptr[:, :])

    for i in range(NDT):
        p.dma("G", WT[i][:, :], woi.ap()[i * 128:(i + 1) * 128, :])
    for do in range(NDT):
        for tn in range(NTN):
            c0 = tn * TN
            for dv in range(NDT):
                p.op("T", "matmul", pden, [WT[dv], YT[dv]],
                     pden[:, :], WT[dv][:, do * 128:(do + 1) * 128], YT[dv][:, c0:c0 + TN],
                     start=(dv == 0), stop=(dv == NDT - 1))
            p.op("V", "tensor_add", X[do], [pden, X[do]],
                 X[do][:, c0:c0 + TN], pden[:, :], X[do][:, c0:c0 + TN])

    emit_ln(p, X, H, sb, ps)
    for i in range(NDT):
        p.dma("G", WT[i][:, :], l0wi.ap()[i * 128:(i + 1) * 128, :])
    emit_l0ffn(p, X, H, WT, DSH, GSM, sb, ps)

    p.op("V", "tensor_single_scalar", scl, [scl], scl[0:1, 0:1], scl[0:1, 0:1], 1e-5, ALU.max)
    p.op("V", "reciprocal", scl, [scl], scl[0:1, 1:2], scl[0:1, 0:1])
    p.op("T", "matmul", psc, [sb["onesr"], scl],
         psc[:, 0:2], sb["onesr"][:, :], scl[0:1, 0:2], start=True, stop=True)
    p.op("V", "tensor_copy", scl, [psc], scl[:, :], psc[:, 0:2])
    emit_ln_quant(p, X, H, sb, ps)
    for i in range(NDT):
        p.dma("Y", sb["sq"][:, 0:1024].bitcast(I32), terni.ap()[i * 128:(i + 1) * 128, :])
        p.op("V", "tensor_copy", ca, [sb["sq"]], ca[:, 0:1024], sb["sq"][:, 0:1024].bitcast(I32))
        p.op("V", "tensor_single_scalar", WT[i], [ca], WT[i][:, :], ca[:, 0:1024], 1.0, ALU.subtract)
    for do in range(NDT):
        for tn in range(NTN):
            c0 = tn * TN
            for di in range(NDT):
                p.op("T", "matmul", pden, [WT[di], H[di]],
                     pden[:, :], WT[di][:, do * 128:(do + 1) * 128], H[di][:, c0:c0 + TN],
                     start=(di == 0), stop=(di == NDT - 1))
            p.op("V", "tensor_scalar", ca, [pden, scl, bcol],
                 ca[:, 0:TN], pden[:, :], scl[:, 0:1], bcol[:, do:do + 1],
                 ALU.mult, ALU.add)
            p.op("V", "scalar_tensor_tensor", X[do], [ca, X[do]],
                 X[do][:, c0:c0 + TN], ca[:, 0:TN], 0.5, X[do][:, c0:c0 + TN],
                 ALU.mult, ALU.add)
    for i in range(NDT):
        p.dma("Y", x4o.ap()[i * 128:(i + 1) * 128, :], X[i][:, :])

    emit_ln(p, X, H, sb, ps)

    for og in range(4):
        for i in range(NDT):
            p.dma("G", WT[i][:, :], wini.ap()[i * 128:(i + 1) * 128, og * 1024:(og + 1) * 1024])
        for sub in range(8):
            ot = og * 8 + sub
            for tn in range(NTN):
                c0 = tn * TN
                for di in range(NDT):
                    p.op("T", "matmul", pden, [WT[di], H[di]],
                         pden[:, :], WT[di][:, sub * 128:(sub + 1) * 128], H[di][:, c0:c0 + TN],
                         start=(di == 0), stop=(di == NDT - 1))
                if og < 2:
                    p.op("S", "activation", gstg, [pden],
                         gstg[:, c0:c0 + TN], pden[:, :], AF.Silu)
                else:
                    p.op("S", "activation", vrp, [pden],
                         vrp[:, 3 + c0:3 + c0 + TN], pden[:, :], AF.Copy)
            if og < 2:
                p.dma("Y", gato.ap()[ot * 128:(ot + 1) * 128, :], gstg[:, :])
            else:
                ct = ot - 16
                p.dma("S", dwc[:, :], dwwi.ap()[ct * 128:(ct + 1) * 128, :])
                p.dma("S", dbc[:, :], dwbi.ap()[ct * 128:(ct + 1) * 128, :])
                p.op("V", "memset", vrp, [], vrp[:, 0:3], 0.0)
                p.dma("Y", hfo.ap()[:, ct * 3:(ct + 1) * 3], vrp[:, 3:6])
                p.dma("Y", hlo.ap()[:, ct * 3:(ct + 1) * 3], vrp[:, TC:TC + 3])
                p.op("V", "tensor_single_scalar", ca, [vrp, dwc],
                     ca[:, :], vrp[:, 1:1 + TC], dwc[:, 1:2], ALU.mult)
                p.op("V", "scalar_tensor_tensor", ca, [vrp, dwc, ca],
                     ca[:, :], vrp[:, 0:TC], dwc[:, 0:1], ca[:, :], ALU.mult, ALU.add)
                p.op("V", "scalar_tensor_tensor", ca, [vrp, dwc, ca],
                     ca[:, :], vrp[:, 2:2 + TC], dwc[:, 2:3], ca[:, :], ALU.mult, ALU.add)
                p.op("V", "scalar_tensor_tensor", ca, [vrp, dwc, ca],
                     ca[:, :], vrp[:, 3:3 + TC], dwc[:, 3:4], ca[:, :], ALU.mult, ALU.add)
                p.op("V", "tensor_scalar", ca, [ca, dbc],
                     ca[:, :], ca[:, :], dbc[:, 0:1], None, ALU.add)
                p.op("S", "activation", gstg, [ca],
                     gstg[:, :], ca[:, :], AF.Silu)
                p.dma("Y", valo.ap()[ct * 128:(ct + 1) * 128, :], gstg[:, :])
    for i in range(NDT):
        p.dma("G", WT[i][:, 0:64], wini.ap()[i * 128:(i + 1) * 128, 4096:4160])
    for tn in range(NTN):
        c0 = tn * TN
        for di in range(NDT):
            p.op("T", "matmul", pden, [WT[di], H[di]],
                 pden[0:64, :], WT[di][:, 0:64], H[di][:, c0:c0 + TN],
                 start=(di == 0), stop=(di == NDT - 1))
        p.op("V", "tensor_copy", praw, [pden], praw[:, c0:c0 + TN], pden[0:64, :])

    aa = ca[0:64, 0:TC]
    rr = vrp[0:64, 0:TC]
    p.op("V", "tensor_scalar", praw, [praw, dtb], praw[:, :], praw[:, :], dtb[:, 0:1], None, ALU.add)
    p.op("S", "activation", ca, [praw], aa, praw[:, :], AF.Abs)
    p.op("S", "activation", ca, [ca], aa, aa, AF.Exp, scale=-1.0)
    p.op("S", "activation", ca, [ca], aa, aa, AF.Ln, bias=sb["onesf"][0:64, 0:1])
    p.op("V", "tensor_single_scalar", vrp, [praw], rr, praw[:, :], 0.0, ALU.max)
    p.op("V", "tensor_add", ca, [ca, vrp], aa, aa, rr)
    p.op("V", "tensor_scalar", ca, [ca], aa, aa, 0.1, 1e-4, ALU.min, ALU.max)
    p.dma("Y", dtkdo.ap()[:, 0:TC], ca[0:64, 0:TC])
    p.op("S", "activation", vrp, [praw], rr, praw[:, :], AF.Sigmoid)
    p.op("V", "tensor_single_scalar", vrp, [vrp], rr, rr, 0.5, ALU.mult)
    p.dma("Y", dtkdo.ap()[:, TC:2 * TC], vrp[0:64, 0:TC])


def alloc_sec2_small(nc, S, sb, ps):
    def P(name, shape, dt):
        if name not in ps:
            ps[name] = nc.alloc_psum_tensor(name, shape, dt)
        return ps[name]
    for nm, shp, dt in [("DSH", [64, D], BF16), ("GSM", [128, 64], BF16),
                        ("pooled", [64, TC], BF16), ("kvacc", [64, 1040], F32),
                        ("mb", [64, 8], F32), ("kvbf", [128, 512], BF16),
                        ("ksbf", [128, 8], BF16), ("dstg", [128, 16], F32),
                        ("drec", [128, 16], F32), ("dwc", [128, 4], F32),
                        ("dbc", [128, 1], F32), ("scl", [128, 2], F32),
                        ("bcol", [128, 8], F32), ("dtb", [64, 1], F32),
                        ("IDN", [128, 128], BF16)]:
        if nm not in sb:
            S(nm, shp, dt)
    P("P1024", [128, 1024], F32)
    P("P512a", [128, TN], F32)
    P("PTR", [128, 128], BF16)


def build_launch2(nc):
    io = {}
    for nm, shp, dt, kind in [
            ("x1", [D, TC], F32, "ExternalInput"),
            ("qphi", [D, TC], BF16, "ExternalInput"),
            ("num_part", [TC, D], F32, "ExternalInput"),
            ("den_part", [128, NCH * 16], F32, "ExternalInput"),
            ("kvall", [512, 1024], F32, "ExternalInput"),
            ("ksall", [512, 16], F32, "ExternalInput"),
            ("cmask", [1, 8], F32, "ExternalInput"),
            ("wo_t", [D, D], F32, "ExternalInput"),
            ("l0w_t", [D, D], F32, "ExternalInput"),
            ("disha", [64, D], F32, "ExternalInput"),
            ("gsm", [128, 64], F32, "ExternalInput"),
            ("tern_t", [D, D], I32, "ExternalInput"),
            ("l1bias_t", [128, 8], F32, "ExternalInput"),
            ("act_scale", [1, 1], F32, "ExternalInput"),
            ("win_t", [D, 4160], F32, "ExternalInput"),
            ("dww", [DI, 4], F32, "ExternalInput"),
            ("dwb", [DI, 1], F32, "ExternalInput"),
            ("dtb64", [64, 1], F32, "ExternalInput"),
            ("x4", [D, TC], F32, "ExternalOutput"),
            ("value_bf", [DI, TC], BF16, "ExternalOutput"),
            ("gate_sil", [DI, TC], BF16, "ExternalOutput"),
            ("dtkd", [64, 2 * TC], F32, "ExternalOutput"),
            ("halo_first", [128, 48], F32, "ExternalOutput"),
            ("halo_last", [128, 48], F32, "ExternalOutput")]:
        io[nm] = nc.dram_tensor(nm, shp, dt, kind=kind)
    io["idn_h"] = nc.inline_tensor(np.eye(128, dtype=ml_dtypes.bfloat16), "idn_const")
    kvai, ksai = io["kvall"], io["ksall"]
    io["kv_src"] = lambda cc: (kvai.ap()[cc * 64:cc * 64 + 64, :],
                               ksai.ap()[cc * 64:cc * 64 + 64, :])
    p = Prog(nc)
    sb = {}
    ps = {}
    S = std_alloc(nc, sb, ps)
    sb["QP"] = [S(f"QP{i}", [128, TC], BF16) for i in range(NDT)]
    alloc_sec2_small(nc, S, sb, ps)
    for nm, shp, dt in [("nstg", [128, 1024], F32), ("ybf", [128, 1024], BF16),
                        ("vrp", [128, 2051], F32), ("ca", [128, TC], F32),
                        ("praw", [64, TC], F32), ("gstg", [128, TC], BF16)]:
        S(nm, shp, dt)
    if "PSC" not in ps:
        ps["PSC"] = nc.alloc_psum_tensor("PSC", [128, 8], F32)
    ps["PTRASH"] = nc.alloc_psum_tensor("PTRASH", [128, 16], F32)
    emit_common_init(p, sb)
    emit_sec2(p, sb, ps, io)
    p.finish()
    p.build()
    return nc


def emit_sec3(p, sb, ps, io, fused=False):
    HSN = HS // CL
    NCHK = HS // C1
    CLC = C1 // CL
    nc = p.nc
    vali = io["value_bf"]; dtkdi = io["dtkd"]; hfi = io["halo_first"]
    dwwi = io["dww"]; dwbi = io["dwb"]; wbci = io["wbc_t"]; alogi = io["acol_log"]
    seldi = io["seld"]; selki = io["selk"]; gs32i = io["gs32"]; sel4i = io["sel4"]
    triui = io["triu"]
    yto = io["yT_bf"]; g2co = io["g2c_bf"]; vspo = io["vspan"]

    VAL = sb["VAL"]; WBC = sb["WBC"]
    DTF = sb["DTF"]; KDF = sb["KDF"]
    SELD = sb["SELD"]; SELK = sb["SELK"]; GS32 = sb["GS32"]; SEL4 = sb["SEL4"]
    TRIU = sb["TRIU"]; ACOL = sb["ACOL"]; RESETM = sb["RESETM"]
    PL = sb["PL"]
    (dtb, kdb, DTA, SP, BP, CP, DP, BZ, BX, CZ, CX, C0, C1p,
     P00, P01, P10, P11, DET) = PL
    tmpA = kdb
    g0, g1 = BZ, BX
    Gc0, Gc1 = DTA, SP
    W0, W1 = CZ, CX
    H0, H1 = BP, CP
    RDT = DP
    QP00 = sb["QP00"]; QP01 = sb["QP01"]; QP10 = sb["QP10"]; QP11 = sb["QP11"]
    DQP = sb["DQP"]; PCH = sb["PCH"]; QPRE = sb["QPRE"]; QSPC = sb["QSPC"]
    GcP = sb["GcP"]; HcP = sb["HcP"]; G2S = sb["G2S"]
    utk = sb["utk"]; hct = sb["hct"]; AMS = sb["AMS"]
    YTS = sb["YTS"]; VST = sb["VST"]; VSB = sb["VSB"]; VSW = sb["VSW"]
    PAC = sb["PAC"]; PBC2 = sb["PBC2"]
    bcs = sb["bcs"]; sqs = sb["sqs"]; rns = sb["rns"]
    hp6 = sb["hp6"]; dwc = sb["dwc"]; dbc = sb["dbc"]; cstg = sb["cstg"]
    IDN = sb["IDN"]
    pA = ps["PA"]
    pY = ps["PY"]
    pW = ps["PW"]
    ptr = ps["PTR"]
    pbc = ps["PBC"]
    p4 = ps["P4"]
    prn = ps["PRN"]

    for i in range(16):
        p.dma("Y", VAL[i][:, :], vali.ap()[i * 128:(i + 1) * 128, :])
    p.dma("Y", SELD[:, :], seldi.ap()[:, :])
    p.dma("Y", SELK[:, :], selki.ap()[:, :])
    p.dma("Y", GS32[:, :], gs32i.ap()[:, :])
    p.dma("Y", SEL4[:, :], sel4i.ap()[:, :])
    if not fused:
        p.dma("Y", TRIU[:, :], triui.ap()[:, :])
        p.dma("Y", IDN[:, :], io["idn_h"].ap()[:, :])
    p.dma("Y", RESETM[:, :], io["rm_h"].ap()[:, :])
    p.dma("Y", ACOL[:, :], alogi.ap()[:, :])
    p.op("S", "activation", ACOL, [ACOL], ACOL[:, :], ACOL[:, :], AF.Exp)

    if fused:
        hpall = sb["hpall"]
        p.dma("Y", hpall[:, :], io["halo_out"].ap().rearrange("(cc p) q -> p (cc q)", p=128))
        mrow = sb["lnrows"]
        p.dma("Y", mrow[0:1, 8:16], io["mprev"].ap()[:, :])
        p.op("T", "matmul", ps["PSC"], [sb["onesr"], mrow],
             ps["PSC"][:, 0:8], sb["onesr"][:, :], mrow[0:1, 8:16], start=True, stop=True)
        p.op("V", "tensor_copy", sb["mbp"], [ps["PSC"]], sb["mbp"][:, :], ps["PSC"][:, 0:8])
    for ct in range(16):
        rows = slice(ct * 128, (ct + 1) * 128)
        if fused:
            p.op("V", "memset", hp6, [], hp6[:, 0:3], 0.0)
            for cc in range(NCORE):
                p.op("V", "scalar_tensor_tensor", hp6, [sb["hpall"], sb["mbp"], hp6],
                     hp6[:, 0:3], sb["hpall"][:, cc * 48 + ct * 3:cc * 48 + (ct + 1) * 3],
                     sb["mbp"][:, cc:cc + 1], hp6[:, 0:3], ALU.mult, ALU.add)
        else:
            p.dma("Y", hp6[:, 0:3], io["halo_prev"].ap()[:, ct * 3:(ct + 1) * 3])
        p.dma("Y", hp6[:, 3:6], hfi.ap()[:, ct * 3:(ct + 1) * 3])
        p.dma("S", dwc[:, :], dwwi.ap()[rows, :])
        p.dma("S", dbc[:, :], dwbi.ap()[rows, :])
        p.op("V", "tensor_single_scalar", cstg, [hp6, dwc],
             cstg[:, :], hp6[:, 1:4], dwc[:, 1:2], ALU.mult)
        p.op("V", "scalar_tensor_tensor", cstg, [hp6, dwc, cstg],
             cstg[:, :], hp6[:, 0:3], dwc[:, 0:1], cstg[:, :], ALU.mult, ALU.add)
        p.op("V", "scalar_tensor_tensor", cstg, [hp6, dwc, cstg],
             cstg[:, :], hp6[:, 2:5], dwc[:, 2:3], cstg[:, :], ALU.mult, ALU.add)
        p.op("V", "scalar_tensor_tensor", cstg, [hp6, dwc, cstg],
             cstg[:, :], hp6[:, 3:6], dwc[:, 3:4], cstg[:, :], ALU.mult, ALU.add)
        p.op("V", "tensor_scalar", cstg, [cstg, dbc],
             cstg[:, :], cstg[:, :], dbc[:, 0:1], None, ALU.add)
        p.op("S", "activation", VAL[ct], [cstg], VAL[ct][:, 0:3], cstg[:, :], AF.Silu)

    for g in range(8):
        ag = ACOL[:, g:g + 1]
        for i in range(16):
            p.dma("G", WBC[i][:, :], wbci.ap()[i * 128:(i + 1) * 128, g * 512:(g + 1) * 512])
        for q in range(2):
            p.op("V", "memset", VST[q], [], VST[q][:, :], 0.0)
            p.op("V", "memset", VSB[q], [], VSB[q][:, :], 0.0)
        p.op("V", "memset", QSPC, [], QSPC[:, :], 0.0)
        p.op("V", "tensor_single_scalar", QSPC, [QSPC], QSPC[:, 0:1], QSPC[:, 0:1], 1.0, ALU.add)
        p.op("V", "tensor_single_scalar", QSPC, [QSPC], QSPC[:, 3:4], QSPC[:, 3:4], 1.0, ALU.add)

        for hs in range(2):
            hsr = slice(hs * HS, (hs + 1) * HS)
            hs0 = hs * HS
            for bt in range(4):
                for tn in range(2):
                    csl = slice(tn * 512, (tn + 1) * 512)
                    for ct in range(16):
                        p.op("T", "matmul", pbc, [WBC[ct], VAL[ct]],
                             pbc[:, :], WBC[ct][:, bt * 128:(bt + 1) * 128],
                             VAL[ct][:, hs0 + tn * 512:hs0 + (tn + 1) * 512],
                             start=(ct == 0), stop=(ct == 15))
                    p.op("S", "activation", sqs, [pbc], sqs[:, :], pbc[:, :], AF.Square)
                    p.op("T", "matmul", p4, [GS32, sqs],
                         p4[:, :], GS32[:, :], sqs[:, :], start=True, stop=True)
                    p.op("S", "activation", rns, [p4], rns[:, :], p4[:, :], AF.Sqrt)
                    p.op("V", "tensor_single_scalar", rns, [rns], rns[:, :], rns[:, :], 1e-8, ALU.add)
                    p.op("V", "reciprocal", rns, [rns], rns[:, :], rns[:, :])
                    p.op("T", "matmul", prn, [SEL4, rns],
                         prn[:, :], SEL4[:, :], rns[:, :], start=True, stop=True)
                    p.op("V", "tensor_copy", bcs, [prn], bcs[:, :], prn[:, :])
                    prow = slice(32 * bt, 32 * bt + 32)
                    for comp, PLt in [(0, BZ), (1, BX), (2, CZ), (3, CX)]:
                        crow = slice(32 * comp, 32 * comp + 32)
                        p.op("V", "tensor_mul", PLt, [pbc, bcs],
                             PLt[prow, csl], pbc[crow, :], bcs[crow, :])
            p.dma("Y", DTF[:, :], dtkdi.ap()[:, hs0:hs0 + HS])
            p.dma("Y", KDF[:, :], dtkdi.ap()[:, TC + hs0:TC + hs0 + HS])
            for tn in range(2):
                csl = slice(tn * 512, (tn + 1) * 512)
                p.op("T", "matmul", prn, [SELD, DTF],
                     prn[:, :], SELD[:, g * 128:(g + 1) * 128], DTF[:, csl],
                     start=True, stop=True)
                p.op("V", "tensor_copy", dtb, [prn], dtb[:, csl], prn[:, :])
                p.op("T", "matmul", prn, [SELK, KDF],
                     prn[:, :], SELK[:, g * 128:(g + 1) * 128], KDF[:, csl],
                     start=True, stop=True)
                p.op("V", "tensor_copy", kdb, [prn], kdb[:, csl], prn[:, :])
            p.op("V", "tensor_mul", C0, [dtb, kdb], C0[:, :], dtb[:, :], kdb[:, :])
            p.op("V", "tensor_mul", C1p, [dtb], C1p[:, :], dtb[:, :], dtb[:, :])
            p.op("S", "activation", DTA, [dtb], DTA[:, :], dtb[:, :], AF.Copy, scale=ag)
            p.op("S", "activation", C1p, [C1p], C1p[:, :], C1p[:, :], AF.Copy, scale=ag)
            p.op("V", "tensor_add", C0, [C0, C1p], C0[:, :], C0[:, :], C1p[:, :])
            p.op("V", "tensor_single_scalar", C0, [C0], C0[:, :], C0[:, :], 1.0, ALU.add)
            p.op("V", "reciprocal", SP, [C0], SP[:, :], C0[:, :])
            p.op("V", "scalar_tensor_tensor", BP, [SP, DTA],
                 BP[:, :], SP[:, :], -1.0, DTA[:, :], ALU.mult, ALU.mult)
            p.op("V", "tensor_mul", CP, [SP, dtb], CP[:, :], SP[:, :], dtb[:, :])
            p.op("V", "scalar_tensor_tensor", DP, [DTA, CP],
                 DP[:, :], DTA[:, :], -1.0, CP[:, :], ALU.mult, ALU.mult)
            p.op("V", "tensor_single_scalar", DP, [DP], DP[:, :], DP[:, :], 1.0, ALU.add)
            p.op("V", "tensor_mul", C0, [CP, BZ], C0[:, :], CP[:, :], BZ[:, :])
            p.op("V", "tensor_mul", tmpA, [CP, dtb], tmpA[:, :], CP[:, :], dtb[:, :])
            p.op("V", "tensor_mul", C1p, [tmpA, BZ], C1p[:, :], tmpA[:, :], BZ[:, :])
            p.op("V", "tensor_mul", tmpA, [dtb, BX], tmpA[:, :], dtb[:, :], BX[:, :])
            p.op("V", "tensor_add", C1p, [C1p, tmpA], C1p[:, :], C1p[:, :], tmpA[:, :])
            p.op("V", "tensor_mul", tmpA, [SP, RESETM], tmpA[:, :], SP[:, :], RESETM[:, :])
            p.op("V", "tensor_sub", g0, [SP, tmpA], g0[:, :], SP[:, :], tmpA[:, :])
            p.op("V", "tensor_tensor_scan", DET, [tmpA, g0], DET[:, :], tmpA[:, :], g0[:, :],
                 0.0, ALU.mult, ALU.add)
            def slv(pl, i):
                return pl[:, i::CL]
            for (dst, sa) in [(P00, SP), (P01, BP), (P10, CP), (P11, DP)]:
                p.op("V", "tensor_copy", dst, [sa], slv(dst, 0), slv(sa, 0))
            for i in range(1, CL):
                p.op("V", "tensor_mul", tmpA, [SP, P00], slv(tmpA, i), slv(SP, i), slv(P00, i - 1))
                p.op("V", "tensor_mul", P00, [BP, P10], slv(P00, i), slv(BP, i), slv(P10, i - 1))
                p.op("V", "tensor_add", P00, [P00, tmpA], slv(P00, i), slv(P00, i), slv(tmpA, i))
                p.op("V", "tensor_mul", tmpA, [SP, P01], slv(tmpA, i), slv(SP, i), slv(P01, i - 1))
                p.op("V", "tensor_mul", P01, [BP, P11], slv(P01, i), slv(BP, i), slv(P11, i - 1))
                p.op("V", "tensor_add", P01, [P01, tmpA], slv(P01, i), slv(P01, i), slv(tmpA, i))
                p.op("V", "tensor_mul", tmpA, [CP, P00], slv(tmpA, i), slv(CP, i), slv(P00, i - 1))
                p.op("V", "tensor_mul", P10, [DP, P10], slv(P10, i), slv(DP, i), slv(P10, i - 1))
                p.op("V", "tensor_add", P10, [P10, tmpA], slv(P10, i), slv(P10, i), slv(tmpA, i))
                p.op("V", "tensor_mul", tmpA, [CP, P01], slv(tmpA, i), slv(CP, i), slv(P01, i - 1))
                p.op("V", "tensor_mul", P11, [DP, P11], slv(P11, i), slv(DP, i), slv(P11, i - 1))
                p.op("V", "tensor_add", P11, [P11, tmpA], slv(P11, i), slv(P11, i), slv(tmpA, i))
            p.op("G", "memset", QP00, [], QP00[:, :], 0.0)
            p.op("G", "memset", QP01, [], QP01[:, :], 0.0)
            p.op("G", "memset", QP10, [], QP10[:, :], 0.0)
            p.op("G", "memset", QP11, [], QP11[:, :], 0.0)
            p.op("V", "tensor_single_scalar", QP00, [QP00], QP00[:, 0::CLC], QP00[:, 0::CLC], 1.0, ALU.add)
            p.op("V", "tensor_single_scalar", QP11, [QP11], QP11[:, 0::CLC], QP11[:, 0::CLC], 1.0, ALU.add)

            def pend(pl):
                return pl[:, CL - 1::CL]
            for cl in range(1, CLC):
                def qv(qp, c):
                    return qp[:, c::CLC]
                pe = lambda pl: pl[:, (cl - 1) * CL + CL - 1::C1]
                p.op("V", "tensor_mul", VSW, [P00, QP00], VSW[:, 0:NCHK], pe(P00), qv(QP00, cl - 1))
                p.op("V", "tensor_mul", QP00, [P01, QP10], qv(QP00, cl), pe(P01), qv(QP10, cl - 1))
                p.op("V", "tensor_add", QP00, [QP00, VSW], qv(QP00, cl), qv(QP00, cl), VSW[:, 0:NCHK])
                p.op("V", "tensor_mul", VSW, [P00, QP01], VSW[:, 0:NCHK], pe(P00), qv(QP01, cl - 1))
                p.op("V", "tensor_mul", QP01, [P01, QP11], qv(QP01, cl), pe(P01), qv(QP11, cl - 1))
                p.op("V", "tensor_add", QP01, [QP01, VSW], qv(QP01, cl), qv(QP01, cl), VSW[:, 0:NCHK])
                p.op("V", "tensor_mul", VSW, [P10, QP00], VSW[:, 0:NCHK], pe(P10), qv(QP00, cl - 1))
                p.op("V", "tensor_mul", QP10, [P11, QP10], qv(QP10, cl), pe(P11), qv(QP10, cl - 1))
                p.op("V", "tensor_add", QP10, [QP10, VSW], qv(QP10, cl), qv(QP10, cl), VSW[:, 0:NCHK])
                p.op("V", "tensor_mul", VSW, [P10, QP01], VSW[:, 0:NCHK], pe(P10), qv(QP01, cl - 1))
                p.op("V", "tensor_mul", QP11, [P11, QP11], qv(QP11, cl), pe(P11), qv(QP11, cl - 1))
                p.op("V", "tensor_add", QP11, [QP11, VSW], qv(QP11, cl), qv(QP11, cl), VSW[:, 0:NCHK])
            p.op("V", "tensor_mul", DQP, [QP00, QP11], DQP[:, :], QP00[:, :], QP11[:, :])
            p.op("V", "tensor_mul", VSW, [QP01, QP10], VSW[:, 0:HSN], QP01[:, :], QP10[:, :])
            p.op("V", "tensor_sub", DQP, [DQP, VSW], DQP[:, :], DQP[:, :], VSW[:, 0:HSN])
            lcl = CLC - 1
            pe2 = lambda pl: pl[:, lcl * CL + CL - 1::C1]
            qv2 = lambda qp: qp[:, lcl::CLC]
            for (cc, aa_, bb_, cc_, dd_) in [(0, P00, P01, QP00, QP10), (1, P00, P01, QP01, QP11),
                                             (2, P10, P11, QP00, QP10), (3, P10, P11, QP01, QP11)]:
                p.op("V", "tensor_mul", VSW, [aa_, cc_], VSW[:, 0:NCHK], pe2(aa_), qv2(cc_))
                p.op("V", "tensor_mul", PCH, [bb_, dd_], PCH[:, cc::4], pe2(bb_), qv2(dd_))
                p.op("V", "tensor_add", PCH, [PCH, VSW], PCH[:, cc::4], PCH[:, cc::4], VSW[:, 0:NCHK])
            for j in range(NCHK):
                if j == 0:
                    p.op("V", "tensor_copy", QPRE, [QSPC], QPRE[:, 0:4], QSPC[:, :])
                else:
                    a_, b_, c_, d_ = (PCH[:, (j - 1) * 4 + 0:(j - 1) * 4 + 1],
                                      PCH[:, (j - 1) * 4 + 1:(j - 1) * 4 + 2],
                                      PCH[:, (j - 1) * 4 + 2:(j - 1) * 4 + 3],
                                      PCH[:, (j - 1) * 4 + 3:(j - 1) * 4 + 4])
                    q_ = lambda k: QPRE[:, (j - 1) * 4 + k:(j - 1) * 4 + k + 1]
                    o_ = lambda k: QPRE[:, j * 4 + k:j * 4 + k + 1]
                    p.op("V", "tensor_mul", VSW, [PCH, QPRE], VSW[:, 0:1], a_, q_(0))
                    p.op("V", "tensor_mul", QPRE, [PCH, QPRE], o_(0), b_, q_(2))
                    p.op("V", "tensor_add", QPRE, [QPRE, VSW], o_(0), o_(0), VSW[:, 0:1])
                    p.op("V", "tensor_mul", VSW, [PCH, QPRE], VSW[:, 0:1], a_, q_(1))
                    p.op("V", "tensor_mul", QPRE, [PCH, QPRE], o_(1), b_, q_(3))
                    p.op("V", "tensor_add", QPRE, [QPRE, VSW], o_(1), o_(1), VSW[:, 0:1])
                    p.op("V", "tensor_mul", VSW, [PCH, QPRE], VSW[:, 0:1], c_, q_(0))
                    p.op("V", "tensor_mul", QPRE, [PCH, QPRE], o_(2), d_, q_(2))
                    p.op("V", "tensor_add", QPRE, [QPRE, VSW], o_(2), o_(2), VSW[:, 0:1])
                    p.op("V", "tensor_mul", VSW, [PCH, QPRE], VSW[:, 0:1], c_, q_(1))
                    p.op("V", "tensor_mul", QPRE, [PCH, QPRE], o_(3), d_, q_(3))
                    p.op("V", "tensor_add", QPRE, [QPRE, VSW], o_(3), o_(3), VSW[:, 0:1])
            jl = NCHK - 1
            a_, b_, c_, d_ = (PCH[:, jl * 4 + 0:jl * 4 + 1], PCH[:, jl * 4 + 1:jl * 4 + 2],
                              PCH[:, jl * 4 + 2:jl * 4 + 3], PCH[:, jl * 4 + 3:jl * 4 + 4])
            q_ = lambda k: QPRE[:, jl * 4 + k:jl * 4 + k + 1]
            p.op("V", "tensor_mul", VSW, [PCH, QPRE], VSW[:, 0:1], a_, q_(0))
            p.op("V", "tensor_mul", QSPC, [PCH, QPRE], QSPC[:, 0:1], b_, q_(2))
            p.op("V", "tensor_add", QSPC, [QSPC, VSW], QSPC[:, 0:1], QSPC[:, 0:1], VSW[:, 0:1])
            p.op("V", "tensor_mul", VSW, [PCH, QPRE], VSW[:, 0:1], a_, q_(1))
            p.op("V", "tensor_mul", QSPC, [PCH, QPRE], QSPC[:, 1:2], b_, q_(3))
            p.op("V", "tensor_add", QSPC, [QSPC, VSW], QSPC[:, 1:2], QSPC[:, 1:2], VSW[:, 0:1])
            p.op("V", "tensor_mul", VSW, [PCH, QPRE], VSW[:, 0:1], c_, q_(0))
            p.op("V", "tensor_mul", QSPC, [PCH, QPRE], QSPC[:, 2:3], d_, q_(2))
            p.op("V", "tensor_add", QSPC, [QSPC, VSW], QSPC[:, 2:3], QSPC[:, 2:3], VSW[:, 0:1])
            p.op("V", "tensor_mul", VSW, [PCH, QPRE], VSW[:, 0:1], c_, q_(1))
            p.op("V", "tensor_mul", QSPC, [PCH, QPRE], QSPC[:, 3:4], d_, q_(3))
            p.op("V", "tensor_add", QSPC, [QSPC, VSW], QSPC[:, 3:4], QSPC[:, 3:4], VSW[:, 0:1])

            p.op("V", "tensor_mul", tmpA, [CZ, P00], tmpA[:, :], CZ[:, :], P00[:, :])
            p.op("V", "tensor_mul", g0, [CX, P10], g0[:, :], CX[:, :], P10[:, :])
            p.op("V", "tensor_add", g0, [g0, tmpA], g0[:, :], g0[:, :], tmpA[:, :])
            p.op("V", "tensor_mul", tmpA, [CZ, P01], tmpA[:, :], CZ[:, :], P01[:, :])
            p.op("V", "tensor_mul", g1, [CX, P11], g1[:, :], CX[:, :], P11[:, :])
            p.op("V", "tensor_add", g1, [g1, tmpA], g1[:, :], g1[:, :], tmpA[:, :])
            qb = lambda qp: qp[:, :].rearrange("p (c l) -> p c l", l=CLC).unsqueeze(3).broadcast_to((128, NCHK, CLC, CL))
            pv = lambda pl: pl[:, :].rearrange("p (c l i) -> p c l i", l=CLC, i=CL)
            p.op("V", "tensor_mul", tmpA, [g0, QP00], pv(tmpA), pv(g0), qb(QP00))
            p.op("V", "tensor_mul", Gc0, [g1, QP10], pv(Gc0), pv(g1), qb(QP10))
            p.op("V", "tensor_add", Gc0, [Gc0, tmpA], Gc0[:, :], Gc0[:, :], tmpA[:, :])
            p.op("V", "tensor_mul", tmpA, [g0, QP01], pv(tmpA), pv(g0), qb(QP01))
            p.op("V", "tensor_mul", Gc1, [g1, QP11], pv(Gc1), pv(g1), qb(QP11))
            p.op("V", "tensor_add", Gc1, [Gc1, tmpA], Gc1[:, :], Gc1[:, :], tmpA[:, :])
            for q in range(2):
                for (blk, srcpl, r0) in [(0, Gc0, 64 * q), (1, Gc1, 64 * q),
                                         (2, Gc0, 64 * q + 32), (3, Gc1, 64 * q + 32)]:
                    p.op("S", "activation", GcP[q], [srcpl],
                         GcP[q][32 * blk:32 * blk + 32, :], srcpl[r0:r0 + 32, :], AF.Copy)
            q2 = lambda k: QPRE[:, k::4].rearrange("p c -> p c").unsqueeze(2).unsqueeze(3).broadcast_to((128, NCHK, CLC, CL))
            p.op("V", "tensor_mul", tmpA, [Gc0, QPRE], pv(tmpA), pv(Gc0), q2(0))
            p.op("V", "tensor_mul", g0, [Gc1, QPRE], pv(g0), pv(Gc1), q2(2))
            p.op("V", "tensor_add", g0, [g0, tmpA], g0[:, :], g0[:, :], tmpA[:, :])
            p.op("V", "tensor_mul", tmpA, [Gc0, QPRE], pv(tmpA), pv(Gc0), q2(1))
            p.op("V", "tensor_mul", g1, [Gc1, QPRE], pv(g1), pv(Gc1), q2(3))
            p.op("V", "tensor_add", g1, [g1, tmpA], g1[:, :], g1[:, :], tmpA[:, :])
            for q in range(2):
                for (blk, srcpl, r0) in [(0, g0, 64 * q), (1, g1, 64 * q),
                                         (2, g0, 64 * q + 32), (3, g1, 64 * q + 32)]:
                    p.op("S", "activation", G2S[q], [srcpl],
                         G2S[q][32 * blk:32 * blk + 32, :], srcpl[r0:r0 + 32, :], AF.Copy)
                p.dma("Y", g2co.ap()[(g * 2 + q) * 128:(g * 2 + q + 1) * 128, hsr], G2S[q][:, :])
            p.op("V", "tensor_mul", W0, [P11, C0], W0[:, :], P11[:, :], C0[:, :])
            p.op("V", "tensor_mul", tmpA, [P01, C1p], tmpA[:, :], P01[:, :], C1p[:, :])
            p.op("V", "tensor_sub", W0, [W0, tmpA], W0[:, :], W0[:, :], tmpA[:, :])
            p.op("V", "tensor_mul", W1, [P00, C1p], W1[:, :], P00[:, :], C1p[:, :])
            p.op("V", "tensor_mul", tmpA, [P10, C0], tmpA[:, :], P10[:, :], C0[:, :])
            p.op("V", "tensor_sub", W1, [W1, tmpA], W1[:, :], W1[:, :], tmpA[:, :])
            p.op("V", "tensor_mul", H0, [W0, QP11], pv(H0), pv(W0), qb(QP11))
            p.op("V", "tensor_mul", tmpA, [W1, QP01], pv(tmpA), pv(W1), qb(QP01))
            p.op("V", "tensor_sub", H0, [H0, tmpA], H0[:, :], H0[:, :], tmpA[:, :])
            p.op("V", "tensor_mul", H1, [W1, QP00], pv(H1), pv(W1), qb(QP00))
            p.op("V", "tensor_mul", tmpA, [W0, QP10], pv(tmpA), pv(W0), qb(QP10))
            p.op("V", "tensor_sub", H1, [H1, tmpA], H1[:, :], H1[:, :], tmpA[:, :])
            p.op("V", "tensor_mul", RDT, [DET, DQP], pv(RDT), pv(DET), qb(DQP))
            p.op("V", "reciprocal", RDT, [RDT], RDT[:, :], RDT[:, :])
            p.op("V", "tensor_mul", H0, [H0, RDT], H0[:, :], H0[:, :], RDT[:, :])
            p.op("V", "tensor_mul", H1, [H1, RDT], H1[:, :], H1[:, :], RDT[:, :])
            for q in range(2):
                for (blk, srcpl, r0) in [(0, H0, 64 * q), (1, H1, 64 * q),
                                         (2, H0, 64 * q + 32), (3, H1, 64 * q + 32)]:
                    p.op("S", "activation", HcP[q], [srcpl],
                         HcP[q][32 * blk:32 * blk + 32, :], srcpl[r0:r0 + 32, :], AF.Copy)
            for q in range(2):
                p.op("S", "activation", PAC, [PCH], PAC[0:32, q * NCHK:(q + 1) * NCHK], PCH[32 * (2 * q):32 * (2 * q) + 32, 0::4], AF.Copy)
                p.op("S", "activation", PAC, [PCH], PAC[32:64, q * NCHK:(q + 1) * NCHK], PCH[32 * (2 * q):32 * (2 * q) + 32, 3::4], AF.Copy)
                p.op("S", "activation", PAC, [PCH], PAC[64:96, q * NCHK:(q + 1) * NCHK], PCH[32 * (2 * q + 1):32 * (2 * q + 1) + 32, 0::4], AF.Copy)
                p.op("S", "activation", PAC, [PCH], PAC[96:128, q * NCHK:(q + 1) * NCHK], PCH[32 * (2 * q + 1):32 * (2 * q + 1) + 32, 3::4], AF.Copy)
                p.op("S", "activation", PBC2, [PCH], PBC2[0:32, q * NCHK:(q + 1) * NCHK], PCH[32 * (2 * q):32 * (2 * q) + 32, 1::4], AF.Copy)
                p.op("S", "activation", PBC2, [PCH], PBC2[32:64, q * NCHK:(q + 1) * NCHK], PCH[32 * (2 * q):32 * (2 * q) + 32, 2::4], AF.Copy)
                p.op("S", "activation", PBC2, [PCH], PBC2[64:96, q * NCHK:(q + 1) * NCHK], PCH[32 * (2 * q + 1):32 * (2 * q + 1) + 32, 1::4], AF.Copy)
                p.op("S", "activation", PBC2, [PCH], PBC2[96:128, q * NCHK:(q + 1) * NCHK], PCH[32 * (2 * q + 1):32 * (2 * q + 1) + 32, 2::4], AF.Copy)

            for j in range(NCHK):
                jc = slice(j * C1, (j + 1) * C1)
                for q in range(2):
                    vt = 2 * g + q
                    for sub in range(2):
                        p.op("T", "transpose", ptr, [VAL[vt], IDN],
                             ptr[:, :], VAL[vt][:, hs0 + j * C1 + sub * 128:hs0 + j * C1 + sub * 128 + 128], IDN[:, :])
                        p.op("V", "tensor_copy", utk, [ptr], utk[:, sub * 128:(sub + 1) * 128], ptr[:, :])
                        p.op("T", "transpose", ptr, [HcP[q], IDN],
                             ptr[:, :], HcP[q][:, j * C1 + sub * 128:j * C1 + sub * 128 + 128], IDN[:, :])
                        p.op("V", "tensor_copy", hct, [ptr], hct[:, sub * 128:(sub + 1) * 128], ptr[:, :])
                    for par in range(2):
                        rows64 = slice(64 * par, 64 * par + 64)
                        for (ssb, tsb, k) in [(0, 0, 0), (0, 1, 1), (1, 1, 2)]:
                            p.op("T", "matmul", pA, [HcP[q], GcP[q]],
                                 pA[:, :], HcP[q][rows64, j * C1 + ssb * 128:j * C1 + ssb * 128 + 128],
                                 GcP[q][rows64, j * C1 + tsb * 128:j * C1 + tsb * 128 + 128],
                                 start=True, stop=True)
                            if ssb == tsb:
                                p.op("V", "tensor_mul", AMS, [pA, TRIU],
                                     AMS[:, k * 128:(k + 1) * 128], pA[:, :], TRIU[:, :])
                            else:
                                p.op("V", "tensor_copy", AMS, [pA],
                                     AMS[:, k * 128:(k + 1) * 128], pA[:, :])
                        ycol = lambda tsb: pY[0:64, par * 256 + tsb * 128:par * 256 + tsb * 128 + 128]
                        p.op("T", "matmul", pY, [utk, AMS],
                             ycol(0), utk[:, 0 * 128 + par * 64:0 * 128 + par * 64 + 64],
                             AMS[:, 0:128], start=True, stop=False)
                        p.op("T", "matmul", pY, [VSB[q], GcP[q]],
                             ycol(0), VSB[q][rows64, :],
                             GcP[q][rows64, j * C1:j * C1 + 128], start=False, stop=True)
                        p.op("T", "matmul", pY, [utk, AMS],
                             ycol(1), utk[:, 0 * 128 + par * 64:0 * 128 + par * 64 + 64],
                             AMS[:, 128:256], start=True, stop=False)
                        p.op("T", "matmul", pY, [utk, AMS],
                             ycol(1), utk[:, 1 * 128 + par * 64:1 * 128 + par * 64 + 64],
                             AMS[:, 256:384], start=False, stop=False)
                        p.op("T", "matmul", pY, [VSB[q], GcP[q]],
                             ycol(1), VSB[q][rows64, :],
                             GcP[q][rows64, j * C1 + 128:j * C1 + 256], start=False, stop=True)
                        p.op("T", "matmul", pW, [hct, utk],
                             pW[0:64, par * 64:par * 64 + 64],
                             hct[:, 0 * 128 + par * 64:0 * 128 + par * 64 + 64],
                             utk[:, 0 * 128 + par * 64:0 * 128 + par * 64 + 64],
                             start=True, stop=False)
                        p.op("T", "matmul", pW, [hct, utk],
                             pW[0:64, par * 64:par * 64 + 64],
                             hct[0:128, 1 * 128 + par * 64:1 * 128 + par * 64 + 64],
                             utk[:, 1 * 128 + par * 64:1 * 128 + par * 64 + 64],
                             start=False, stop=True)
                    p.op("V", "tensor_copy", YTS[q], [pY],
                         YTS[q][0:64, hs0 + j * C1:hs0 + (j + 1) * C1], pY[0:64, 0:256])
                    p.op("V", "tensor_copy", YTS[q], [pY],
                         YTS[q][64:128, hs0 + j * C1:hs0 + (j + 1) * C1], pY[0:64, 256:512])
                    p.op("V", "tensor_add", VST[q], [VST[q], pW],
                         VST[q][0:64, :], VST[q][0:64, :], pW[0:64, 0:64])
                    p.op("V", "tensor_add", VST[q], [VST[q], pW],
                         VST[q][64:128, :], VST[q][64:128, :], pW[0:64, 64:128])
                    p.op("S", "activation", VSW, [VST[q]], VSW[0:32, :], VST[q][32:64, :], AF.Copy)
                    p.op("S", "activation", VSW, [VST[q]], VSW[32:64, :], VST[q][0:32, :], AF.Copy)
                    p.op("S", "activation", VSW, [VST[q]], VSW[64:96, :], VST[q][96:128, :], AF.Copy)
                    p.op("S", "activation", VSW, [VST[q]], VSW[96:128, :], VST[q][64:96, :], AF.Copy)
                    p.op("V", "tensor_scalar", VST[q], [VST[q], PAC],
                         VST[q][:, :], VST[q][:, :], PAC[:, q * NCHK + j:q * NCHK + j + 1], None, ALU.mult)
                    p.op("V", "scalar_tensor_tensor", VST[q], [VSW, PBC2, VST[q]],
                         VST[q][:, :], VSW[:, :], PBC2[:, q * NCHK + j:q * NCHK + j + 1], VST[q][:, :],
                         ALU.mult, ALU.add)
                    p.op("V", "tensor_copy", VSB[q], [VST[q]], VSB[q][:, :], VST[q][:, :])
        for q in range(2):
            p.dma("Y", vspo.ap()[(g * 2 + q) * 128:(g * 2 + q + 1) * 128, :], VST[q][:, :])
            p.dma("Y", yto.ap()[(g * 2 + q) * 128:(g * 2 + q + 1) * 128, :], YTS[q][:, :])
        if fused:
            pabst = sb["pabst"]
            for q in range(2):
                he, ho = 32 * (2 * q), 32 * (2 * q + 1)
                c = (g * 2 + q) * 2
                p.op("S", "activation", pabst, [QSPC], pabst[0:32, c:c + 1], QSPC[he:he + 32, 0:1], AF.Copy)
                p.op("S", "activation", pabst, [QSPC], pabst[32:64, c:c + 1], QSPC[he:he + 32, 3:4], AF.Copy)
                p.op("S", "activation", pabst, [QSPC], pabst[64:96, c:c + 1], QSPC[ho:ho + 32, 0:1], AF.Copy)
                p.op("S", "activation", pabst, [QSPC], pabst[96:128, c:c + 1], QSPC[ho:ho + 32, 3:4], AF.Copy)
                p.op("S", "activation", pabst, [QSPC], pabst[0:32, c + 1:c + 2], QSPC[he:he + 32, 1:2], AF.Copy)
                p.op("S", "activation", pabst, [QSPC], pabst[32:64, c + 1:c + 2], QSPC[he:he + 32, 2:3], AF.Copy)
                p.op("S", "activation", pabst, [QSPC], pabst[64:96, c + 1:c + 2], QSPC[ho:ho + 32, 1:2], AF.Copy)
                p.op("S", "activation", pabst, [QSPC], pabst[96:128, c + 1:c + 2], QSPC[ho:ho + 32, 2:3], AF.Copy)
        else:
            p.dma("Y", io["pspan"].ap()[:, g * 4:(g + 1) * 4], QSPC[:, :])
    if fused:
        p.dma("Y", io["pab_in"].ap()[:, :], sb["pabst"][:, :])


def alloc_sec3_small(nc, S, sb, ps):
    HSN = HS // CL
    NCHK = HS // C1
    def P(name, shape, dt):
        if name not in ps:
            ps[name] = nc.alloc_psum_tensor(name, shape, dt)
        return ps[name]
    for nm, shp, dt in [("SELD", [64, 1024], F32), ("SELK", [64, 1024], F32),
                        ("GS32", [128, 4], F32), ("SEL4", [4, 128], F32),
                        ("ACOL", [128, 8], F32),
                        ("QP00", [128, HSN], F32), ("QP01", [128, HSN], F32),
                        ("QP10", [128, HSN], F32), ("QP11", [128, HSN], F32),
                        ("DQP", [128, HSN], F32), ("PCH", [128, 4 * NCHK], F32),
                        ("QPRE", [128, 4 * NCHK], F32), ("QSPC", [128, 4], F32),
                        ("VSW", [128, 64], F32), ("PAC", [128, 2 * NCHK], F32),
                        ("PBC2", [128, 2 * NCHK], F32), ("rns", [4, 512], F32),
                        ("hp6", [128, 6], F32), ("dwc", [128, 4], F32),
                        ("dbc", [128, 1], F32), ("cstg", [128, 3], F32)]:
        if nm not in sb:
            S(nm, shp, dt)
    if "VST" not in sb:
        sb["VST"] = [S(f"VST{q}", [128, 64], F32) for q in range(2)]


def build_launch3(nc):
    io = {}
    for nm, shp, dt, kind in [
            ("value_bf", [DI, TC], BF16, "ExternalInput"),
            ("dtkd", [64, 2 * TC], F32, "ExternalInput"),
            ("halo_prev", [128, 48], F32, "ExternalInput"),
            ("halo_first", [128, 48], F32, "ExternalInput"),
            ("dww", [DI, 4], F32, "ExternalInput"),
            ("dwb", [DI, 1], F32, "ExternalInput"),
            ("wbc_t", [DI, 4096], F32, "ExternalInput"),
            ("acol_log", [128, 8], F32, "ExternalInput"),
            ("seld", [64, 1024], F32, "ExternalInput"),
            ("selk", [64, 1024], F32, "ExternalInput"),
            ("gs32", [128, 4], F32, "ExternalInput"),
            ("sel4", [4, 128], F32, "ExternalInput"),
            ("triu", [128, 128], F32, "ExternalInput"),
            ("yT_bf", [DI, TC], BF16, "ExternalOutput"),
            ("g2c_bf", [DI, TC], BF16, "ExternalOutput"),
            ("vspan", [DI, 64], F32, "ExternalOutput"),
            ("pspan", [128, 32], F32, "ExternalOutput")]:
        io[nm] = nc.dram_tensor(nm, shp, dt, kind=kind)
    io["idn_h"] = nc.inline_tensor(np.eye(128, dtype=ml_dtypes.bfloat16), "idn_const")
    rm = np.ones((128, HS), np.float32)
    rm[:, 0::CL] = 0.0
    io["rm_h"] = nc.inline_tensor(rm, "resetm_const")
    p = Prog(nc)
    sb = {}
    ps = {}

    def S(name, shape, dt):
        sb[name] = nc.alloc_sbuf_tensor(name, shape, dt)
        return sb[name]

    def P(name, shape, dt):
        ps[name] = nc.alloc_psum_tensor(name, shape, dt)
        return ps[name]

    sb["VAL"] = [S(f"VAL{i}", [128, TC], BF16) for i in range(16)]
    sb["WBC"] = [S(f"WBC{i}", [128, 512], BF16) for i in range(16)]
    S("DTF", [64, HS], F32)
    S("KDF", [64, HS], F32)
    S("TRIU", [128, 128], F32)
    S("RESETM", [128, HS], F32)
    sb["PL"] = [S(f"PL{i}", [128, HS], F32) for i in range(18)]
    sb["GcP"] = [S(f"GcP{q}", [128, HS], BF16) for q in range(2)]
    sb["HcP"] = [S(f"HcP{q}", [128, HS], BF16) for q in range(2)]
    sb["G2S"] = [S(f"G2S{q}", [128, HS], BF16) for q in range(2)]
    S("utk", [128, C1], BF16)
    S("hct", [128, C1], BF16)
    S("AMS", [128, 3 * 128], BF16)
    sb["YTS"] = [S(f"YTS{q}", [128, TC], BF16) for q in range(2)]
    sb["VSB"] = [S(f"VSB{q}", [128, 64], BF16) for q in range(2)]
    S("bcs", [128, 512], F32)
    S("sqs", [128, 512], F32)
    S("IDN", [128, 128], BF16)
    alloc_sec3_small(nc, S, sb, ps)
    P("PA", [128, 128], F32)
    P("PY", [64, 512], F32)
    P("PW", [64, 128], F32)
    P("PTR", [128, 128], BF16)
    P("PBC", [128, 512], F32)
    P("P4", [4, 512], F32)
    P("PRN", [128, 512], F32)
    emit_sec3(p, sb, ps, io)
    p.finish()
    p.build()
    return nc


def emit_sec4(p, sb, ps, io, fused=False):
    nc = p.nc
    x4i = io["x4"]; yti = io["yT_bf"]; g2i = io["g2c_bf"]; vali = io["value_bf"]
    gati = io["gate_sil"]; cmki = io["cmask"]; skpi = io["skip_t"]
    wouti = io["wout_t"]; terni = io["tern_t"]; l1bi = io["l1bias_t"]
    asci = io["act_scale"]; yout = io["y"]; yg_h = io["yg_scratch"]
    X = sb["X"]; H = sb["H"]; WT = sb["WT"]
    VSP = sb["VSP"]; PAB = sb["PAB"]
    VINALL = sb["VINALL"]; VIN = sb["VIN"]; VSW = sb["VSW4"]; PMA = sb["PMA"]
    MB = sb["MB"]
    G2L = sb["G2L"]; YTL = sb["YTL"]; VL = sb["VL"]; GL = sb["GL"]
    YGT = sb["YGT"]
    ca = sb["ca"]; SKP = sb["SKP"]; scl = sb["scl"]; bcol = sb["bcol"]
    pyc = ps["PYC"]
    pden = ps["P512a"]
    psc = ps["PSC"]

    if fused:
        p.dma("Y", VSP[0][:, :], io["vsp_out"].ap().rearrange("(cc pt pp) dd -> pp (cc pt dd)", pp=128, dd=64)[:, 0:2048])
        p.dma("Y", VSP[1][:, :], io["vsp_out"].ap().rearrange("(cc pt pp) dd -> pp (cc pt dd)", pp=128, dd=64)[:, 2048:4096])
        p.dma("Y", VSP[2][:, :], io["vsp_out"].ap().rearrange("(cc pt pp) dd -> pp (cc pt dd)", pp=128, dd=64)[:, 4096:6144])
        p.dma("Y", VSP[3][:, :], io["vsp_out"].ap().rearrange("(cc pt pp) dd -> pp (cc pt dd)", pp=128, dd=64)[:, 6144:8192])
        p.dma("Y", PAB[:, :], io["pab_out"].ap().rearrange("(cc pp) k -> pp (cc k)", pp=128))
    else:
        p.dma("Y", VSP[0][:, :], io["vsp_all"].ap()[:, 0:2048])
        p.dma("Y", VSP[1][:, :], io["vsp_all"].ap()[:, 2048:4096])
        p.dma("Y", VSP[2][:, :], io["vsp_all"].ap()[:, 4096:6144])
        p.dma("Y", VSP[3][:, :], io["vsp_all"].ap()[:, 6144:8192])
        p.dma("Y", PAB[:, :], io["pab_all"].ap()[:, :])
    p.dma("Y", SKP[:, :], skpi.ap()[:, :])
    p.dma("Y", sb["lnrows"][0:1, 0:8], cmki.ap()[:, :])
    p.dma("Y", sb["scl"][0:1, 0:1], asci.ap()[:, :])
    p.dma("Y", bcol[:, :], l1bi.ap()[:, :])
    p.op("V", "memset", sb["onesf"], [], sb["onesf"][:, :], 1.0)
    p.op("V", "memset", sb["onesr"], [], sb["onesr"][:, :], 1.0)
    p.op("V", "memset", sb["epsr"], [], sb["epsr"][:, :], 1e-5)
    p.op("T", "matmul", psc, [sb["onesr"], sb["lnrows"]],
         psc[:, :], sb["onesr"][:, :], sb["lnrows"][0:1, 0:8], start=True, stop=True)
    p.op("V", "tensor_copy", MB, [psc], MB[:, :], psc[:, :])

    for pt in range(16):
        p.op("V", "memset", VIN, [], VIN[:, :], 0.0)
        for cc in range(NCORE):
            gcol = (cc * 16 + pt) * 64
            vt_ = VSP[gcol // 2048]
            vsl = vt_[:, gcol % 2048:gcol % 2048 + 64]
            psl = PAB[:, (cc * 16 + pt) * 2:(cc * 16 + pt) * 2 + 2]
            mcol = MB[:, cc:cc + 1]
            p.op("V", "tensor_single_scalar", PMA, [PAB], PMA[:, 0:1], psl[:, 0:1], 1.0, ALU.subtract)
            p.op("V", "tensor_mul", PMA, [PMA, MB], PMA[:, 0:1], PMA[:, 0:1], mcol)
            p.op("V", "tensor_single_scalar", PMA, [PMA], PMA[:, 0:1], PMA[:, 0:1], 1.0, ALU.add)
            p.op("V", "tensor_mul", PMA, [PAB, MB], PMA[:, 1:2], psl[:, 1:2], mcol)
            p.op("S", "activation", VSW, [VIN], VSW[0:32, :], VIN[32:64, :], AF.Copy)
            p.op("S", "activation", VSW, [VIN], VSW[32:64, :], VIN[0:32, :], AF.Copy)
            p.op("S", "activation", VSW, [VIN], VSW[64:96, :], VIN[96:128, :], AF.Copy)
            p.op("S", "activation", VSW, [VIN], VSW[96:128, :], VIN[64:96, :], AF.Copy)
            p.op("V", "tensor_scalar", VIN, [VIN, PMA], VIN[:, :], VIN[:, :],
                 PMA[:, 0:1], None, ALU.mult)
            p.op("V", "scalar_tensor_tensor", VIN, [VSW, PMA, VIN],
                 VIN[:, :], VSW[:, :], PMA[:, 1:2], VIN[:, :], ALU.mult, ALU.add)
            p.op("V", "scalar_tensor_tensor", VIN, [vt_, MB, VIN],
                 VIN[:, :], vsl, mcol, VIN[:, :], ALU.mult, ALU.add)
        p.op("V", "tensor_copy", VINALL, [VIN], VINALL[:, pt * 64:(pt + 1) * 64], VIN[:, :])

    for pt in range(16):
        rows = slice(pt * 128, (pt + 1) * 128)
        p.dma("Y", G2L[:, :], g2i.ap()[rows, :])
        p.dma("Y", YTL[:, :], yti.ap()[rows, :])
        p.dma("G", VL[:, :], vali.ap()[rows, :])
        p.dma("G", GL[:, :], gati.ap()[rows, :])
        for jj in range(TC // C1):
            for par in range(2):
                r64 = slice(64 * par, 64 * par + 64)
                p.op("T", "matmul", pyc, [VINALL, G2L],
                     pyc[0:64, par * 256:(par + 1) * 256],
                     VINALL[r64, pt * 64:(pt + 1) * 64],
                     G2L[r64, jj * C1:(jj + 1) * C1], start=True, stop=True)
            p.op("V", "tensor_add", YTL, [pyc, YTL],
                 YTL[0:64, jj * C1:(jj + 1) * C1], pyc[0:64, 0:256],
                 YTL[0:64, jj * C1:(jj + 1) * C1])
            p.op("V", "tensor_add", YTL, [pyc, YTL],
                 YTL[64:128, jj * C1:(jj + 1) * C1], pyc[0:64, 256:512],
                 YTL[64:128, jj * C1:(jj + 1) * C1])
        p.op("V", "tensor_scalar", ca, [VL, SKP], ca[:, :], VL[:, :],
             SKP[:, pt:pt + 1], None, ALU.mult)
        p.op("V", "tensor_add", ca, [ca, YTL], ca[:, :], ca[:, :], YTL[:, :])
        p.op("V", "tensor_mul", GL, [ca, GL], GL[:, :], ca[:, :], GL[:, :])
        p.dma("Y", yg_h.ap()[rows, :], GL[:, :])

    for i in range(NDT):
        p.dma("Y", X[i][:, :], x4i.ap()[i * 128:(i + 1) * 128, :])
    for i in range(NDT):
        p.dma("G", WT[i][:, :], wouti.ap()[(2 * i) * 128:(2 * i + 1) * 128, :])
    _wt2host = [G2L, YTL, VL, GL]
    WT2 = [_wt2host[i // 2][:, (i % 2) * 1024:(i % 2) * 1024 + 1024] for i in range(NDT)]
    for i in range(NDT):
        p.dma("G", WT2[i], wouti.ap()[(2 * i + 1) * 128:(2 * i + 2) * 128, :])
    for tn in range(NTN):
        c0 = tn * TN
        for ct in range(16):
            p.dma("S", YGT[ct][:, :], yg_h.ap()[ct * 128:(ct + 1) * 128, c0:c0 + TN])
        for do in range(NDT):
            for ct in range(16):
                if ct % 2 == 0:
                    w = WT[ct // 2][:, do * 128:(do + 1) * 128]
                    wd = WT[ct // 2]
                else:
                    w = WT2[ct // 2][:, do * 128:(do + 1) * 128]
                    wd = WT2[ct // 2]
                p.op("T", "matmul", pden, [wd, YGT[ct]],
                     pden[:, :], w, YGT[ct][:, :],
                     start=(ct == 0), stop=(ct == 15))
            p.op("V", "tensor_add", X[do], [pden, X[do]],
                 X[do][:, c0:c0 + TN], pden[:, :], X[do][:, c0:c0 + TN])

    p.op("V", "tensor_single_scalar", scl, [scl], scl[0:1, 0:1], scl[0:1, 0:1], 1e-5, ALU.max)
    p.op("V", "reciprocal", scl, [scl], scl[0:1, 1:2], scl[0:1, 0:1])
    p.op("T", "matmul", psc, [sb["onesr"], scl],
         psc[:, 0:2], sb["onesr"][:, :], scl[0:1, 0:2], start=True, stop=True)
    p.op("V", "tensor_copy", scl, [psc], scl[:, :], psc[:, 0:2])
    emit_ln_quant(p, X, H, sb, ps)
    for i in range(NDT):
        p.dma("Y", sb["sq"][:, 0:1024].bitcast(I32), terni.ap()[i * 128:(i + 1) * 128, :])
        p.op("V", "tensor_copy", ca, [sb["sq"]], ca[:, 0:1024], sb["sq"][:, 0:1024].bitcast(I32))
        p.op("V", "tensor_single_scalar", WT[i], [ca], WT[i][:, :], ca[:, 0:1024], 1.0, ALU.subtract)
    for do in range(NDT):
        for tn in range(NTN):
            c0 = tn * TN
            for di in range(NDT):
                p.op("T", "matmul", pden, [WT[di], H[di]],
                     pden[:, :], WT[di][:, do * 128:(do + 1) * 128], H[di][:, c0:c0 + TN],
                     start=(di == 0), stop=(di == NDT - 1))
            p.op("V", "tensor_scalar", ca, [pden, scl, bcol],
                 ca[:, 0:TN], pden[:, :], scl[:, 0:1], bcol[:, do:do + 1],
                 ALU.mult, ALU.add)
            p.op("V", "scalar_tensor_tensor", X[do], [ca, X[do]],
                 X[do][:, c0:c0 + TN], ca[:, 0:TN], 0.5, X[do][:, c0:c0 + TN],
                 ALU.mult, ALU.add)
    for i in range(NDT):
        p.dma("Y", yout.ap()[i * 128:(i + 1) * 128, :], X[i][:, :])


def build_launch4(nc):
    io = {}
    for nm, shp, dt, kind in [
            ("x4", [D, TC], F32, "ExternalInput"),
            ("yT_bf", [DI, TC], BF16, "ExternalInput"),
            ("g2c_bf", [DI, TC], BF16, "ExternalInput"),
            ("value_bf", [DI, TC], BF16, "ExternalInput"),
            ("gate_sil", [DI, TC], BF16, "ExternalInput"),
            ("vsp_all", [128, NCORE * 16 * 64], F32, "ExternalInput"),
            ("pab_all", [128, NCORE * 16 * 2], F32, "ExternalInput"),
            ("cmask", [1, 8], F32, "ExternalInput"),
            ("skip_t", [128, 16], F32, "ExternalInput"),
            ("wout_t", [DI, D], F32, "ExternalInput"),
            ("tern_t", [D, D], I32, "ExternalInput"),
            ("l1bias_t", [128, 8], F32, "ExternalInput"),
            ("act_scale", [1, 1], F32, "ExternalInput"),
            ("y", [D, TC], F32, "ExternalOutput")]:
        io[nm] = nc.dram_tensor(nm, shp, dt, kind=kind)
    io["yg_scratch"] = nc.dram_tensor("yg_scratch", [DI, TC], BF16, kind="Internal")
    p = Prog(nc)
    sb = {}
    ps = {}
    S = std_alloc(nc, sb, ps)
    sb["VSP"] = [S(f"VSPt{i}", [128, 2048], F32) for i in range(4)]
    S("PAB", [128, NCORE * 16 * 2], F32)
    S("VINALL", [128, 16 * 64], BF16)
    S("VIN", [128, 64], F32)
    S("VSW4", [128, 64], F32)
    S("PMA", [128, 2], F32)
    S("MB", [128, 8], F32)
    S("G2L", [128, TC], BF16)
    S("YTL", [128, TC], BF16)
    S("VL", [128, TC], BF16)
    S("GL", [128, TC], BF16)
    sb["YGT"] = [S(f"YGT{i}", [128, TN], BF16) for i in range(16)]
    S("ca", [128, TC], F32)
    S("SKP", [128, 16], F32)
    S("scl", [128, 2], F32)
    S("bcol", [128, 8], F32)
    ps["PYC"] = nc.alloc_psum_tensor("PYC", [64, 512], F32)
    ps["P512a"] = nc.alloc_psum_tensor("P512a", [128, TN], F32)
    ps["P1024"] = nc.alloc_psum_tensor("P1024", [128, 1024], F32)
    ps["PSC"] = nc.alloc_psum_tensor("PSC", [128, 8], F32)
    emit_common_init(p, sb)
    emit_sec4(p, sb, ps, io)
    p.finish()
    p.build()
    return nc


def run_launch4(inputs, r2, r3):
    nc = _get_nc("l4", build_launch4)
    f32 = np.float32
    wout_t = np.ascontiguousarray(np.asarray(inputs["L1_attn_W_out"], f32).T)
    tern_t = np.ascontiguousarray(np.asarray(inputs["L1_ffn_ternary_weight"], np.int32).T)
    l1bias_t = np.ascontiguousarray(np.asarray(inputs["L1_ffn_bias"], f32).reshape(8, 128).T)
    act_scale = np.asarray(inputs["L1_ffn_act_scale"], f32).reshape(1, 1)
    skip_t = np.ascontiguousarray(np.asarray(inputs["L1_attn_skip"], f32).reshape(16, 128).T)
    vsp = np.zeros((128, NCORE * 16 * 64), f32)
    pab = np.zeros((128, NCORE * 16 * 2), f32)
    for cc in range(NCORE):
        v = np.asarray(r3[cc]["vspan"])
        pspan = np.asarray(r3[cc]["pspan"])
        for pt in range(16):
            vsp[:, (cc * 16 + pt) * 64:(cc * 16 + pt) * 64 + 64] = v[pt * 128:(pt + 1) * 128, :]
            g, q = pt // 2, pt % 2
            he = 32 * (2 * q)
            ho = 32 * (2 * q + 1)
            pa = np.concatenate([pspan[he:he + 32, g * 4 + 0], pspan[he:he + 32, g * 4 + 3],
                                 pspan[ho:ho + 32, g * 4 + 0], pspan[ho:ho + 32, g * 4 + 3]])
            pb = np.concatenate([pspan[he:he + 32, g * 4 + 1], pspan[he:he + 32, g * 4 + 2],
                                 pspan[ho:ho + 32, g * 4 + 1], pspan[ho:ho + 32, g * 4 + 2]])
            pab[:, (cc * 16 + pt) * 2] = pa
            pab[:, (cc * 16 + pt) * 2 + 1] = pb
    in_maps = []
    for core in range(NCORE):
        m = {"x4": np.asarray(r2[core]["x4"]),
             "yT_bf": np.asarray(r3[core]["yT_bf"]),
             "g2c_bf": np.asarray(r3[core]["g2c_bf"]),
             "value_bf": np.asarray(r2[core]["value_bf"]),
             "gate_sil": np.asarray(r2[core]["gate_sil"]),
             "vsp_all": vsp, "pab_all": pab,
             "skip_t": skip_t, "wout_t": wout_t, "tern_t": tern_t,
             "l1bias_t": l1bias_t, "act_scale": act_scale}
        msk = np.zeros((1, 8), np.float32)
        for cc in range(NCORE):
            if cc // 4 == core // 4 and cc < core:
                msk[0, cc] = 1.0
        m["cmask"] = msk
        in_maps.append(m)
    res = run_bass_kernel_spmd(nc, in_maps, core_ids=list(range(NCORE)))
    return res.results


def host_common(inputs):
    f32 = np.float32
    c = {}
    c["wq_t"] = np.ascontiguousarray(np.asarray(inputs["L0_attn_Wq"], f32).T)
    c["wk_t"] = np.ascontiguousarray(np.asarray(inputs["L0_attn_Wk"], f32).T)
    c["wv_t"] = np.ascontiguousarray(np.asarray(inputs["L0_attn_Wv"], f32).T)
    c["wo_t"] = np.ascontiguousarray(np.asarray(inputs["L0_attn_Wo"], f32).T)
    c["l0w_t"] = np.ascontiguousarray(np.asarray(inputs["L0_ffn_weight"], f32).T)
    c["disha"] = np.ascontiguousarray(np.asarray(inputs["L0_ffn_disha"], f32))
    c["gsm"] = np.vstack([np.eye(64, dtype=f32), np.eye(64, dtype=f32)])
    c["triu"] = np.triu(np.ones((128, 128), dtype=f32))
    return c


def host_core(inputs, core):
    f32 = np.float32
    b, t0 = core // 4, (core % 4) * TC
    d = {}
    x = np.asarray(inputs["L0_attn_x"], f32)
    d["x"] = np.ascontiguousarray(x[b, t0:t0 + TC, :].T)
    half = 32
    inv_freq = (10000.0 ** (-np.arange(half, dtype=f32) / half)).astype(f32)
    pos = np.arange(t0, t0 + TC, dtype=f32)
    ang = pos[None, :] * inv_freq[:, None]
    cos, sin = np.cos(ang).astype(f32), np.sin(ang).astype(f32)
    d["cc"] = np.ascontiguousarray(np.tile(cos, (4, 1)))
    d["ssx"] = np.ascontiguousarray(np.concatenate([sin, -sin, sin, -sin], axis=0))
    return d


_nc_cache = {}


def _get_nc(key, builder):
    if key not in _nc_cache:
        nc = bass.Bass("TRN2", target_bir_lowering=False)
        builder(nc)
        _nc_cache[key] = nc
    return _nc_cache[key]


def host_common2(inputs):
    f32 = np.float32
    c = {}
    c["wo_t"] = np.ascontiguousarray(np.asarray(inputs["L0_attn_Wo"], f32).T)
    c["l0w_t"] = np.ascontiguousarray(np.asarray(inputs["L0_ffn_weight"], f32).T)
    c["disha"] = np.ascontiguousarray(np.asarray(inputs["L0_ffn_disha"], f32))
    c["gsm"] = np.vstack([np.eye(64, dtype=f32), np.eye(64, dtype=f32)])
    c["tern_t"] = np.ascontiguousarray(np.asarray(inputs["L1_ffn_ternary_weight"], np.int32).T)
    c["l1bias_t"] = np.ascontiguousarray(np.asarray(inputs["L1_ffn_bias"], f32).reshape(8, 128).T)
    c["act_scale"] = np.asarray(inputs["L1_ffn_act_scale"], f32).reshape(1, 1)
    c["win_t"] = np.ascontiguousarray(np.asarray(inputs["L1_attn_W_in"], f32).T)
    c["dww"] = np.ascontiguousarray(np.asarray(inputs["L1_attn_dw_weight"], f32))
    c["dwb"] = np.ascontiguousarray(np.asarray(inputs["L1_attn_dw_bias"], f32).reshape(DI, 1))
    dtb = np.zeros((64, 1), f32)
    dtb[0::2, 0] = np.asarray(inputs["L1_attn_dt_bias"], f32)
    c["dtb64"] = dtb
    return c


def run_launch2(inputs, r1):
    com = host_common2(inputs)
    nc = _get_nc("l2", build_launch2)
    kvall = np.concatenate([np.asarray(r1[c]["kvloc"]) for c in range(NCORE)], axis=0)
    ksall = np.concatenate([np.asarray(r1[c]["ksloc"]) for c in range(NCORE)], axis=0)
    in_maps = []
    for core in range(NCORE):
        m = dict(com)
        m["x1"] = np.asarray(r1[core]["x1"])
        m["qphi"] = np.asarray(r1[core]["qphi"])
        m["num_part"] = np.asarray(r1[core]["num_part"])
        m["den_part"] = np.asarray(r1[core]["den_part"])
        m["kvall"] = kvall
        m["ksall"] = ksall
        msk = np.zeros((1, 8), np.float32)
        for cc in range(NCORE):
            if cc // 4 == core // 4 and cc < core:
                msk[0, cc] = 1.0
        m["cmask"] = msk
        in_maps.append(m)
    res = run_bass_kernel_spmd(nc, in_maps, core_ids=list(range(NCORE)))
    return res.results


def host_common3(inputs):
    f32 = np.float32
    c = {}
    c["wbc_t"] = np.ascontiguousarray(np.asarray(inputs["L1_attn_W_bc"], f32).T)
    c["dww"] = np.ascontiguousarray(np.asarray(inputs["L1_attn_dw_weight"], f32))
    c["dwb"] = np.ascontiguousarray(np.asarray(inputs["L1_attn_dw_bias"], f32).reshape(DI, 1))
    alog = np.asarray(inputs["L1_attn_A_log"], f32)
    acol = np.zeros((128, 8), f32)
    for g in range(8):
        for pp in range(128):
            acol[pp, g] = alog[4 * g + pp // 32, pp % 32]
    c["acol_log"] = acol
    seld = np.zeros((64, 1024), f32)
    selk = np.zeros((64, 1024), f32)
    for g in range(8):
        for pp in range(128):
            h = 4 * g + pp // 32
            seld[2 * h, g * 128 + pp] = 1.0
            selk[2 * h + 1, g * 128 + pp] = 1.0
    c["seld"] = seld
    c["selk"] = selk
    gs32 = np.zeros((128, 4), f32)
    for pp in range(128):
        gs32[pp, pp // 32] = 1.0
    c["gs32"] = gs32
    c["sel4"] = np.ascontiguousarray(gs32.T)
    c["triu"] = np.triu(np.ones((128, 128), dtype=f32))
    return c


def run_launch3(inputs, r2):
    com = host_common3(inputs)
    nc = _get_nc("l3", build_launch3)
    in_maps = []
    for core in range(NCORE):
        m = dict(com)
        m["value_bf"] = np.asarray(r2[core]["value_bf"])
        m["dtkd"] = np.asarray(r2[core]["dtkd"])
        m["halo_first"] = np.asarray(r2[core]["halo_first"])
        if core % 4 == 0:
            m["halo_prev"] = np.zeros((128, 48), np.float32)
        else:
            m["halo_prev"] = np.asarray(r2[core - 1]["halo_last"])
        in_maps.append(m)
    res = run_bass_kernel_spmd(nc, in_maps, core_ids=list(range(NCORE)))
    return res.results


def run_launch1(inputs):
    com = host_common(inputs)
    nc = _get_nc("l1", build_launch1)
    in_maps = []
    for core in range(NCORE):
        m = dict(com)
        del m["wo_t"]
        m.update(host_core(inputs, core))
        in_maps.append(m)
    res = run_bass_kernel_spmd(nc, in_maps, core_ids=list(range(NCORE)))
    return res.results


def kernel(**inputs):
    r1 = run_launch1(inputs)
    r2 = run_launch2(inputs, r1)
    r3 = run_launch3(inputs, r2)
    r4 = run_launch4(inputs, r2, r3)
    out = np.zeros((B, T, D), np.float32)
    for core in range(NCORE):
        b, t0 = core // 4, (core % 4) * TC
        out[b, t0:t0 + TC, :] = np.asarray(r4[core]["y"]).T
    return out


# version 08
# speedup vs baseline: 1.0061x; 1.0061x over previous
import numpy as np
import ml_dtypes
import concourse.bass as bass
import concourse.mybir as mybir
from concourse.bass_utils import run_bass_kernel_spmd

F32 = mybir.dt.float32
BF16 = mybir.dt.bfloat16
I32 = mybir.dt.int32
AF = mybir.ActivationFunctionType
ALU = mybir.AluOpType

NCORE = 8
B, T, D = 2, 8192, 1024
TC = 2048
NDT = D // 128
TN = 512
NTN = TC // TN
CH = 128
NCH = TC // CH
MAGIC = float(3 * 2.0 ** 22)

DI = 2048
NH1 = 32
NST = 32
DH1 = 64
C1 = 256
CL = 16
HS = 1024

ENGINES = ("T", "V", "S", "G", "Y")
ENG_ATTR = {"T": "tensor", "V": "vector", "S": "scalar", "G": "gpsimd", "Y": "sync"}


class Prog:

    def __init__(self, nc):
        self.nc = nc
        self.q = {e: [] for e in ENGINES}
        self.cnt = {e: 0 for e in ENGINES}
        self.dma_cnt = {e: 0 for e in ENGINES}
        self.last_write = {}
        self.readers = {}
        self.known = {e: {} for e in ENGINES}
        self.sems = {}

    def _sem(self, kind, e):
        key = (kind, e)
        if key not in self.sems:
            self.sems[key] = self.nc.semaphore(f"sem_{kind}_{e}").__enter__()
        return self.sems[key]

    def _wait(self, e, dep):
        kind, de, val = dep
        k = (kind, de)
        if self.known[e].get(k, 0) >= val:
            return
        self.known[e][k] = val
        sem = self._sem(kind, de)
        self.q[e].append(("wait", sem, val))

    def _deps_for_read(self, e, names):
        for n in names:
            d = self.last_write.get(n)
            if d is not None:
                self._wait(e, d)

    def _deps_for_write(self, e, name):
        d = self.last_write.get(name)
        if d is not None:
            self._wait(e, d)
        for r in self.readers.get(name, ()):
            self._wait(e, r)

    @staticmethod
    def _nm(ap):
        return ap.name if hasattr(ap, "name") else None

    def op(self, e, method, out, ins, *args, _opname=None, **kwargs):
        outs = out if isinstance(out, (list, tuple)) else [out]
        in_names = [self._nm(a) for a in ins if self._nm(a)]
        out_names = [self._nm(a) for a in outs if self._nm(a)]
        self._deps_for_read(e, in_names)
        for on in out_names:
            self._deps_for_write(e, on)
        self.cnt[e] += 1
        cval = self.cnt[e]
        sem = self._sem("c", e)
        self.q[e].append(("op", method, args, kwargs, sem))
        for on in out_names:
            self.last_write[on] = ("c", e, cval)
            self.readers[on] = []
        for n in in_names:
            self.readers.setdefault(n, []).append(("c", e, cval))

    def dma(self, e, out, in_):
        names_in = [self._nm(in_)] if self._nm(in_) else []
        self._deps_for_read(e, names_in)
        on = self._nm(out)
        if on:
            self._deps_for_write(e, on)
        if self.dma_cnt[e]:
            self._wait(e, ("d", e, 16 * self.dma_cnt[e]))
        self.dma_cnt[e] += 1
        dval = 16 * self.dma_cnt[e]
        sem = self._sem("d", e)
        self.q[e].append(("dma", out, in_, sem))
        if on:
            self.last_write[on] = ("d", e, dval)
            self.readers[on] = []
        if names_in:
            self.readers.setdefault(names_in[0], []).append(("d", e, dval))

    def collective(self, in_ap, out_ap, in_name, out_name):
        e = "G"
        self._deps_for_read(e, [in_name])
        self._deps_for_write(e, out_name)
        if self.dma_cnt[e]:
            self._wait(e, ("d", e, 16 * self.dma_cnt[e]))
        self.dma_cnt[e] += 1
        dval = 16 * self.dma_cnt[e]
        sem = self._sem("d", e)
        self.q[e].append(("cc", in_ap, out_ap, sem))
        self.last_write[out_name] = ("d", e, dval)
        self.readers[out_name] = []
        self.readers.setdefault(in_name, []).append(("d", e, dval))

    def barrier_all(self):
        for e in ENGINES:
            for ee in ENGINES:
                if self.cnt[ee]:
                    self._wait(e, ("c", ee, self.cnt[ee]))
                if self.dma_cnt[ee]:
                    self._wait(e, ("d", ee, 16 * self.dma_cnt[ee]))

    def barrier_dma(self, e):
        if self.dma_cnt[e]:
            self._wait(e, ("d", e, 16 * self.dma_cnt[e]))

    def finish(self):
        e = "Y"
        for ee in ENGINES:
            if self.cnt[ee]:
                self._wait(e, ("c", ee, self.cnt[ee]))
            if self.dma_cnt[ee]:
                self._wait(e, ("d", ee, 16 * self.dma_cnt[ee]))

    def build(self):
        nc = self.nc
        with nc.Block() as block:
            for e in ENGINES:
                ops = self.q[e]
                if not ops:
                    continue

                def make(eng_ops):
                    def body(eng):
                        for it in eng_ops:
                            if it[0] == "wait":
                                eng.wait_ge(it[1], it[2])
                            elif it[0] == "op":
                                _, method, args, kwargs, sem = it
                                getattr(eng, method)(*args, **kwargs).then_inc(sem, 1)
                            elif it[0] == "cc":
                                _, in_ap, out_ap, sem = it
                                eng.collective_compute(
                                    "AllGather", mybir.AluOpType.bypass,
                                    replica_groups=[list(range(NCORE))],
                                    ins=[in_ap], outs=[out_ap]).then_inc(sem, 16)
                            else:
                                _, out, in_, sem = it
                                eng.dma_start(out=out, in_=in_).then_inc(sem, 16)
                    return body

                getattr(block, ENG_ATTR[e])(make(ops))



def emit_ln(p, X, H, sb, ps):
    nc = p.nc
    sq = sb["sq"]
    onesf = sb["onesf"]
    onesr = sb["onesr"]
    pstat = ps["P512a"]
    pbig = ps["P1024"]
    lnr = sb["lnrows"]
    mean = lnr[:, 0:TN]
    m2 = lnr[:, TN:2 * TN]
    rb = sb["rb"]
    nb = sb["nb"]
    tmp = sb["lntmp"]

    for tn in range(NTN):
        c0 = tn * TN
        for di in range(NDT):
            p.op("T", "matmul", pstat[0:1, :], [onesf, X[di]],
                 pstat[0:1, :], onesf[:, :], X[di][:, c0:c0 + TN],
                 start=(di == 0), stop=(di == NDT - 1))
        for di in range(NDT):
            sl = sq[:, (di % 2) * TN:(di % 2) * TN + TN]
            p.op("S", "activation", sl, [X[di]], sl, X[di][:, c0:c0 + TN], AF.Square)
            p.op("T", "matmul", pbig, [onesf, sq],
                 pbig[0:1, TN:2 * TN], onesf[:, :], sl,
                 start=(di == 0), stop=(di == NDT - 1))
        p.op("V", "tensor_single_scalar", lnr, [pstat], mean, pstat[0:1, :], 1.0 / D, ALU.mult)
        p.op("V", "tensor_single_scalar", lnr, [pbig], m2, pbig[0:1, TN:2 * TN], 1.0 / D, ALU.mult)
        p.op("V", "tensor_mul", rb, [lnr], rb[0:1, :], mean, mean)
        p.op("V", "tensor_sub", lnr, [lnr, rb], m2, m2, rb[0:1, :])
        p.op("S", "activation", lnr, [lnr], m2, m2, AF.Sqrt, bias=sb["epsr"][0:1, 0:1])
        p.op("V", "reciprocal", rb, [lnr], rb[0:1, :], m2)
        p.op("V", "scalar_tensor_tensor", lnr, [lnr, rb],
             m2, mean, -1.0, rb[0:1, :], ALU.mult, ALU.mult)
        p.op("T", "matmul", pbig[:, 0:TN], [onesr, rb],
             pbig[:, 0:TN], onesr[:, :], rb[0:1, :], start=True, stop=True)
        p.op("T", "matmul", pbig[:, TN:2 * TN], [onesr, lnr],
             pbig[:, TN:2 * TN], onesr[:, :], m2, start=True, stop=True)
        p.op("V", "tensor_copy", rb, [pbig], rb[:, :], pbig[:, 0:TN])
        p.op("V", "tensor_copy", nb, [pbig], nb[:, :], pbig[:, TN:2 * TN])
        for di in range(NDT):
            p.op("V", "tensor_mul", tmp, [X[di], rb], tmp[:, :], X[di][:, c0:c0 + TN], rb[:, :])
            p.op("V", "tensor_add", H[di], [tmp, nb], H[di][:, c0:c0 + TN], tmp[:, :], nb[:, :])


def emit_ln_quant(p, X, H, sb, ps):
    sq = sb["sq"]; onesf = sb["onesf"]; onesr = sb["onesr"]
    pstat = ps["P512a"]; pbig = ps["P1024"]
    lnr = sb["lnrows"]
    mean = lnr[:, 0:TN]
    m2 = lnr[:, TN:2 * TN]
    rb = sb["rb"]; nb = sb["nb"]; tmp = sb["lntmp"]
    scl = sb["scl"]
    for tn in range(NTN):
        c0 = tn * TN
        for di in range(NDT):
            p.op("T", "matmul", pstat[0:1, :], [onesf, X[di]],
                 pstat[0:1, :], onesf[:, :], X[di][:, c0:c0 + TN],
                 start=(di == 0), stop=(di == NDT - 1))
        for di in range(NDT):
            sl = sq[:, (di % 2) * TN:(di % 2) * TN + TN]
            p.op("S", "activation", sl, [X[di]], sl, X[di][:, c0:c0 + TN], AF.Square)
            p.op("T", "matmul", pbig, [onesf, sq],
                 pbig[0:1, TN:2 * TN], onesf[:, :], sl,
                 start=(di == 0), stop=(di == NDT - 1))
        p.op("V", "tensor_single_scalar", lnr, [pstat], mean, pstat[0:1, :], 1.0 / D, ALU.mult)
        p.op("V", "tensor_single_scalar", lnr, [pbig], m2, pbig[0:1, TN:2 * TN], 1.0 / D, ALU.mult)
        p.op("V", "tensor_mul", rb, [lnr], rb[0:1, :], mean, mean)
        p.op("V", "tensor_sub", lnr, [lnr, rb], m2, m2, rb[0:1, :])
        p.op("S", "activation", lnr, [lnr], m2, m2, AF.Sqrt, bias=sb["epsr"][0:1, 0:1])
        p.op("V", "reciprocal", rb, [lnr], rb[0:1, :], m2)
        p.op("V", "scalar_tensor_tensor", lnr, [lnr, rb],
             m2, mean, -1.0, rb[0:1, :], ALU.mult, ALU.mult)
        p.op("T", "matmul", pbig[:, 0:TN], [onesr, rb],
             pbig[:, 0:TN], onesr[:, :], rb[0:1, :], start=True, stop=True)
        p.op("T", "matmul", pbig[:, TN:2 * TN], [onesr, lnr],
             pbig[:, TN:2 * TN], onesr[:, :], m2, start=True, stop=True)
        p.op("V", "tensor_copy", rb, [pbig], rb[:, :], pbig[:, 0:TN])
        p.op("V", "tensor_copy", nb, [pbig], nb[:, :], pbig[:, TN:2 * TN])
        for di in range(NDT):
            p.op("V", "tensor_mul", tmp, [X[di], rb], tmp[:, :], X[di][:, c0:c0 + TN], rb[:, :])
            p.op("V", "tensor_add", tmp, [tmp, nb], tmp[:, :], tmp[:, :], nb[:, :])
            p.op("V", "tensor_scalar", tmp, [tmp, scl], tmp[:, :], tmp[:, :],
                 scl[:, 1:2], 127.0, ALU.mult, ALU.min)
            p.op("V", "tensor_scalar", tmp, [tmp], tmp[:, :], tmp[:, :],
                 -127.0, MAGIC, ALU.max, ALU.add)
            p.op("V", "tensor_single_scalar", H[di], [tmp], H[di][:, c0:c0 + TN],
                 tmp[:, :], MAGIC, ALU.subtract)


def emit_l0ffn(p, X, H, W, DSH, GSM, sb, ps):
    pool_ps = ps["P512a"]
    pooled = sb["pooled"]
    po = ps["P512a"]
    for tn in range(NTN):
        c0 = tn * TN
        for di in range(NDT):
            p.op("T", "matmul", pool_ps[0:64, :], [GSM, H[di]],
                 pool_ps[0:64, :], GSM[:, :], H[di][:, c0:c0 + TN],
                 start=(di == 0), stop=(di == NDT - 1))
        p.op("S", "activation", pooled, [pool_ps],
             pooled[:, c0:c0 + TN], pool_ps[0:64, :], AF.Copy)
    for do in range(NDT):
        for tn in range(NTN):
            c0 = tn * TN
            for di in range(NDT):
                p.op("T", "matmul", po, [W[di], H[di]],
                     po[:, :], W[di][:, do * 128:(do + 1) * 128], H[di][:, c0:c0 + TN],
                     start=(di == 0), stop=False)
            p.op("T", "matmul", po, [DSH, pooled],
                 po[:, :], DSH[:, do * 128:(do + 1) * 128], pooled[:, c0:c0 + TN],
                 start=False, stop=True)
            p.op("V", "scalar_tensor_tensor", X[do], [po, X[do]],
                 X[do][:, c0:c0 + TN], po[:, :], 0.5, X[do][:, c0:c0 + TN],
                 ALU.mult, ALU.add)


def rope_phi(p, QT, W, H, CC, SSX, sb, ps, tag):
    qf = sb["qf"]
    qs = sb["qs"]
    qw = sb["qw"]
    po = ps["P512a"]
    for t in range(NDT):
        for tn in range(NTN):
            c0 = tn * TN
            for di in range(NDT):
                p.op("T", "matmul", po, [W[di], H[di]],
                     po[:, :], W[di][:, t * 128:(t + 1) * 128], H[di][:, c0:c0 + TN],
                     start=(di == 0), stop=(di == NDT - 1))
            p.op("S", "activation", qf, [po], qf[:, c0:c0 + TN], po[:, :], AF.Copy)
        p.op("V", "tensor_mul", qs, [qf, SSX], qs[:, :], qf[:, :], SSX[:, :])
        p.op("S", "activation", qw, [qs], qw[0:32, :], qs[32:64, :], AF.Copy)
        p.op("S", "activation", qw, [qs], qw[32:64, :], qs[0:32, :], AF.Copy)
        p.op("S", "activation", qw, [qs], qw[64:96, :], qs[96:128, :], AF.Copy)
        p.op("S", "activation", qw, [qs], qw[96:128, :], qs[64:96, :], AF.Copy)
        p.op("V", "tensor_mul", qs, [qf, CC], qs[:, :], qf[:, :], CC[:, :])
        p.op("V", "tensor_add", qw, [qs, qw], qw[:, :], qs[:, :], qw[:, :])
        p.op("V", "tensor_single_scalar", qs, [qw], qs[:, :], qw[:, :], 0.0, ALU.min)
        p.op("S", "activation", qf, [qs], qf[:, :], qs[:, :], AF.Exp)
        p.op("V", "tensor_single_scalar", qs, [qw], qs[:, :], qw[:, :], 0.0, ALU.max)
        p.op("V", "tensor_add", QT[t], [qs, qf], QT[t][:, :], qs[:, :], qf[:, :])


def std_alloc(nc, sb, ps):
    def S(name, shape, dt):
        sb[name] = nc.alloc_sbuf_tensor(name, shape, dt)
        return sb[name]
    X = [S(f"X{i}", [128, TC], F32) for i in range(NDT)]
    H = [S(f"H{i}", [128, TC], BF16) for i in range(NDT)]
    WT = [S(f"WT{i}", [128, D], BF16) for i in range(NDT)]
    S("sq", [128, 2 * TN], F32)
    S("onesf", [128, 1], F32)
    S("onesr", [1, 128], F32)
    S("onesbf", [128, 1], BF16)
    S("lnrows", [1, 2 * TN], F32)
    S("epsr", [1, 1], F32)
    S("rb", [128, TN], F32); S("nb", [128, TN], F32); S("lntmp", [128, TN], F32)
    sb["X"] = X; sb["H"] = H; sb["WT"] = WT
    return S


def emit_common_init(p, sb):
    p.op("V", "memset", sb["onesf"], [], sb["onesf"][:, :], 1.0)
    p.op("V", "memset", sb["epsr"], [], sb["epsr"][:, :], 1e-5)
    p.op("V", "memset", sb["onesr"], [], sb["onesr"][:, :], 1.0)
    p.op("V", "memset", sb["onesbf"], [], sb["onesbf"][:, :], 1.0)


def emit_sec1(p, sb, ps, io, fused=False):
    nc = p.nc
    xin = io["x"]; wq = io["wq_t"]; wk = io["wk_t"]; wv = io["wv_t"]
    l0w = io["l0w_t"]; dsh = io["disha"]; ccd = io["cc"]; ssd = io["ssx"]
    gsd = io["gsm"]; trid = io["triu"]
    x1o = io["x1"]; numo = io["num_part"]; deno = io["den_part"]
    kvo = io["kvloc"]; kso = io["ksloc"]
    X = sb["X"]; H = sb["H"]; QP = sb["QP"]; KP = sb["KP"]; WT = sb["WT"]
    DSH = sb["DSH"]; GSM = sb["GSM"]; TRI = sb["TRI"]
    CC = X[3]
    SSX = X[4]
    sb["qf"] = X[0]; sb["qs"] = X[1]; sb["qw"] = X[2]
    vtok = sb["vtok"]; ktok = sb["ktok"]; smh = sb["smh"]
    kvbf = sb["kvbf"]; ksbf = sb["ksbf"]
    nstg = X[5]
    dstg = sb["dstg"]
    kvst = X[6]
    IDN = sb["IDN"]
    pden = ps["P512a"]
    pS = ps["PS"]
    pkv = ps["PKV"]
    pks = ps["PKS"]
    ptr = ps["PTR"]
    idn_h = io["idn_h"]

    for i in range(NDT):
        p.dma("Y", X[i][:, :], xin.ap()[i * 128:(i + 1) * 128, :])
    for i in range(NDT):
        p.dma("G", WT[i][:, :], l0w.ap()[i * 128:(i + 1) * 128, :])
    p.dma("G", DSH[:, :], dsh.ap()[:, :])
    p.dma("G", GSM[:, :], gsd.ap()[:, :])
    p.dma("Y", TRI[:, :], trid.ap()[:, :])
    p.dma("Y", IDN[:, :], idn_h.ap()[:, :])
    zc = sb["zc"]
    zr = sb["zr"]
    p.op("V", "memset", zc, [], zc[:, :], 0.0)
    p.op("V", "memset", zr, [], zr[:, :], 0.0)
    p.op("T", "matmul", pkv, [zc, zr], pkv[:, 0:512], zc[:, 0:64], zr[:, 0:512], start=True, stop=True)
    p.op("T", "matmul", pkv, [zc, zr], pkv[:, 512:1024], zc[:, 0:64], zr[:, 512:1024], start=True, stop=True)
    p.op("T", "matmul", pks, [zc, zr], pks[:, :], zc[:, 0:64], zr[:, 0:16], start=True, stop=True)
    for e in ENGINES:
        p.barrier_dma(e)

    emit_ln(p, X, H, sb, ps)
    emit_l0ffn(p, X, H, WT, DSH, GSM, sb, ps)
    for i in range(NDT):
        p.dma("Y", x1o.ap()[i * 128:(i + 1) * 128, :], X[i][:, :])
    del x1o

    emit_ln(p, X, H, sb, ps)
    p.dma("Y", CC[:, :], ccd.ap()[:, :])
    p.dma("Y", SSX[:, :], ssd.ap()[:, :])
    for i in range(NDT):
        p.dma("G", WT[i][:, :], wq.ap()[i * 128:(i + 1) * 128, :])
    rope_phi(p, QP, WT, H, CC, SSX, sb, ps, "q")
    for i in range(NDT):
        p.dma("G", WT[i][:, :], wk.ap()[i * 128:(i + 1) * 128, :])
    rope_phi(p, KP, WT, H, CC, SSX, sb, ps, "k")
    for i in range(NDT):
        p.dma("G", WT[i][:, :], wv.ap()[i * 128:(i + 1) * 128, :])
    if not fused:
        for i in range(NDT):
            p.dma("Y", io["qphi"].ap()[i * 128:(i + 1) * 128, :], QP[i][:, :])

    pbig = ps["P1024"]
    for j in range(NCH):
        c0 = j * CH
        for half in range(2):
            for di in range(NDT):
                p.op("T", "matmul", pbig, [H[di], WT[di]],
                     pbig[:, half * 512:(half + 1) * 512],
                     H[di][:, c0:c0 + CH], WT[di][:, half * 512:(half + 1) * 512],
                     start=(di == 0), stop=(di == NDT - 1))
            p.op("S", "activation", vtok, [pbig],
                 vtok[:, half * 512:(half + 1) * 512], pbig[:, half * 512:(half + 1) * 512], AF.Copy)
        for t in range(NDT):
            p.op("T", "transpose", ptr, [KP[t], IDN],
                 ptr[:, :], KP[t][:, c0:c0 + CH], IDN[:, :])
            p.op("V", "tensor_copy", ktok, [ptr], ktok[:, t * 128:(t + 1) * 128], ptr[:, :])
        for h in range(16):
            t, hh = h // 2, h % 2
            rows = slice(hh * 64, hh * 64 + 64)
            p.op("T", "matmul", pS, [KP[t], QP[t]],
                 pS[:, :], KP[t][rows, c0:c0 + CH], QP[t][rows, c0:c0 + CH],
                 start=True, stop=True)
            p.op("V", "tensor_mul", smh, [pS, TRI], smh[:, :], pS[:, :], TRI[:, :])
            p.op("T", "matmul", pbig, [smh, vtok],
                 pbig[:, h * 64:(h + 1) * 64], smh[:, :], vtok[:, h * 64:(h + 1) * 64],
                 start=True, stop=(j == 0))
            if j > 0:
                p.op("T", "matmul", pbig, [QP[t], kvbf],
                     pbig[:, h * 64:(h + 1) * 64], QP[t][rows, c0:c0 + CH],
                     kvbf[rows, t * 64:(t + 1) * 64], start=False, stop=True)
            p.op("T", "matmul", pden, [smh, sb["onesbf"]],
                 pden[:, h:h + 1], smh[:, :], sb["onesbf"][:, :],
                 start=True, stop=(j == 0))
            if j > 0:
                p.op("T", "matmul", pden, [QP[t], ksbf],
                     pden[:, h:h + 1], QP[t][rows, c0:c0 + CH], ksbf[rows, t:t + 1],
                     start=False, stop=True)
        p.op("V", "tensor_copy", nstg, [pbig], nstg[:, 0:1024], pbig[:, :])
        p.op("V", "tensor_copy", dstg, [pden], dstg[:, :], pden[:, 0:16])
        p.dma("Y", numo.ap()[c0:c0 + CH, :], nstg[:, 0:1024])
        p.dma("S", deno.ap()[:, j * 16:(j + 1) * 16], dstg[:, :])
        for h in range(16):
            t, hh = h // 2, h % 2
            rows = slice(hh * 64, hh * 64 + 64)
            kcols = slice(t * 128 + hh * 64, t * 128 + hh * 64 + 64)
            p.op("T", "matmul", pkv, [ktok, vtok],
                 pkv[0:64, h * 64:(h + 1) * 64], ktok[:, kcols], vtok[:, h * 64:(h + 1) * 64],
                 start=False, stop=(j == NCH - 1))
            p.op("T", "matmul", pks, [ktok, sb["onesbf"]],
                 pks[0:64, h:h + 1], ktok[:, kcols], sb["onesbf"][:, :],
                 start=False, stop=(j == NCH - 1))
        ev = pkv[0:64, :].rearrange("p (t two d) -> p two t d", two=2, d=64)
        p.op("V", "tensor_copy", kvbf, [pkv], kvbf[0:64, :].rearrange("p (t d) -> p t d", d=64), ev[:, 0])
        p.op("V", "tensor_copy", kvbf, [pkv], kvbf[64:128, :].rearrange("p (t d) -> p t d", d=64), ev[:, 1])
        ek = pks[0:64, :].rearrange("p (t two) -> p two t", two=2)
        p.op("V", "tensor_copy", ksbf, [pks], ksbf[0:64, :], ek[:, 0])
        p.op("V", "tensor_copy", ksbf, [pks], ksbf[64:128, :], ek[:, 1])
    p.op("V", "tensor_copy", kvst, [pkv], kvst[0:64, 0:1024], pkv[0:64, :])
    p.op("V", "tensor_copy", kvst, [pks], kvst[0:64, 1024:1040], pks[0:64, :])
    p.dma("Y", kvo, kvst[0:64, 0:1024])
    p.dma("Y", kso, kvst[0:64, 1024:1040])


def build_launch1(nc):
    io = {}
    for nm, shp, dt in [("x", [D, TC], F32), ("wq_t", [D, D], F32), ("wk_t", [D, D], F32),
                        ("wv_t", [D, D], F32), ("l0w_t", [D, D], F32), ("disha", [64, D], F32),
                        ("cc", [128, TC], F32), ("ssx", [128, TC], F32), ("gsm", [128, 64], F32),
                        ("triu", [128, 128], F32)]:
        io[nm] = nc.dram_tensor(nm, shp, dt, kind="ExternalInput")
    io["x1"] = nc.dram_tensor("x1", [D, TC], F32, kind="ExternalOutput")
    io["qphi"] = nc.dram_tensor("qphi", [D, TC], BF16, kind="ExternalOutput")
    io["num_part"] = nc.dram_tensor("num_part", [TC, D], F32, kind="ExternalOutput")
    io["den_part"] = nc.dram_tensor("den_part", [128, NCH * 16], F32, kind="ExternalOutput")
    io["kvloc"] = nc.dram_tensor("kvloc", [64, 1024], F32, kind="ExternalOutput").ap()[:, :]
    io["ksloc"] = nc.dram_tensor("ksloc", [64, 16], F32, kind="ExternalOutput").ap()[:, :]
    io["idn_h"] = nc.inline_tensor(np.eye(128, dtype=ml_dtypes.bfloat16), "idn_const")
    p = Prog(nc)
    sb = {}
    ps = {}
    S = std_alloc(nc, sb, ps)
    sb["QP"] = [S(f"QP{i}", [128, TC], BF16) for i in range(NDT)]
    sb["KP"] = [S(f"KP{i}", [128, TC], BF16) for i in range(NDT)]
    alloc_sec1_small(nc, S, sb, ps, p)
    emit_common_init(p, sb)
    emit_sec1(p, sb, ps, io)
    p.finish()
    p.build()
    return nc


def alloc_sec1_small(nc, S, sb, ps, p):
    def P(name, shape, dt):
        ps[name] = nc.alloc_psum_tensor(name, shape, dt)
        return ps[name]
    S("DSH", [64, D], BF16)
    S("GSM", [128, 64], BF16)
    S("TRI", [128, 128], F32)
    if "pooled" not in sb:
        S("pooled", [64, TC], BF16)
    S("vtok", [128, 1024], BF16)
    S("ktok", [128, 1024], BF16)
    S("smh", [128, 128], BF16)
    S("kvbf", [128, 512], BF16)
    S("ksbf", [128, 8], BF16)
    S("dstg", [128, 16], F32)
    S("IDN", [128, 128], BF16)
    if "zc" not in sb:
        S("zc", [1, 128], BF16)
    if "zr" not in sb:
        S("zr", [1, 1024], BF16)
    if "P1024" not in ps:
        P("P1024", [128, 1024], F32)
        P("P512a", [128, TN], F32)
        P("PS", [128, 128], F32)
        P("PKV", [64, 1024], F32)
        P("PKS", [64, 16], F32)
        P("PTR", [128, 128], BF16)


def emit_sec2(p, sb, ps, io, fused=False):
    nc = p.nc
    x1i = io["x1"]; qpi = io.get("qphi"); numi = io["num_part"]; deni = io["den_part"]
    cmki = io["cmask"]
    woi = io["wo_t"]; l0wi = io["l0w_t"]; dshi = io["disha"]; gsdi = io["gsm"]
    terni = io["tern_t"]; l1bi = io["l1bias_t"]; asci = io["act_scale"]
    wini = io["win_t"]; dwwi = io["dww"]; dwbi = io["dwb"]; dtbi = io["dtb64"]
    x4o = io["x4"]; valo = io["value_bf"]; gato = io["gate_sil"]; dtkdo = io["dtkd"]
    hfo = io["halo_first"]; hlo = io["halo_last"]
    kv_src = io["kv_src"]
    X = sb["X"]; H = sb["H"]; QP = sb["QP"]; WT = sb["WT"]
    YT = H
    DSH = sb["DSH"]; GSM = sb["GSM"]; IDN = sb["IDN"]
    kvacc = sb["kvacc"]; mb = sb["mb"]; kvbf = sb["kvbf"]; ksbf = sb["ksbf"]
    nstg = sb["nstg"]; dstg = sb["dstg"]; drec = sb["drec"]; ybf = sb["ybf"]
    vrp = sb["vrp"]; ca = sb["ca"]; praw = sb["praw"]; gstg = sb["gstg"]
    dwc = sb["dwc"]; dbc = sb["dbc"]; scl = sb["scl"]; bcol = sb["bcol"]; dtb = sb["dtb"]
    pbig = ps["P1024"]
    pden = ps["P512a"]
    ptr = ps["PTR"]
    psc = ps["PSC"]

    if not fused:
        for i in range(NDT):
            p.dma("Y", X[i][:, :], x1i.ap()[i * 128:(i + 1) * 128, :])
            p.dma("Y", QP[i][:, :], qpi.ap()[i * 128:(i + 1) * 128, :])
        p.dma("G", DSH[:, :], dshi.ap()[:, :])
        p.dma("G", GSM[:, :], gsdi.ap()[:, :])
        p.dma("Y", IDN[:, :], io["idn_h"].ap()[:, :])
    else:
        for i in range(NDT):
            p.dma("Y", X[i][:, :], x1i.ap()[i * 128:(i + 1) * 128, :])
    p.dma("Y", sb["scl"][0:1, 0:1], asci.ap()[:, :])
    p.dma("Y", bcol[:, :], l1bi.ap()[:, :])
    p.dma("Y", dtb[:, :], dtbi.ap()[:, :])

    mrow = sb["lnrows"]
    p.dma("Y", mrow[0:1, 0:8], cmki.ap()[:, :])
    p.op("T", "matmul", psc, [sb["onesr"], mrow],
         psc[0:64, 0:8], sb["onesr"][:, 0:64], mrow[0:1, 0:8], start=True, stop=True)
    p.op("V", "tensor_copy", mb, [psc], mb[:, :], psc[0:64, 0:8])
    p.op("V", "memset", kvacc, [], kvacc[:, :], 0.0)
    for cc in range(NCORE):
        kv_ap, ks_ap = kv_src(cc)
        p.dma("Y", nstg[0:64, :], kv_ap)
        p.dma("Y", dstg[0:64, :], ks_ap)
        p.op("V", "scalar_tensor_tensor", kvacc, [nstg, mb, kvacc],
             kvacc[0:64, 0:1024], nstg[0:64, :], mb[:, cc:cc + 1], kvacc[0:64, 0:1024],
             ALU.mult, ALU.add)
        p.op("V", "scalar_tensor_tensor", kvacc, [dstg, mb, kvacc],
             kvacc[0:64, 1024:1040], dstg[0:64, 0:16], mb[:, cc:cc + 1], kvacc[0:64, 1024:1040],
             ALU.mult, ALU.add)
    evv = kvacc[0:64, 0:1024].rearrange("p (t two d) -> p two t d", two=2, d=64)
    p.op("V", "tensor_copy", kvbf, [kvacc], kvbf[0:64, :].rearrange("p (t d) -> p t d", d=64), evv[:, 0])
    p.op("V", "tensor_copy", kvbf, [kvacc], kvbf[64:128, :].rearrange("p (t d) -> p t d", d=64), evv[:, 1])
    ekk = kvacc[0:64, 1024:1040].rearrange("p (t two) -> p two t", two=2)
    p.op("V", "tensor_copy", ksbf, [kvacc], ksbf[0:64, :], ekk[:, 0])
    p.op("V", "tensor_copy", ksbf, [kvacc], ksbf[64:128, :], ekk[:, 1])

    for j in range(NCH):
        c0 = j * CH
        p.dma("Y", nstg[:, :], numi.ap()[c0:c0 + CH, :])
        p.dma("S", dstg[:, :], deni.ap()[:, j * 16:(j + 1) * 16])
        for h in range(16):
            t, hh = h // 2, h % 2
            rows = slice(hh * 64, hh * 64 + 64)
            p.op("T", "matmul", pbig, [QP[t], kvbf],
                 pbig[:, h * 64:(h + 1) * 64], QP[t][rows, c0:c0 + CH],
                 kvbf[rows, t * 64:(t + 1) * 64], start=True, stop=True)
            p.op("T", "matmul", ps["PTRASH"], [QP[t], ksbf],
                 ps["PTRASH"][:, h:h + 1], QP[t][rows, c0:c0 + CH], ksbf[rows, t:t + 1],
                 start=True, stop=True)
        p.op("V", "tensor_add", nstg, [nstg, pbig], nstg[:, :], nstg[:, :], pbig[:, :])
        p.op("V", "tensor_add", dstg, [dstg, ps["PTRASH"]], dstg[:, :], dstg[:, :], ps["PTRASH"][:, 0:16])
        p.op("V", "tensor_single_scalar", dstg, [dstg], dstg[:, :], dstg[:, :], 1e-6, ALU.max)
        p.op("V", "reciprocal", drec, [dstg], drec[:, :], dstg[:, :])
        for h in range(16):
            p.op("V", "tensor_single_scalar", ybf, [nstg, drec],
                 ybf[:, h * 64:(h + 1) * 64], nstg[:, h * 64:(h + 1) * 64],
                 drec[:, h:h + 1], ALU.mult)
        for t in range(NDT):
            p.op("T", "transpose", ptr, [ybf, IDN],
                 ptr[:, :], ybf[:, t * 128:(t + 1) * 128], IDN[:, :])
            p.op("V", "tensor_copy", YT[t], [ptr], YT[t][:, c0:c0 + CH], ptr[:, :])

    for i in range(NDT):
        p.dma("G", WT[i][:, :], woi.ap()[i * 128:(i + 1) * 128, :])
    for do in range(NDT):
        for tn in range(NTN):
            c0 = tn * TN
            for dv in range(NDT):
                p.op("T", "matmul", pden, [WT[dv], YT[dv]],
                     pden[:, :], WT[dv][:, do * 128:(do + 1) * 128], YT[dv][:, c0:c0 + TN],
                     start=(dv == 0), stop=(dv == NDT - 1))
            p.op("V", "tensor_add", X[do], [pden, X[do]],
                 X[do][:, c0:c0 + TN], pden[:, :], X[do][:, c0:c0 + TN])

    emit_ln(p, X, H, sb, ps)
    for i in range(NDT):
        p.dma("G", WT[i][:, :], l0wi.ap()[i * 128:(i + 1) * 128, :])
    emit_l0ffn(p, X, H, WT, DSH, GSM, sb, ps)

    p.op("V", "tensor_single_scalar", scl, [scl], scl[0:1, 0:1], scl[0:1, 0:1], 1e-5, ALU.max)
    p.op("V", "reciprocal", scl, [scl], scl[0:1, 1:2], scl[0:1, 0:1])
    p.op("T", "matmul", psc, [sb["onesr"], scl],
         psc[:, 0:2], sb["onesr"][:, :], scl[0:1, 0:2], start=True, stop=True)
    p.op("V", "tensor_copy", scl, [psc], scl[:, :], psc[:, 0:2])
    emit_ln_quant(p, X, H, sb, ps)
    for i in range(NDT):
        p.dma("Y", sb["sq"][:, 0:1024].bitcast(I32), terni.ap()[i * 128:(i + 1) * 128, :])
        p.op("V", "tensor_copy", ca, [sb["sq"]], ca[:, 0:1024], sb["sq"][:, 0:1024].bitcast(I32))
        p.op("V", "tensor_single_scalar", WT[i], [ca], WT[i][:, :], ca[:, 0:1024], 1.0, ALU.subtract)
    for do in range(NDT):
        for tn in range(NTN):
            c0 = tn * TN
            for di in range(NDT):
                p.op("T", "matmul", pden, [WT[di], H[di]],
                     pden[:, :], WT[di][:, do * 128:(do + 1) * 128], H[di][:, c0:c0 + TN],
                     start=(di == 0), stop=(di == NDT - 1))
            p.op("V", "tensor_scalar", ca, [pden, scl, bcol],
                 ca[:, 0:TN], pden[:, :], scl[:, 0:1], bcol[:, do:do + 1],
                 ALU.mult, ALU.add)
            p.op("V", "scalar_tensor_tensor", X[do], [ca, X[do]],
                 X[do][:, c0:c0 + TN], ca[:, 0:TN], 0.5, X[do][:, c0:c0 + TN],
                 ALU.mult, ALU.add)
    for i in range(NDT):
        p.dma("Y", x4o.ap()[i * 128:(i + 1) * 128, :], X[i][:, :])

    emit_ln(p, X, H, sb, ps)

    for og in range(4):
        for i in range(NDT):
            p.dma("G", WT[i][:, :], wini.ap()[i * 128:(i + 1) * 128, og * 1024:(og + 1) * 1024])
        for sub in range(8):
            ot = og * 8 + sub
            for tn in range(NTN):
                c0 = tn * TN
                for di in range(NDT):
                    p.op("T", "matmul", pden, [WT[di], H[di]],
                         pden[:, :], WT[di][:, sub * 128:(sub + 1) * 128], H[di][:, c0:c0 + TN],
                         start=(di == 0), stop=(di == NDT - 1))
                if og < 2:
                    p.op("S", "activation", gstg, [pden],
                         gstg[:, c0:c0 + TN], pden[:, :], AF.Silu)
                else:
                    p.op("S", "activation", vrp, [pden],
                         vrp[:, 3 + c0:3 + c0 + TN], pden[:, :], AF.Copy)
            if og < 2:
                p.dma("Y", gato.ap()[ot * 128:(ot + 1) * 128, :], gstg[:, :])
            else:
                ct = ot - 16
                p.dma("S", dwc[:, :], dwwi.ap()[ct * 128:(ct + 1) * 128, :])
                p.dma("S", dbc[:, :], dwbi.ap()[ct * 128:(ct + 1) * 128, :])
                p.op("V", "memset", vrp, [], vrp[:, 0:3], 0.0)
                p.dma("Y", hfo.ap()[:, ct * 3:(ct + 1) * 3], vrp[:, 3:6])
                p.dma("Y", hlo.ap()[:, ct * 3:(ct + 1) * 3], vrp[:, TC:TC + 3])
                p.op("V", "tensor_single_scalar", ca, [vrp, dwc],
                     ca[:, :], vrp[:, 1:1 + TC], dwc[:, 1:2], ALU.mult)
                p.op("V", "scalar_tensor_tensor", ca, [vrp, dwc, ca],
                     ca[:, :], vrp[:, 0:TC], dwc[:, 0:1], ca[:, :], ALU.mult, ALU.add)
                p.op("V", "scalar_tensor_tensor", ca, [vrp, dwc, ca],
                     ca[:, :], vrp[:, 2:2 + TC], dwc[:, 2:3], ca[:, :], ALU.mult, ALU.add)
                p.op("V", "scalar_tensor_tensor", ca, [vrp, dwc, ca],
                     ca[:, :], vrp[:, 3:3 + TC], dwc[:, 3:4], ca[:, :], ALU.mult, ALU.add)
                p.op("V", "tensor_scalar", ca, [ca, dbc],
                     ca[:, :], ca[:, :], dbc[:, 0:1], None, ALU.add)
                p.op("S", "activation", gstg, [ca],
                     gstg[:, :], ca[:, :], AF.Silu)
                p.dma("Y", valo.ap()[ct * 128:(ct + 1) * 128, :], gstg[:, :])
    for i in range(NDT):
        p.dma("G", WT[i][:, 0:64], wini.ap()[i * 128:(i + 1) * 128, 4096:4160])
    for tn in range(NTN):
        c0 = tn * TN
        for di in range(NDT):
            p.op("T", "matmul", pden, [WT[di], H[di]],
                 pden[0:64, :], WT[di][:, 0:64], H[di][:, c0:c0 + TN],
                 start=(di == 0), stop=(di == NDT - 1))
        p.op("V", "tensor_copy", praw, [pden], praw[:, c0:c0 + TN], pden[0:64, :])

    aa = ca[0:64, 0:TC]
    rr = vrp[0:64, 0:TC]
    p.op("V", "tensor_scalar", praw, [praw, dtb], praw[:, :], praw[:, :], dtb[:, 0:1], None, ALU.add)
    p.op("S", "activation", ca, [praw], aa, praw[:, :], AF.Abs)
    p.op("S", "activation", ca, [ca], aa, aa, AF.Exp, scale=-1.0)
    p.op("S", "activation", ca, [ca], aa, aa, AF.Ln, bias=sb["onesf"][0:64, 0:1])
    p.op("V", "tensor_single_scalar", vrp, [praw], rr, praw[:, :], 0.0, ALU.max)
    p.op("V", "tensor_add", ca, [ca, vrp], aa, aa, rr)
    p.op("V", "tensor_scalar", ca, [ca], aa, aa, 0.1, 1e-4, ALU.min, ALU.max)
    p.dma("Y", dtkdo.ap()[:, 0:TC], ca[0:64, 0:TC])
    p.op("S", "activation", vrp, [praw], rr, praw[:, :], AF.Sigmoid)
    p.op("V", "tensor_single_scalar", vrp, [vrp], rr, rr, 0.5, ALU.mult)
    p.dma("Y", dtkdo.ap()[:, TC:2 * TC], vrp[0:64, 0:TC])


def alloc_sec2_small(nc, S, sb, ps):
    def P(name, shape, dt):
        if name not in ps:
            ps[name] = nc.alloc_psum_tensor(name, shape, dt)
        return ps[name]
    for nm, shp, dt in [("DSH", [64, D], BF16), ("GSM", [128, 64], BF16),
                        ("pooled", [64, TC], BF16), ("kvacc", [64, 1040], F32),
                        ("mb", [64, 8], F32), ("kvbf", [128, 512], BF16),
                        ("ksbf", [128, 8], BF16), ("dstg", [128, 16], F32),
                        ("drec", [128, 16], F32), ("dwc", [128, 4], F32),
                        ("dbc", [128, 1], F32), ("scl", [128, 2], F32),
                        ("bcol", [128, 8], F32), ("dtb", [64, 1], F32),
                        ("IDN", [128, 128], BF16)]:
        if nm not in sb:
            S(nm, shp, dt)
    P("P1024", [128, 1024], F32)
    P("P512a", [128, TN], F32)
    P("PTR", [128, 128], BF16)


def build_launch2(nc):
    io = {}
    for nm, shp, dt, kind in [
            ("x1", [D, TC], F32, "ExternalInput"),
            ("qphi", [D, TC], BF16, "ExternalInput"),
            ("num_part", [TC, D], F32, "ExternalInput"),
            ("den_part", [128, NCH * 16], F32, "ExternalInput"),
            ("kvall", [512, 1024], F32, "ExternalInput"),
            ("ksall", [512, 16], F32, "ExternalInput"),
            ("cmask", [1, 8], F32, "ExternalInput"),
            ("wo_t", [D, D], F32, "ExternalInput"),
            ("l0w_t", [D, D], F32, "ExternalInput"),
            ("disha", [64, D], F32, "ExternalInput"),
            ("gsm", [128, 64], F32, "ExternalInput"),
            ("tern_t", [D, D], I32, "ExternalInput"),
            ("l1bias_t", [128, 8], F32, "ExternalInput"),
            ("act_scale", [1, 1], F32, "ExternalInput"),
            ("win_t", [D, 4160], F32, "ExternalInput"),
            ("dww", [DI, 4], F32, "ExternalInput"),
            ("dwb", [DI, 1], F32, "ExternalInput"),
            ("dtb64", [64, 1], F32, "ExternalInput"),
            ("x4", [D, TC], F32, "ExternalOutput"),
            ("value_bf", [DI, TC], BF16, "ExternalOutput"),
            ("gate_sil", [DI, TC], BF16, "ExternalOutput"),
            ("dtkd", [64, 2 * TC], F32, "ExternalOutput"),
            ("halo_first", [128, 48], F32, "ExternalOutput"),
            ("halo_last", [128, 48], F32, "ExternalOutput")]:
        io[nm] = nc.dram_tensor(nm, shp, dt, kind=kind)
    io["idn_h"] = nc.inline_tensor(np.eye(128, dtype=ml_dtypes.bfloat16), "idn_const")
    kvai, ksai = io["kvall"], io["ksall"]
    io["kv_src"] = lambda cc: (kvai.ap()[cc * 64:cc * 64 + 64, :],
                               ksai.ap()[cc * 64:cc * 64 + 64, :])
    p = Prog(nc)
    sb = {}
    ps = {}
    S = std_alloc(nc, sb, ps)
    sb["QP"] = [S(f"QP{i}", [128, TC], BF16) for i in range(NDT)]
    alloc_sec2_small(nc, S, sb, ps)
    for nm, shp, dt in [("nstg", [128, 1024], F32), ("ybf", [128, 1024], BF16),
                        ("vrp", [128, 2051], F32), ("ca", [128, TC], F32),
                        ("praw", [64, TC], F32), ("gstg", [128, TC], BF16)]:
        S(nm, shp, dt)
    if "PSC" not in ps:
        ps["PSC"] = nc.alloc_psum_tensor("PSC", [128, 8], F32)
    ps["PTRASH"] = nc.alloc_psum_tensor("PTRASH", [128, 16], F32)
    emit_common_init(p, sb)
    emit_sec2(p, sb, ps, io)
    p.finish()
    p.build()
    return nc


def emit_sec3(p, sb, ps, io, fused=False):
    HSN = HS // CL
    NCHK = HS // C1
    CLC = C1 // CL
    nc = p.nc
    vali = io["value_bf"]; dtkdi = io["dtkd"]; hfi = io["halo_first"]
    dwwi = io["dww"]; dwbi = io["dwb"]; wbci = io["wbc_t"]; alogi = io["acol_log"]
    seldi = io["seld"]; gs32i = io["gs32"]; sel4i = io["sel4"]
    triui = io["triu"]
    yto = io["yT_bf"]; g2co = io["g2c_bf"]; vspo = io["vspan"]

    VAL = sb["VAL"]; WBC = sb["WBC"]
    DTF = sb["DTF"]; KDF = sb["KDF"]
    SELD = sb["SELD"]; GS32 = sb["GS32"]; SEL4 = sb["SEL4"]
    TRIU = sb["TRIU"]; ACOL = sb["ACOL"]; RESETM = sb["RESETM"]
    PL = sb["PL"]
    (dtb, kdb, DTA, SP, BP, CP, DP, BZ, BX, CZ, CX, C0, C1p,
     P00, P01, P10, P11, DET) = PL
    tmpA = kdb
    g0, g1 = BZ, BX
    Gc0, Gc1 = DTA, SP
    W0, W1 = CZ, CX
    H0, H1 = BP, CP
    RDT = DP
    QP00 = sb["QP00"]; QP01 = sb["QP01"]; QP10 = sb["QP10"]; QP11 = sb["QP11"]
    DQP = sb["DQP"]; PCH = sb["PCH"]; QPRE = sb["QPRE"]; QSPC = sb["QSPC"]
    GcP = sb["GcP"]; HcP = sb["HcP"]; G2S = sb["G2S"]
    utk = sb["utk"]; hct = sb["hct"]; AMS = sb["AMS"]
    YTS = sb["YTS"]; VST = sb["VST"]; VSB = sb["VSB"]; VSW = sb["VSW"]
    PAC = sb["PAC"]; PBC2 = sb["PBC2"]
    bcs = sb["bcs"]; sqs = sb["sqs"]; rns = sb["rns"]
    hp6 = sb["hp6"]; dwc = sb["dwc"]; dbc = sb["dbc"]; cstg = sb["cstg"]
    IDN = sb["IDN"]
    pA = ps["PA"]
    pY = ps["PY"]
    pW = ps["PW"]
    ptr = ps["PTR"]
    pbc = ps["PBC"]
    p4 = ps["P4"]
    prn = ps["PRN"]

    for i in range(16):
        p.dma("Y", VAL[i][:, :], vali.ap()[i * 128:(i + 1) * 128, :])
    p.dma("Y", SELD[:, :], seldi.ap()[:, :])
    p.dma("Y", GS32[:, :], gs32i.ap()[:, :])
    p.dma("Y", SEL4[:, :], sel4i.ap()[:, :])
    if not fused:
        p.dma("Y", TRIU[:, :], triui.ap()[:, :])
        p.dma("Y", IDN[:, :], io["idn_h"].ap()[:, :])
    p.dma("Y", RESETM[:, :], io["rm_h"].ap()[:, :])
    p.dma("Y", ACOL[:, :], alogi.ap()[:, :])
    p.op("S", "activation", ACOL, [ACOL], ACOL[:, :], ACOL[:, :], AF.Exp)

    if fused:
        hpall = sb["hpall"]
        p.dma("Y", hpall[:, :].rearrange("p (cc q) -> p cc q", cc=8),
              io["halo_out"].ap().rearrange("(cc p) q -> p cc q", cc=8))
        mrow = sb["lnrows"]
        p.dma("Y", mrow[0:1, 8:16], io["mprev"].ap()[:, :])
        p.op("T", "matmul", ps["PSC"], [sb["onesr"], mrow],
             ps["PSC"][:, 0:8], sb["onesr"][:, :], mrow[0:1, 8:16], start=True, stop=True)
        p.op("V", "tensor_copy", sb["mbp"], [ps["PSC"]], sb["mbp"][:, :], ps["PSC"][:, 0:8])
    for ct in range(16):
        rows = slice(ct * 128, (ct + 1) * 128)
        if fused:
            p.op("V", "memset", hp6, [], hp6[:, 0:3], 0.0)
            for cc in range(NCORE):
                p.op("V", "scalar_tensor_tensor", hp6, [sb["hpall"], sb["mbp"], hp6],
                     hp6[:, 0:3], sb["hpall"][:, cc * 48 + ct * 3:cc * 48 + (ct + 1) * 3],
                     sb["mbp"][:, cc:cc + 1], hp6[:, 0:3], ALU.mult, ALU.add)
        else:
            p.dma("Y", hp6[:, 0:3], io["halo_prev"].ap()[:, ct * 3:(ct + 1) * 3])
        p.dma("Y", hp6[:, 3:6], hfi.ap()[:, ct * 3:(ct + 1) * 3])
        p.dma("S", dwc[:, :], dwwi.ap()[rows, :])
        p.dma("S", dbc[:, :], dwbi.ap()[rows, :])
        p.op("V", "tensor_single_scalar", cstg, [hp6, dwc],
             cstg[:, :], hp6[:, 1:4], dwc[:, 1:2], ALU.mult)
        p.op("V", "scalar_tensor_tensor", cstg, [hp6, dwc, cstg],
             cstg[:, :], hp6[:, 0:3], dwc[:, 0:1], cstg[:, :], ALU.mult, ALU.add)
        p.op("V", "scalar_tensor_tensor", cstg, [hp6, dwc, cstg],
             cstg[:, :], hp6[:, 2:5], dwc[:, 2:3], cstg[:, :], ALU.mult, ALU.add)
        p.op("V", "scalar_tensor_tensor", cstg, [hp6, dwc, cstg],
             cstg[:, :], hp6[:, 3:6], dwc[:, 3:4], cstg[:, :], ALU.mult, ALU.add)
        p.op("V", "tensor_scalar", cstg, [cstg, dbc],
             cstg[:, :], cstg[:, :], dbc[:, 0:1], None, ALU.add)
        p.op("S", "activation", VAL[ct], [cstg], VAL[ct][:, 0:3], cstg[:, :], AF.Silu)

    for g in range(8):
        ag = ACOL[:, g:g + 1]
        for i in range(16):
            p.dma("G", WBC[i][:, :], wbci.ap()[i * 128:(i + 1) * 128, g * 512:(g + 1) * 512])
        for q in range(2):
            p.op("V", "memset", VST[q], [], VST[q][:, :], 0.0)
            p.op("V", "memset", VSB[q], [], VSB[q][:, :], 0.0)
        p.op("V", "memset", QSPC, [], QSPC[:, :], 0.0)
        p.op("V", "tensor_single_scalar", QSPC, [QSPC], QSPC[:, 0:1], QSPC[:, 0:1], 1.0, ALU.add)
        p.op("V", "tensor_single_scalar", QSPC, [QSPC], QSPC[:, 3:4], QSPC[:, 3:4], 1.0, ALU.add)

        for hs in range(2):
            hsr = slice(hs * HS, (hs + 1) * HS)
            hs0 = hs * HS
            for bt in range(4):
                for tn in range(2):
                    csl = slice(tn * 512, (tn + 1) * 512)
                    for ct in range(16):
                        p.op("T", "matmul", pbc, [WBC[ct], VAL[ct]],
                             pbc[:, :], WBC[ct][:, bt * 128:(bt + 1) * 128],
                             VAL[ct][:, hs0 + tn * 512:hs0 + (tn + 1) * 512],
                             start=(ct == 0), stop=(ct == 15))
                    p.op("S", "activation", sqs, [pbc], sqs[:, :], pbc[:, :], AF.Square)
                    p.op("T", "matmul", p4, [GS32, sqs],
                         p4[:, :], GS32[:, :], sqs[:, :], start=True, stop=True)
                    p.op("S", "activation", rns, [p4], rns[:, :], p4[:, :], AF.Sqrt)
                    p.op("V", "tensor_single_scalar", rns, [rns], rns[:, :], rns[:, :], 1e-8, ALU.add)
                    p.op("V", "reciprocal", rns, [rns], rns[:, :], rns[:, :])
                    p.op("T", "matmul", prn, [SEL4, rns],
                         prn[:, :], SEL4[:, :], rns[:, :], start=True, stop=True)
                    p.op("V", "tensor_copy", bcs, [prn], bcs[:, :], prn[:, :])
                    prow = slice(32 * bt, 32 * bt + 32)
                    for comp, PLt in [(0, BZ), (1, BX), (2, CZ), (3, CX)]:
                        crow = slice(32 * comp, 32 * comp + 32)
                        p.op("V", "tensor_mul", PLt, [pbc, bcs],
                             PLt[prow, csl], pbc[crow, :], bcs[crow, :])
            p.dma("Y", DTF[:, :], dtkdi.ap()[:, hs0:hs0 + HS])
            p.dma("Y", KDF[0:63, :], dtkdi.ap()[1:64, TC + hs0:TC + hs0 + HS])
            for tn in range(2):
                csl = slice(tn * 512, (tn + 1) * 512)
                p.op("T", "matmul", prn, [SELD, DTF],
                     prn[:, :], SELD[:, g * 128:(g + 1) * 128], DTF[:, csl],
                     start=True, stop=True)
                p.op("V", "tensor_copy", dtb, [prn], dtb[:, csl], prn[:, :])
                p.op("T", "matmul", prn, [SELD, KDF],
                     prn[:, :], SELD[0:63, g * 128:(g + 1) * 128], KDF[0:63, csl],
                     start=True, stop=True)
                p.op("V", "tensor_copy", kdb, [prn], kdb[:, csl], prn[:, :])
            p.op("V", "tensor_mul", C0, [dtb, kdb], C0[:, :], dtb[:, :], kdb[:, :])
            p.op("V", "tensor_mul", C1p, [dtb], C1p[:, :], dtb[:, :], dtb[:, :])
            p.op("S", "activation", DTA, [dtb], DTA[:, :], dtb[:, :], AF.Copy, scale=ag)
            p.op("S", "activation", C1p, [C1p], C1p[:, :], C1p[:, :], AF.Copy, scale=ag)
            p.op("V", "tensor_add", C0, [C0, C1p], C0[:, :], C0[:, :], C1p[:, :])
            p.op("V", "tensor_single_scalar", C0, [C0], C0[:, :], C0[:, :], 1.0, ALU.add)
            p.op("V", "reciprocal", SP, [C0], SP[:, :], C0[:, :])
            p.op("V", "scalar_tensor_tensor", BP, [SP, DTA],
                 BP[:, :], SP[:, :], -1.0, DTA[:, :], ALU.mult, ALU.mult)
            p.op("V", "tensor_mul", CP, [SP, dtb], CP[:, :], SP[:, :], dtb[:, :])
            p.op("V", "scalar_tensor_tensor", DP, [DTA, CP],
                 DP[:, :], DTA[:, :], -1.0, CP[:, :], ALU.mult, ALU.mult)
            p.op("V", "tensor_single_scalar", DP, [DP], DP[:, :], DP[:, :], 1.0, ALU.add)
            p.op("V", "tensor_mul", C0, [CP, BZ], C0[:, :], CP[:, :], BZ[:, :])
            p.op("V", "tensor_mul", tmpA, [CP, dtb], tmpA[:, :], CP[:, :], dtb[:, :])
            p.op("V", "tensor_mul", C1p, [tmpA, BZ], C1p[:, :], tmpA[:, :], BZ[:, :])
            p.op("V", "tensor_mul", tmpA, [dtb, BX], tmpA[:, :], dtb[:, :], BX[:, :])
            p.op("V", "tensor_add", C1p, [C1p, tmpA], C1p[:, :], C1p[:, :], tmpA[:, :])
            p.op("V", "tensor_mul", tmpA, [SP, RESETM], tmpA[:, :], SP[:, :], RESETM[:, :])
            p.op("V", "tensor_sub", g0, [SP, tmpA], g0[:, :], SP[:, :], tmpA[:, :])
            p.op("V", "tensor_tensor_scan", DET, [tmpA, g0], DET[:, :], tmpA[:, :], g0[:, :],
                 0.0, ALU.mult, ALU.add)
            def slv(pl, i):
                return pl[:, i::CL]
            for (dst, sa) in [(P00, SP), (P01, BP), (P10, CP), (P11, DP)]:
                p.op("V", "tensor_copy", dst, [sa], slv(dst, 0), slv(sa, 0))
            for i in range(1, CL):
                p.op("V", "tensor_mul", tmpA, [SP, P00], slv(tmpA, i), slv(SP, i), slv(P00, i - 1))
                p.op("V", "tensor_mul", P00, [BP, P10], slv(P00, i), slv(BP, i), slv(P10, i - 1))
                p.op("V", "tensor_add", P00, [P00, tmpA], slv(P00, i), slv(P00, i), slv(tmpA, i))
                p.op("V", "tensor_mul", tmpA, [SP, P01], slv(tmpA, i), slv(SP, i), slv(P01, i - 1))
                p.op("V", "tensor_mul", P01, [BP, P11], slv(P01, i), slv(BP, i), slv(P11, i - 1))
                p.op("V", "tensor_add", P01, [P01, tmpA], slv(P01, i), slv(P01, i), slv(tmpA, i))
                p.op("V", "tensor_mul", tmpA, [CP, P00], slv(tmpA, i), slv(CP, i), slv(P00, i - 1))
                p.op("V", "tensor_mul", P10, [DP, P10], slv(P10, i), slv(DP, i), slv(P10, i - 1))
                p.op("V", "tensor_add", P10, [P10, tmpA], slv(P10, i), slv(P10, i), slv(tmpA, i))
                p.op("V", "tensor_mul", tmpA, [CP, P01], slv(tmpA, i), slv(CP, i), slv(P01, i - 1))
                p.op("V", "tensor_mul", P11, [DP, P11], slv(P11, i), slv(DP, i), slv(P11, i - 1))
                p.op("V", "tensor_add", P11, [P11, tmpA], slv(P11, i), slv(P11, i), slv(tmpA, i))
            p.op("G", "memset", QP00, [], QP00[:, :], 0.0)
            p.op("G", "memset", QP01, [], QP01[:, :], 0.0)
            p.op("G", "memset", QP10, [], QP10[:, :], 0.0)
            p.op("G", "memset", QP11, [], QP11[:, :], 0.0)
            p.op("V", "tensor_single_scalar", QP00, [QP00], QP00[:, 0::CLC], QP00[:, 0::CLC], 1.0, ALU.add)
            p.op("V", "tensor_single_scalar", QP11, [QP11], QP11[:, 0::CLC], QP11[:, 0::CLC], 1.0, ALU.add)

            def pend(pl):
                return pl[:, CL - 1::CL]
            for cl in range(1, CLC):
                def qv(qp, c):
                    return qp[:, c::CLC]
                pe = lambda pl: pl[:, (cl - 1) * CL + CL - 1::C1]
                p.op("V", "tensor_mul", VSW, [P00, QP00], VSW[:, 0:NCHK], pe(P00), qv(QP00, cl - 1))
                p.op("V", "tensor_mul", QP00, [P01, QP10], qv(QP00, cl), pe(P01), qv(QP10, cl - 1))
                p.op("V", "tensor_add", QP00, [QP00, VSW], qv(QP00, cl), qv(QP00, cl), VSW[:, 0:NCHK])
                p.op("V", "tensor_mul", VSW, [P00, QP01], VSW[:, 0:NCHK], pe(P00), qv(QP01, cl - 1))
                p.op("V", "tensor_mul", QP01, [P01, QP11], qv(QP01, cl), pe(P01), qv(QP11, cl - 1))
                p.op("V", "tensor_add", QP01, [QP01, VSW], qv(QP01, cl), qv(QP01, cl), VSW[:, 0:NCHK])
                p.op("V", "tensor_mul", VSW, [P10, QP00], VSW[:, 0:NCHK], pe(P10), qv(QP00, cl - 1))
                p.op("V", "tensor_mul", QP10, [P11, QP10], qv(QP10, cl), pe(P11), qv(QP10, cl - 1))
                p.op("V", "tensor_add", QP10, [QP10, VSW], qv(QP10, cl), qv(QP10, cl), VSW[:, 0:NCHK])
                p.op("V", "tensor_mul", VSW, [P10, QP01], VSW[:, 0:NCHK], pe(P10), qv(QP01, cl - 1))
                p.op("V", "tensor_mul", QP11, [P11, QP11], qv(QP11, cl), pe(P11), qv(QP11, cl - 1))
                p.op("V", "tensor_add", QP11, [QP11, VSW], qv(QP11, cl), qv(QP11, cl), VSW[:, 0:NCHK])
            p.op("V", "tensor_mul", DQP, [QP00, QP11], DQP[:, :], QP00[:, :], QP11[:, :])
            p.op("V", "tensor_mul", VSW, [QP01, QP10], VSW[:, 0:HSN], QP01[:, :], QP10[:, :])
            p.op("V", "tensor_sub", DQP, [DQP, VSW], DQP[:, :], DQP[:, :], VSW[:, 0:HSN])
            lcl = CLC - 1
            pe2 = lambda pl: pl[:, lcl * CL + CL - 1::C1]
            qv2 = lambda qp: qp[:, lcl::CLC]
            for (cc, aa_, bb_, cc_, dd_) in [(0, P00, P01, QP00, QP10), (1, P00, P01, QP01, QP11),
                                             (2, P10, P11, QP00, QP10), (3, P10, P11, QP01, QP11)]:
                p.op("V", "tensor_mul", VSW, [aa_, cc_], VSW[:, 0:NCHK], pe2(aa_), qv2(cc_))
                p.op("V", "tensor_mul", PCH, [bb_, dd_], PCH[:, cc::4], pe2(bb_), qv2(dd_))
                p.op("V", "tensor_add", PCH, [PCH, VSW], PCH[:, cc::4], PCH[:, cc::4], VSW[:, 0:NCHK])
            for j in range(NCHK):
                if j == 0:
                    p.op("V", "tensor_copy", QPRE, [QSPC], QPRE[:, 0:4], QSPC[:, :])
                else:
                    a_, b_, c_, d_ = (PCH[:, (j - 1) * 4 + 0:(j - 1) * 4 + 1],
                                      PCH[:, (j - 1) * 4 + 1:(j - 1) * 4 + 2],
                                      PCH[:, (j - 1) * 4 + 2:(j - 1) * 4 + 3],
                                      PCH[:, (j - 1) * 4 + 3:(j - 1) * 4 + 4])
                    q_ = lambda k: QPRE[:, (j - 1) * 4 + k:(j - 1) * 4 + k + 1]
                    o_ = lambda k: QPRE[:, j * 4 + k:j * 4 + k + 1]
                    p.op("V", "tensor_mul", VSW, [PCH, QPRE], VSW[:, 0:1], a_, q_(0))
                    p.op("V", "tensor_mul", QPRE, [PCH, QPRE], o_(0), b_, q_(2))
                    p.op("V", "tensor_add", QPRE, [QPRE, VSW], o_(0), o_(0), VSW[:, 0:1])
                    p.op("V", "tensor_mul", VSW, [PCH, QPRE], VSW[:, 0:1], a_, q_(1))
                    p.op("V", "tensor_mul", QPRE, [PCH, QPRE], o_(1), b_, q_(3))
                    p.op("V", "tensor_add", QPRE, [QPRE, VSW], o_(1), o_(1), VSW[:, 0:1])
                    p.op("V", "tensor_mul", VSW, [PCH, QPRE], VSW[:, 0:1], c_, q_(0))
                    p.op("V", "tensor_mul", QPRE, [PCH, QPRE], o_(2), d_, q_(2))
                    p.op("V", "tensor_add", QPRE, [QPRE, VSW], o_(2), o_(2), VSW[:, 0:1])
                    p.op("V", "tensor_mul", VSW, [PCH, QPRE], VSW[:, 0:1], c_, q_(1))
                    p.op("V", "tensor_mul", QPRE, [PCH, QPRE], o_(3), d_, q_(3))
                    p.op("V", "tensor_add", QPRE, [QPRE, VSW], o_(3), o_(3), VSW[:, 0:1])
            jl = NCHK - 1
            a_, b_, c_, d_ = (PCH[:, jl * 4 + 0:jl * 4 + 1], PCH[:, jl * 4 + 1:jl * 4 + 2],
                              PCH[:, jl * 4 + 2:jl * 4 + 3], PCH[:, jl * 4 + 3:jl * 4 + 4])
            q_ = lambda k: QPRE[:, jl * 4 + k:jl * 4 + k + 1]
            p.op("V", "tensor_mul", VSW, [PCH, QPRE], VSW[:, 0:1], a_, q_(0))
            p.op("V", "tensor_mul", QSPC, [PCH, QPRE], QSPC[:, 0:1], b_, q_(2))
            p.op("V", "tensor_add", QSPC, [QSPC, VSW], QSPC[:, 0:1], QSPC[:, 0:1], VSW[:, 0:1])
            p.op("V", "tensor_mul", VSW, [PCH, QPRE], VSW[:, 0:1], a_, q_(1))
            p.op("V", "tensor_mul", QSPC, [PCH, QPRE], QSPC[:, 1:2], b_, q_(3))
            p.op("V", "tensor_add", QSPC, [QSPC, VSW], QSPC[:, 1:2], QSPC[:, 1:2], VSW[:, 0:1])
            p.op("V", "tensor_mul", VSW, [PCH, QPRE], VSW[:, 0:1], c_, q_(0))
            p.op("V", "tensor_mul", QSPC, [PCH, QPRE], QSPC[:, 2:3], d_, q_(2))
            p.op("V", "tensor_add", QSPC, [QSPC, VSW], QSPC[:, 2:3], QSPC[:, 2:3], VSW[:, 0:1])
            p.op("V", "tensor_mul", VSW, [PCH, QPRE], VSW[:, 0:1], c_, q_(1))
            p.op("V", "tensor_mul", QSPC, [PCH, QPRE], QSPC[:, 3:4], d_, q_(3))
            p.op("V", "tensor_add", QSPC, [QSPC, VSW], QSPC[:, 3:4], QSPC[:, 3:4], VSW[:, 0:1])

            p.op("V", "tensor_mul", tmpA, [CZ, P00], tmpA[:, :], CZ[:, :], P00[:, :])
            p.op("V", "tensor_mul", g0, [CX, P10], g0[:, :], CX[:, :], P10[:, :])
            p.op("V", "tensor_add", g0, [g0, tmpA], g0[:, :], g0[:, :], tmpA[:, :])
            p.op("V", "tensor_mul", tmpA, [CZ, P01], tmpA[:, :], CZ[:, :], P01[:, :])
            p.op("V", "tensor_mul", g1, [CX, P11], g1[:, :], CX[:, :], P11[:, :])
            p.op("V", "tensor_add", g1, [g1, tmpA], g1[:, :], g1[:, :], tmpA[:, :])
            qb = lambda qp: qp[:, :].rearrange("p (c l) -> p c l", l=CLC).unsqueeze(3).broadcast_to((128, NCHK, CLC, CL))
            pv = lambda pl: pl[:, :].rearrange("p (c l i) -> p c l i", l=CLC, i=CL)
            p.op("V", "tensor_mul", tmpA, [g0, QP00], pv(tmpA), pv(g0), qb(QP00))
            p.op("V", "tensor_mul", Gc0, [g1, QP10], pv(Gc0), pv(g1), qb(QP10))
            p.op("V", "tensor_add", Gc0, [Gc0, tmpA], Gc0[:, :], Gc0[:, :], tmpA[:, :])
            p.op("V", "tensor_mul", tmpA, [g0, QP01], pv(tmpA), pv(g0), qb(QP01))
            p.op("V", "tensor_mul", Gc1, [g1, QP11], pv(Gc1), pv(g1), qb(QP11))
            p.op("V", "tensor_add", Gc1, [Gc1, tmpA], Gc1[:, :], Gc1[:, :], tmpA[:, :])
            for q in range(2):
                for (blk, srcpl, r0) in [(0, Gc0, 64 * q), (1, Gc1, 64 * q),
                                         (2, Gc0, 64 * q + 32), (3, Gc1, 64 * q + 32)]:
                    p.op("S", "activation", GcP[q], [srcpl],
                         GcP[q][32 * blk:32 * blk + 32, :], srcpl[r0:r0 + 32, :], AF.Copy)
            q2 = lambda k: QPRE[:, k::4].rearrange("p c -> p c").unsqueeze(2).unsqueeze(3).broadcast_to((128, NCHK, CLC, CL))
            p.op("V", "tensor_mul", tmpA, [Gc0, QPRE], pv(tmpA), pv(Gc0), q2(0))
            p.op("V", "tensor_mul", g0, [Gc1, QPRE], pv(g0), pv(Gc1), q2(2))
            p.op("V", "tensor_add", g0, [g0, tmpA], g0[:, :], g0[:, :], tmpA[:, :])
            p.op("V", "tensor_mul", tmpA, [Gc0, QPRE], pv(tmpA), pv(Gc0), q2(1))
            p.op("V", "tensor_mul", g1, [Gc1, QPRE], pv(g1), pv(Gc1), q2(3))
            p.op("V", "tensor_add", g1, [g1, tmpA], g1[:, :], g1[:, :], tmpA[:, :])
            for q in range(2):
                for (blk, srcpl, r0) in [(0, g0, 64 * q), (1, g1, 64 * q),
                                         (2, g0, 64 * q + 32), (3, g1, 64 * q + 32)]:
                    p.op("S", "activation", G2S[q], [srcpl],
                         G2S[q][32 * blk:32 * blk + 32, :], srcpl[r0:r0 + 32, :], AF.Copy)
                p.dma("Y", g2co.ap()[(g * 2 + q) * 128:(g * 2 + q + 1) * 128, hsr], G2S[q][:, :])
            p.op("V", "tensor_mul", W0, [P11, C0], W0[:, :], P11[:, :], C0[:, :])
            p.op("V", "tensor_mul", tmpA, [P01, C1p], tmpA[:, :], P01[:, :], C1p[:, :])
            p.op("V", "tensor_sub", W0, [W0, tmpA], W0[:, :], W0[:, :], tmpA[:, :])
            p.op("V", "tensor_mul", W1, [P00, C1p], W1[:, :], P00[:, :], C1p[:, :])
            p.op("V", "tensor_mul", tmpA, [P10, C0], tmpA[:, :], P10[:, :], C0[:, :])
            p.op("V", "tensor_sub", W1, [W1, tmpA], W1[:, :], W1[:, :], tmpA[:, :])
            p.op("V", "tensor_mul", H0, [W0, QP11], pv(H0), pv(W0), qb(QP11))
            p.op("V", "tensor_mul", tmpA, [W1, QP01], pv(tmpA), pv(W1), qb(QP01))
            p.op("V", "tensor_sub", H0, [H0, tmpA], H0[:, :], H0[:, :], tmpA[:, :])
            p.op("V", "tensor_mul", H1, [W1, QP00], pv(H1), pv(W1), qb(QP00))
            p.op("V", "tensor_mul", tmpA, [W0, QP10], pv(tmpA), pv(W0), qb(QP10))
            p.op("V", "tensor_sub", H1, [H1, tmpA], H1[:, :], H1[:, :], tmpA[:, :])
            p.op("V", "tensor_mul", RDT, [DET, DQP], pv(RDT), pv(DET), qb(DQP))
            p.op("V", "reciprocal", RDT, [RDT], RDT[:, :], RDT[:, :])
            p.op("V", "tensor_mul", H0, [H0, RDT], H0[:, :], H0[:, :], RDT[:, :])
            p.op("V", "tensor_mul", H1, [H1, RDT], H1[:, :], H1[:, :], RDT[:, :])
            for q in range(2):
                for (blk, srcpl, r0) in [(0, H0, 64 * q), (1, H1, 64 * q),
                                         (2, H0, 64 * q + 32), (3, H1, 64 * q + 32)]:
                    p.op("S", "activation", HcP[q], [srcpl],
                         HcP[q][32 * blk:32 * blk + 32, :], srcpl[r0:r0 + 32, :], AF.Copy)
            for q in range(2):
                p.op("S", "activation", PAC, [PCH], PAC[0:32, q * NCHK:(q + 1) * NCHK], PCH[32 * (2 * q):32 * (2 * q) + 32, 0::4], AF.Copy)
                p.op("S", "activation", PAC, [PCH], PAC[32:64, q * NCHK:(q + 1) * NCHK], PCH[32 * (2 * q):32 * (2 * q) + 32, 3::4], AF.Copy)
                p.op("S", "activation", PAC, [PCH], PAC[64:96, q * NCHK:(q + 1) * NCHK], PCH[32 * (2 * q + 1):32 * (2 * q + 1) + 32, 0::4], AF.Copy)
                p.op("S", "activation", PAC, [PCH], PAC[96:128, q * NCHK:(q + 1) * NCHK], PCH[32 * (2 * q + 1):32 * (2 * q + 1) + 32, 3::4], AF.Copy)
                p.op("S", "activation", PBC2, [PCH], PBC2[0:32, q * NCHK:(q + 1) * NCHK], PCH[32 * (2 * q):32 * (2 * q) + 32, 1::4], AF.Copy)
                p.op("S", "activation", PBC2, [PCH], PBC2[32:64, q * NCHK:(q + 1) * NCHK], PCH[32 * (2 * q):32 * (2 * q) + 32, 2::4], AF.Copy)
                p.op("S", "activation", PBC2, [PCH], PBC2[64:96, q * NCHK:(q + 1) * NCHK], PCH[32 * (2 * q + 1):32 * (2 * q + 1) + 32, 1::4], AF.Copy)
                p.op("S", "activation", PBC2, [PCH], PBC2[96:128, q * NCHK:(q + 1) * NCHK], PCH[32 * (2 * q + 1):32 * (2 * q + 1) + 32, 2::4], AF.Copy)

            for j in range(NCHK):
                jc = slice(j * C1, (j + 1) * C1)
                for q in range(2):
                    vt = 2 * g + q
                    for sub in range(2):
                        p.op("T", "transpose", ptr, [VAL[vt], IDN],
                             ptr[:, :], VAL[vt][:, hs0 + j * C1 + sub * 128:hs0 + j * C1 + sub * 128 + 128], IDN[:, :])
                        p.op("V", "tensor_copy", utk, [ptr], utk[:, sub * 128:(sub + 1) * 128], ptr[:, :])
                        p.op("T", "transpose", ptr, [HcP[q], IDN],
                             ptr[:, :], HcP[q][:, j * C1 + sub * 128:j * C1 + sub * 128 + 128], IDN[:, :])
                        p.op("V", "tensor_copy", hct, [ptr], hct[:, sub * 128:(sub + 1) * 128], ptr[:, :])
                    for par in range(2):
                        rows64 = slice(64 * par, 64 * par + 64)
                        for (ssb, tsb, k) in [(0, 0, 0), (0, 1, 1), (1, 1, 2)]:
                            p.op("T", "matmul", pA, [HcP[q], GcP[q]],
                                 pA[:, :], HcP[q][rows64, j * C1 + ssb * 128:j * C1 + ssb * 128 + 128],
                                 GcP[q][rows64, j * C1 + tsb * 128:j * C1 + tsb * 128 + 128],
                                 start=True, stop=True)
                            if ssb == tsb:
                                p.op("V", "tensor_mul", AMS, [pA, TRIU],
                                     AMS[:, k * 128:(k + 1) * 128], pA[:, :], TRIU[:, :])
                            else:
                                p.op("V", "tensor_copy", AMS, [pA],
                                     AMS[:, k * 128:(k + 1) * 128], pA[:, :])
                        ycol = lambda tsb: pY[0:64, par * 256 + tsb * 128:par * 256 + tsb * 128 + 128]
                        p.op("T", "matmul", pY, [utk, AMS],
                             ycol(0), utk[:, 0 * 128 + par * 64:0 * 128 + par * 64 + 64],
                             AMS[:, 0:128], start=True, stop=False)
                        p.op("T", "matmul", pY, [VSB[q], GcP[q]],
                             ycol(0), VSB[q][rows64, :],
                             GcP[q][rows64, j * C1:j * C1 + 128], start=False, stop=True)
                        p.op("T", "matmul", pY, [utk, AMS],
                             ycol(1), utk[:, 0 * 128 + par * 64:0 * 128 + par * 64 + 64],
                             AMS[:, 128:256], start=True, stop=False)
                        p.op("T", "matmul", pY, [utk, AMS],
                             ycol(1), utk[:, 1 * 128 + par * 64:1 * 128 + par * 64 + 64],
                             AMS[:, 256:384], start=False, stop=False)
                        p.op("T", "matmul", pY, [VSB[q], GcP[q]],
                             ycol(1), VSB[q][rows64, :],
                             GcP[q][rows64, j * C1 + 128:j * C1 + 256], start=False, stop=True)
                        p.op("T", "matmul", pW, [hct, utk],
                             pW[0:64, par * 64:par * 64 + 64],
                             hct[:, 0 * 128 + par * 64:0 * 128 + par * 64 + 64],
                             utk[:, 0 * 128 + par * 64:0 * 128 + par * 64 + 64],
                             start=True, stop=False)
                        p.op("T", "matmul", pW, [hct, utk],
                             pW[0:64, par * 64:par * 64 + 64],
                             hct[0:128, 1 * 128 + par * 64:1 * 128 + par * 64 + 64],
                             utk[:, 1 * 128 + par * 64:1 * 128 + par * 64 + 64],
                             start=False, stop=True)
                    p.op("V", "tensor_copy", YTS[q], [pY],
                         YTS[q][0:64, hs0 + j * C1:hs0 + (j + 1) * C1], pY[0:64, 0:256])
                    p.op("V", "tensor_copy", YTS[q], [pY],
                         YTS[q][64:128, hs0 + j * C1:hs0 + (j + 1) * C1], pY[0:64, 256:512])
                    p.op("V", "tensor_add", VST[q], [VST[q], pW],
                         VST[q][0:64, :], VST[q][0:64, :], pW[0:64, 0:64])
                    p.op("V", "tensor_add", VST[q], [VST[q], pW],
                         VST[q][64:128, :], VST[q][64:128, :], pW[0:64, 64:128])
                    p.op("S", "activation", VSW, [VST[q]], VSW[0:32, :], VST[q][32:64, :], AF.Copy)
                    p.op("S", "activation", VSW, [VST[q]], VSW[32:64, :], VST[q][0:32, :], AF.Copy)
                    p.op("S", "activation", VSW, [VST[q]], VSW[64:96, :], VST[q][96:128, :], AF.Copy)
                    p.op("S", "activation", VSW, [VST[q]], VSW[96:128, :], VST[q][64:96, :], AF.Copy)
                    p.op("V", "tensor_scalar", VST[q], [VST[q], PAC],
                         VST[q][:, :], VST[q][:, :], PAC[:, q * NCHK + j:q * NCHK + j + 1], None, ALU.mult)
                    p.op("V", "scalar_tensor_tensor", VST[q], [VSW, PBC2, VST[q]],
                         VST[q][:, :], VSW[:, :], PBC2[:, q * NCHK + j:q * NCHK + j + 1], VST[q][:, :],
                         ALU.mult, ALU.add)
                    p.op("V", "tensor_copy", VSB[q], [VST[q]], VSB[q][:, :], VST[q][:, :])
        for q in range(2):
            p.dma("Y", vspo.ap()[(g * 2 + q) * 128:(g * 2 + q + 1) * 128, :], VST[q][:, :])
            p.dma("Y", yto.ap()[(g * 2 + q) * 128:(g * 2 + q + 1) * 128, :], YTS[q][:, :])
        if fused:
            pabst = sb["pabst"]
            for q in range(2):
                he, ho = 32 * (2 * q), 32 * (2 * q + 1)
                c = (g * 2 + q) * 2
                p.op("S", "activation", pabst, [QSPC], pabst[0:32, c:c + 1], QSPC[he:he + 32, 0:1], AF.Copy)
                p.op("S", "activation", pabst, [QSPC], pabst[32:64, c:c + 1], QSPC[he:he + 32, 3:4], AF.Copy)
                p.op("S", "activation", pabst, [QSPC], pabst[64:96, c:c + 1], QSPC[ho:ho + 32, 0:1], AF.Copy)
                p.op("S", "activation", pabst, [QSPC], pabst[96:128, c:c + 1], QSPC[ho:ho + 32, 3:4], AF.Copy)
                p.op("S", "activation", pabst, [QSPC], pabst[0:32, c + 1:c + 2], QSPC[he:he + 32, 1:2], AF.Copy)
                p.op("S", "activation", pabst, [QSPC], pabst[32:64, c + 1:c + 2], QSPC[he:he + 32, 2:3], AF.Copy)
                p.op("S", "activation", pabst, [QSPC], pabst[64:96, c + 1:c + 2], QSPC[ho:ho + 32, 1:2], AF.Copy)
                p.op("S", "activation", pabst, [QSPC], pabst[96:128, c + 1:c + 2], QSPC[ho:ho + 32, 2:3], AF.Copy)
        else:
            p.dma("Y", io["pspan"].ap()[:, g * 4:(g + 1) * 4], QSPC[:, :])
    if fused:
        p.dma("Y", io["pab_in"].ap()[:, :], sb["pabst"][:, :])


def alloc_sec3_small(nc, S, sb, ps):
    HSN = HS // CL
    NCHK = HS // C1
    def P(name, shape, dt):
        if name not in ps:
            ps[name] = nc.alloc_psum_tensor(name, shape, dt)
        return ps[name]
    for nm, shp, dt in [("SELD", [64, 1024], F32),
                        ("GS32", [128, 4], F32), ("SEL4", [4, 128], F32),
                        ("ACOL", [128, 8], F32),
                        ("QP00", [128, HSN], F32), ("QP01", [128, HSN], F32),
                        ("QP10", [128, HSN], F32), ("QP11", [128, HSN], F32),
                        ("DQP", [128, HSN], F32), ("PCH", [128, 4 * NCHK], F32),
                        ("QPRE", [128, 4 * NCHK], F32), ("QSPC", [128, 4], F32),
                        ("VSW", [128, 64], F32), ("PAC", [128, 2 * NCHK], F32),
                        ("PBC2", [128, 2 * NCHK], F32), ("rns", [4, 512], F32),
                        ("hp6", [128, 6], F32), ("dwc", [128, 4], F32),
                        ("dbc", [128, 1], F32), ("cstg", [128, 3], F32)]:
        if nm not in sb:
            S(nm, shp, dt)
    if "VST" not in sb:
        sb["VST"] = [S(f"VST{q}", [128, 64], F32) for q in range(2)]


def build_launch3(nc):
    io = {}
    for nm, shp, dt, kind in [
            ("value_bf", [DI, TC], BF16, "ExternalInput"),
            ("dtkd", [64, 2 * TC], F32, "ExternalInput"),
            ("halo_prev", [128, 48], F32, "ExternalInput"),
            ("halo_first", [128, 48], F32, "ExternalInput"),
            ("dww", [DI, 4], F32, "ExternalInput"),
            ("dwb", [DI, 1], F32, "ExternalInput"),
            ("wbc_t", [DI, 4096], F32, "ExternalInput"),
            ("acol_log", [128, 8], F32, "ExternalInput"),
            ("seld", [64, 1024], F32, "ExternalInput"),
            ("gs32", [128, 4], F32, "ExternalInput"),
            ("sel4", [4, 128], F32, "ExternalInput"),
            ("triu", [128, 128], F32, "ExternalInput"),
            ("yT_bf", [DI, TC], BF16, "ExternalOutput"),
            ("g2c_bf", [DI, TC], BF16, "ExternalOutput"),
            ("vspan", [DI, 64], F32, "ExternalOutput"),
            ("pspan", [128, 32], F32, "ExternalOutput")]:
        io[nm] = nc.dram_tensor(nm, shp, dt, kind=kind)
    io["idn_h"] = nc.inline_tensor(np.eye(128, dtype=ml_dtypes.bfloat16), "idn_const")
    rm = np.ones((128, HS), np.float32)
    rm[:, 0::CL] = 0.0
    io["rm_h"] = nc.inline_tensor(rm, "resetm_const")
    p = Prog(nc)
    sb = {}
    ps = {}

    def S(name, shape, dt):
        sb[name] = nc.alloc_sbuf_tensor(name, shape, dt)
        return sb[name]

    def P(name, shape, dt):
        ps[name] = nc.alloc_psum_tensor(name, shape, dt)
        return ps[name]

    sb["VAL"] = [S(f"VAL{i}", [128, TC], BF16) for i in range(16)]
    sb["WBC"] = [S(f"WBC{i}", [128, 512], BF16) for i in range(16)]
    S("DTF", [64, HS], F32)
    S("KDF", [64, HS], F32)
    S("TRIU", [128, 128], F32)
    S("RESETM", [128, HS], F32)
    sb["PL"] = [S(f"PL{i}", [128, HS], F32) for i in range(18)]
    sb["GcP"] = [S(f"GcP{q}", [128, HS], BF16) for q in range(2)]
    sb["HcP"] = [S(f"HcP{q}", [128, HS], BF16) for q in range(2)]
    sb["G2S"] = [S(f"G2S{q}", [128, HS], BF16) for q in range(2)]
    S("utk", [128, C1], BF16)
    S("hct", [128, C1], BF16)
    S("AMS", [128, 3 * 128], BF16)
    sb["YTS"] = [S(f"YTS{q}", [128, TC], BF16) for q in range(2)]
    sb["VSB"] = [S(f"VSB{q}", [128, 64], BF16) for q in range(2)]
    S("bcs", [128, 512], F32)
    S("sqs", [128, 512], F32)
    S("IDN", [128, 128], BF16)
    alloc_sec3_small(nc, S, sb, ps)
    P("PA", [128, 128], F32)
    P("PY", [64, 512], F32)
    P("PW", [64, 128], F32)
    P("PTR", [128, 128], BF16)
    P("PBC", [128, 512], F32)
    P("P4", [4, 512], F32)
    P("PRN", [128, 512], F32)
    emit_sec3(p, sb, ps, io)
    p.finish()
    p.build()
    return nc


def emit_sec4(p, sb, ps, io, fused=False):
    nc = p.nc
    x4i = io["x4"]; yti = io["yT_bf"]; g2i = io["g2c_bf"]; vali = io["value_bf"]
    gati = io["gate_sil"]; cmki = io["cmask"]; skpi = io["skip_t"]
    wouti = io["wout_t"]; terni = io["tern_t"]; l1bi = io["l1bias_t"]
    asci = io["act_scale"]; yout = io["y"]; yg_h = io["yg_scratch"]
    X = sb["X"]; H = sb["H"]; WT = sb["WT"]
    VSP = sb["VSP"]; PAB = sb["PAB"]
    VINALL = sb["VINALL"]; VIN = sb["VIN"]; VSW = sb["VSW4"]; PMA = sb["PMA"]
    MB = sb["MB"]
    G2L = sb["G2L"]; YTL = sb["YTL"]; VL = sb["VL"]; GL = sb["GL"]
    YGT = sb["YGT"]
    ca = sb["ca"]; SKP = sb["SKP"]; scl = sb["scl"]; bcol = sb["bcol"]
    pyc = ps["PYC"]
    pden = ps["P512a"]
    psc = ps["PSC"]

    if fused:
        vsrc = io["vsp_out"].ap().rearrange("(cc pt pp) dd -> pp cc pt dd", pp=128, pt=16)
        for i in range(4):
            p.dma("Y", VSP[i][:, :].rearrange("p (cc pt dd) -> p cc pt dd", cc=2, dd=64),
                  vsrc[:, 2 * i:2 * i + 2])
        p.dma("Y", PAB[:, :].rearrange("p (cc k) -> p cc k", cc=8),
              io["pab_out"].ap().rearrange("(cc pp) k -> pp cc k", cc=8))
    else:
        p.dma("Y", VSP[0][:, :], io["vsp_all"].ap()[:, 0:2048])
        p.dma("Y", VSP[1][:, :], io["vsp_all"].ap()[:, 2048:4096])
        p.dma("Y", VSP[2][:, :], io["vsp_all"].ap()[:, 4096:6144])
        p.dma("Y", VSP[3][:, :], io["vsp_all"].ap()[:, 6144:8192])
        p.dma("Y", PAB[:, :], io["pab_all"].ap()[:, :])
    p.dma("Y", SKP[:, :], skpi.ap()[:, :])
    p.dma("Y", sb["lnrows"][0:1, 0:8], cmki.ap()[:, :])
    p.dma("Y", sb["scl"][0:1, 0:1], asci.ap()[:, :])
    p.dma("Y", bcol[:, :], l1bi.ap()[:, :])
    p.op("V", "memset", sb["onesf"], [], sb["onesf"][:, :], 1.0)
    p.op("V", "memset", sb["onesr"], [], sb["onesr"][:, :], 1.0)
    p.op("V", "memset", sb["epsr"], [], sb["epsr"][:, :], 1e-5)
    p.op("T", "matmul", psc, [sb["onesr"], sb["lnrows"]],
         psc[:, 0:8], sb["onesr"][:, :], sb["lnrows"][0:1, 0:8], start=True, stop=True)
    p.op("V", "tensor_copy", MB, [psc], MB[:, :], psc[:, 0:8])

    for pt in range(16):
        p.op("V", "memset", VIN, [], VIN[:, :], 0.0)
        for cc in range(NCORE):
            gcol = (cc * 16 + pt) * 64
            vt_ = VSP[gcol // 2048]
            vsl = vt_[:, gcol % 2048:gcol % 2048 + 64]
            psl = PAB[:, (cc * 16 + pt) * 2:(cc * 16 + pt) * 2 + 2]
            mcol = MB[:, cc:cc + 1]
            p.op("V", "tensor_single_scalar", PMA, [PAB], PMA[:, 0:1], psl[:, 0:1], 1.0, ALU.subtract)
            p.op("V", "tensor_mul", PMA, [PMA, MB], PMA[:, 0:1], PMA[:, 0:1], mcol)
            p.op("V", "tensor_single_scalar", PMA, [PMA], PMA[:, 0:1], PMA[:, 0:1], 1.0, ALU.add)
            p.op("V", "tensor_mul", PMA, [PAB, MB], PMA[:, 1:2], psl[:, 1:2], mcol)
            p.op("S", "activation", VSW, [VIN], VSW[0:32, :], VIN[32:64, :], AF.Copy)
            p.op("S", "activation", VSW, [VIN], VSW[32:64, :], VIN[0:32, :], AF.Copy)
            p.op("S", "activation", VSW, [VIN], VSW[64:96, :], VIN[96:128, :], AF.Copy)
            p.op("S", "activation", VSW, [VIN], VSW[96:128, :], VIN[64:96, :], AF.Copy)
            p.op("V", "tensor_scalar", VIN, [VIN, PMA], VIN[:, :], VIN[:, :],
                 PMA[:, 0:1], None, ALU.mult)
            p.op("V", "scalar_tensor_tensor", VIN, [VSW, PMA, VIN],
                 VIN[:, :], VSW[:, :], PMA[:, 1:2], VIN[:, :], ALU.mult, ALU.add)
            p.op("V", "scalar_tensor_tensor", VIN, [vt_, MB, VIN],
                 VIN[:, :], vsl, mcol, VIN[:, :], ALU.mult, ALU.add)
        p.op("V", "tensor_copy", VINALL, [VIN], VINALL[:, pt * 64:(pt + 1) * 64], VIN[:, :])

    for pt in range(16):
        rows = slice(pt * 128, (pt + 1) * 128)
        p.dma("Y", G2L[:, :], g2i.ap()[rows, :])
        p.dma("Y", YTL[:, :], yti.ap()[rows, :])
        p.dma("G", VL[:, :], vali.ap()[rows, :])
        p.dma("G", GL[:, :], gati.ap()[rows, :])
        for jj in range(TC // C1):
            for par in range(2):
                r64 = slice(64 * par, 64 * par + 64)
                p.op("T", "matmul", pyc, [VINALL, G2L],
                     pyc[0:64, par * 256:(par + 1) * 256],
                     VINALL[r64, pt * 64:(pt + 1) * 64],
                     G2L[r64, jj * C1:(jj + 1) * C1], start=True, stop=True)
            p.op("V", "tensor_add", YTL, [pyc, YTL],
                 YTL[0:64, jj * C1:(jj + 1) * C1], pyc[0:64, 0:256],
                 YTL[0:64, jj * C1:(jj + 1) * C1])
            p.op("V", "tensor_add", YTL, [pyc, YTL],
                 YTL[64:128, jj * C1:(jj + 1) * C1], pyc[0:64, 256:512],
                 YTL[64:128, jj * C1:(jj + 1) * C1])
        p.op("V", "tensor_scalar", ca, [VL, SKP], ca[:, :], VL[:, :],
             SKP[:, pt:pt + 1], None, ALU.mult)
        p.op("V", "tensor_add", ca, [ca, YTL], ca[:, :], ca[:, :], YTL[:, :])
        p.op("V", "tensor_mul", GL, [ca, GL], GL[:, :], ca[:, :], GL[:, :])
        p.dma("Y", yg_h.ap()[rows, :], GL[:, :])

    for i in range(NDT):
        p.dma("Y", X[i][:, :], x4i.ap()[i * 128:(i + 1) * 128, :])
    for i in range(NDT):
        p.dma("G", WT[i][:, :], wouti.ap()[(2 * i) * 128:(2 * i + 1) * 128, :])
    _wt2host = [G2L, YTL, VL, GL]
    WT2 = [_wt2host[i // 2][:, (i % 2) * 1024:(i % 2) * 1024 + 1024] for i in range(NDT)]
    for i in range(NDT):
        p.dma("G", WT2[i], wouti.ap()[(2 * i + 1) * 128:(2 * i + 2) * 128, :])
    for tn in range(NTN):
        c0 = tn * TN
        for ct in range(16):
            p.dma("S", YGT[ct][:, :], yg_h.ap()[ct * 128:(ct + 1) * 128, c0:c0 + TN])
        for do in range(NDT):
            for ct in range(16):
                if ct % 2 == 0:
                    w = WT[ct // 2][:, do * 128:(do + 1) * 128]
                    wd = WT[ct // 2]
                else:
                    w = WT2[ct // 2][:, do * 128:(do + 1) * 128]
                    wd = WT2[ct // 2]
                p.op("T", "matmul", pden, [wd, YGT[ct]],
                     pden[:, :], w, YGT[ct][:, :],
                     start=(ct == 0), stop=(ct == 15))
            p.op("V", "tensor_add", X[do], [pden, X[do]],
                 X[do][:, c0:c0 + TN], pden[:, :], X[do][:, c0:c0 + TN])

    p.op("V", "tensor_single_scalar", scl, [scl], scl[0:1, 0:1], scl[0:1, 0:1], 1e-5, ALU.max)
    p.op("V", "reciprocal", scl, [scl], scl[0:1, 1:2], scl[0:1, 0:1])
    p.op("T", "matmul", psc, [sb["onesr"], scl],
         psc[:, 0:2], sb["onesr"][:, :], scl[0:1, 0:2], start=True, stop=True)
    p.op("V", "tensor_copy", scl, [psc], scl[:, :], psc[:, 0:2])
    emit_ln_quant(p, X, H, sb, ps)
    for i in range(NDT):
        p.dma("Y", sb["sq"][:, 0:1024].bitcast(I32), terni.ap()[i * 128:(i + 1) * 128, :])
        p.op("V", "tensor_copy", ca, [sb["sq"]], ca[:, 0:1024], sb["sq"][:, 0:1024].bitcast(I32))
        p.op("V", "tensor_single_scalar", WT[i], [ca], WT[i][:, :], ca[:, 0:1024], 1.0, ALU.subtract)
    for do in range(NDT):
        for tn in range(NTN):
            c0 = tn * TN
            for di in range(NDT):
                p.op("T", "matmul", pden, [WT[di], H[di]],
                     pden[:, :], WT[di][:, do * 128:(do + 1) * 128], H[di][:, c0:c0 + TN],
                     start=(di == 0), stop=(di == NDT - 1))
            p.op("V", "tensor_scalar", ca, [pden, scl, bcol],
                 ca[:, 0:TN], pden[:, :], scl[:, 0:1], bcol[:, do:do + 1],
                 ALU.mult, ALU.add)
            p.op("V", "scalar_tensor_tensor", X[do], [ca, X[do]],
                 X[do][:, c0:c0 + TN], ca[:, 0:TN], 0.5, X[do][:, c0:c0 + TN],
                 ALU.mult, ALU.add)
    for i in range(NDT):
        p.dma("Y", yout.ap()[i * 128:(i + 1) * 128, :], X[i][:, :])


def build_launch4(nc):
    io = {}
    for nm, shp, dt, kind in [
            ("x4", [D, TC], F32, "ExternalInput"),
            ("yT_bf", [DI, TC], BF16, "ExternalInput"),
            ("g2c_bf", [DI, TC], BF16, "ExternalInput"),
            ("value_bf", [DI, TC], BF16, "ExternalInput"),
            ("gate_sil", [DI, TC], BF16, "ExternalInput"),
            ("vsp_all", [128, NCORE * 16 * 64], F32, "ExternalInput"),
            ("pab_all", [128, NCORE * 16 * 2], F32, "ExternalInput"),
            ("cmask", [1, 8], F32, "ExternalInput"),
            ("skip_t", [128, 16], F32, "ExternalInput"),
            ("wout_t", [DI, D], F32, "ExternalInput"),
            ("tern_t", [D, D], I32, "ExternalInput"),
            ("l1bias_t", [128, 8], F32, "ExternalInput"),
            ("act_scale", [1, 1], F32, "ExternalInput"),
            ("y", [D, TC], F32, "ExternalOutput")]:
        io[nm] = nc.dram_tensor(nm, shp, dt, kind=kind)
    io["yg_scratch"] = nc.dram_tensor("yg_scratch", [DI, TC], BF16, kind="Internal")
    p = Prog(nc)
    sb = {}
    ps = {}
    S = std_alloc(nc, sb, ps)
    sb["VSP"] = [S(f"VSPt{i}", [128, 2048], F32) for i in range(4)]
    S("PAB", [128, NCORE * 16 * 2], F32)
    S("VINALL", [128, 16 * 64], BF16)
    S("VIN", [128, 64], F32)
    S("VSW4", [128, 64], F32)
    S("PMA", [128, 2], F32)
    S("MB", [128, 8], F32)
    S("G2L", [128, TC], BF16)
    S("YTL", [128, TC], BF16)
    S("VL", [128, TC], BF16)
    S("GL", [128, TC], BF16)
    sb["YGT"] = [S(f"YGT{i}", [128, TN], BF16) for i in range(16)]
    S("ca", [128, TC], F32)
    S("SKP", [128, 16], F32)
    S("scl", [128, 2], F32)
    S("bcol", [128, 8], F32)
    ps["PYC"] = nc.alloc_psum_tensor("PYC", [64, 512], F32)
    ps["P512a"] = nc.alloc_psum_tensor("P512a", [128, TN], F32)
    ps["P1024"] = nc.alloc_psum_tensor("P1024", [128, 1024], F32)
    ps["PSC"] = nc.alloc_psum_tensor("PSC", [128, 8], F32)
    emit_common_init(p, sb)
    emit_sec4(p, sb, ps, io)
    p.finish()
    p.build()
    return nc


def run_launch4(inputs, r2, r3):
    nc = _get_nc("l4", build_launch4)
    f32 = np.float32
    wout_t = np.ascontiguousarray(np.asarray(inputs["L1_attn_W_out"], f32).T)
    tern_t = np.ascontiguousarray(np.asarray(inputs["L1_ffn_ternary_weight"], np.int32).T)
    l1bias_t = np.ascontiguousarray(np.asarray(inputs["L1_ffn_bias"], f32).reshape(8, 128).T)
    act_scale = np.asarray(inputs["L1_ffn_act_scale"], f32).reshape(1, 1)
    skip_t = np.ascontiguousarray(np.asarray(inputs["L1_attn_skip"], f32).reshape(16, 128).T)
    vsp = np.zeros((128, NCORE * 16 * 64), f32)
    pab = np.zeros((128, NCORE * 16 * 2), f32)
    for cc in range(NCORE):
        v = np.asarray(r3[cc]["vspan"])
        pspan = np.asarray(r3[cc]["pspan"])
        for pt in range(16):
            vsp[:, (cc * 16 + pt) * 64:(cc * 16 + pt) * 64 + 64] = v[pt * 128:(pt + 1) * 128, :]
            g, q = pt // 2, pt % 2
            he = 32 * (2 * q)
            ho = 32 * (2 * q + 1)
            pa = np.concatenate([pspan[he:he + 32, g * 4 + 0], pspan[he:he + 32, g * 4 + 3],
                                 pspan[ho:ho + 32, g * 4 + 0], pspan[ho:ho + 32, g * 4 + 3]])
            pb = np.concatenate([pspan[he:he + 32, g * 4 + 1], pspan[he:he + 32, g * 4 + 2],
                                 pspan[ho:ho + 32, g * 4 + 1], pspan[ho:ho + 32, g * 4 + 2]])
            pab[:, (cc * 16 + pt) * 2] = pa
            pab[:, (cc * 16 + pt) * 2 + 1] = pb
    in_maps = []
    for core in range(NCORE):
        m = {"x4": np.asarray(r2[core]["x4"]),
             "yT_bf": np.asarray(r3[core]["yT_bf"]),
             "g2c_bf": np.asarray(r3[core]["g2c_bf"]),
             "value_bf": np.asarray(r2[core]["value_bf"]),
             "gate_sil": np.asarray(r2[core]["gate_sil"]),
             "vsp_all": vsp, "pab_all": pab,
             "skip_t": skip_t, "wout_t": wout_t, "tern_t": tern_t,
             "l1bias_t": l1bias_t, "act_scale": act_scale}
        msk = np.zeros((1, 8), np.float32)
        for cc in range(NCORE):
            if cc // 4 == core // 4 and cc < core:
                msk[0, cc] = 1.0
        m["cmask"] = msk
        in_maps.append(m)
    res = run_bass_kernel_spmd(nc, in_maps, core_ids=list(range(NCORE)))
    return res.results


def build_fused(nc):
    io = {}
    for nm, shp, dt in [("x", [D, TC], F32), ("wq_t", [D, D], F32), ("wk_t", [D, D], F32),
                        ("wv_t", [D, D], F32), ("wo_t", [D, D], F32), ("l0w_t", [D, D], F32),
                        ("disha", [64, D], F32), ("cc", [128, TC], F32), ("ssx", [128, TC], F32),
                        ("gsm", [128, 64], F32), ("triu", [128, 128], F32),
                        ("cmask", [1, 8], F32), ("mprev", [1, 8], F32),
                        ("tern_t", [D, D], I32), ("l1bias_t", [128, 8], F32),
                        ("act_scale", [1, 1], F32), ("win_t", [D, 4160], F32),
                        ("dww", [DI, 4], F32), ("dwb", [DI, 1], F32), ("dtb64", [64, 1], F32),
                        ("wbc_t", [DI, 4096], F32), ("acol_log", [128, 8], F32),
                        ("seld", [64, 1024], F32),
                        ("gs32", [128, 4], F32), ("sel4", [4, 128], F32),
                        ("wout_t", [DI, D], F32), ("skip_t", [128, 16], F32)]:
        io[nm] = nc.dram_tensor(nm, shp, dt, kind="ExternalInput")
    io["y"] = nc.dram_tensor("y", [D, TC], F32, kind="ExternalOutput")
    for nm, shp, dt in [("x1", [D, TC], F32), ("num_part", [TC, D], F32),
                        ("den_part", [128, NCH * 16], F32), ("x4", [D, TC], F32),
                        ("value_bf", [DI, TC], BF16), ("gate_sil", [DI, TC], BF16),
                        ("dtkd", [64, 2 * TC], F32), ("halo_first", [128, 48], F32),
                        ("yT_bf", [DI, TC], BF16), ("g2c_bf", [DI, TC], BF16),
                        ("yg_scratch", [DI, TC], BF16)]:
        io[nm] = nc.dram_tensor(nm + "_i", shp, dt, kind="Internal")
    cc_kv_in = nc.dram_tensor("cc_kv_in", [64, 1040], F32, kind="Internal")
    cc_kv_out = nc.dram_tensor("cc_kv_out", [512, 1040], F32, kind="Internal", addr_space="Shared")
    cc_halo_in = nc.dram_tensor("cc_halo_in", [128, 48], F32, kind="Internal")
    cc_halo_out = nc.dram_tensor("cc_halo_out", [1024, 48], F32, kind="Internal", addr_space="Shared")
    cc_vsp_in = nc.dram_tensor("cc_vsp_in", [DI, 64], F32, kind="Internal")
    cc_vsp_out = nc.dram_tensor("cc_vsp_out", [NCORE * DI, 64], F32, kind="Internal", addr_space="Shared")
    cc_pab_in = nc.dram_tensor("cc_pab_in", [128, 32], F32, kind="Internal")
    cc_pab_out = nc.dram_tensor("cc_pab_out", [1024, 32], F32, kind="Internal", addr_space="Shared")
    io["halo_last"] = cc_halo_in
    io["halo_out"] = cc_halo_out
    io["vspan"] = cc_vsp_in
    io["pab_in"] = cc_pab_in
    io["vsp_out"] = cc_vsp_out
    io["pab_out"] = cc_pab_out
    io["kvloc"] = cc_kv_in.ap()[0:64, 0:1024]
    io["ksloc"] = cc_kv_in.ap()[0:64, 1024:1040]
    io["kv_src"] = lambda cc: (cc_kv_out.ap()[cc * 64:cc * 64 + 64, 0:1024],
                               cc_kv_out.ap()[cc * 64:cc * 64 + 64, 1024:1040])
    io["idn_h"] = nc.inline_tensor(np.eye(128, dtype=ml_dtypes.bfloat16), "idn_const")
    rm = np.ones((128, HS), np.float32)
    rm[:, 0::CL] = 0.0
    io["rm_h"] = nc.inline_tensor(rm, "resetm_const")

    p = Prog(nc)
    sb = {}
    ps = {}
    S = std_alloc(nc, sb, ps)
    X = sb["X"]
    H = sb["H"]
    sb["QP"] = [S(f"QP{i}", [128, TC], BF16) for i in range(NDT)]
    QP = sb["QP"]
    SCR = S("SCR", [128, 16384], BF16)
    SCRF = SCR[:, :].bitcast(F32)
    sb["KP"] = [SCR[:, i * 2048:(i + 1) * 2048] for i in range(NDT)]
    sb["zr"] = SCR[0:1, 0:1024]
    sb["zc"] = SCR[0:1, 1024:1152]
    sb["pooled"] = SCR[0:64, 0:2048]
    alloc_sec1_small(nc, S, sb, ps, p)
    sb["vrp"] = SCRF[:, 0:2051]
    sb["ca"] = SCRF[:, 2051:4099]
    sb["nstg"] = SCRF[:, 4099:5123]
    sb["praw"] = SCRF[0:64, 5123:7171]
    sb["kvacc"] = SCRF[0:64, 5123:6163]
    sb["gstg"] = SCR[:, 8198:10246]
    sb["ybf"] = SCR[:, 14342:15366]
    alloc_sec2_small(nc, S, sb, ps)
    ps["PTRASH"] = ps["PS"]
    ps["PSC"] = ps["PS"]
    sb["VAL"] = [H[i] for i in range(NDT)] + [QP[i] for i in range(NDT)]
    sb["WBC"] = [sb["WT"][i // 2][:, (i % 2) * 512:(i % 2) * 512 + 512] for i in range(16)]
    _pl = ([X[i][:, 0:1024] for i in range(NDT)] +
           [X[i][:, 1024:2048] for i in range(NDT)] +
           [S("PLx0", [128, HS], F32)])
    _pl.append(_pl[0])
    sb["PL"] = _pl
    sb["TRIU"] = sb["TRI"]
    sb["RESETM"] = sb["sq"][:, 0:HS]
    sb["YTS"] = [SCR[:, 0:2048], SCR[:, 2048:4096]]
    sb["GcP"] = [SCR[:, 4096:5120], SCR[:, 5120:6144]]
    sb["HcP"] = [SCR[:, 6144:7168], SCR[:, 7168:8192]]
    sb["G2S"] = [SCR[:, 8192:9216], SCR[:, 9216:10240]]
    sb["utk"] = SCR[:, 10240:10496]
    sb["hct"] = SCR[:, 10496:10752]
    sb["AMS"] = SCR[:, 10752:11136]
    sb["VSB"] = [SCR[:, 11136:11200], SCR[:, 11200:11264]]
    sb["DTF"] = SCRF[0:64, 5632:6656]
    sb["KDF"] = SCRF[0:64, 6656:7680]
    sb["bcs"] = sb["rb"]
    sb["sqs"] = sb["nb"]
    _vtf = sb["vtok"][:, :].bitcast(F32)
    sb["QP00"] = _vtf[:, 0:64]; sb["QP01"] = _vtf[:, 64:128]
    sb["QP10"] = _vtf[:, 128:192]; sb["QP11"] = _vtf[:, 192:256]
    sb["DQP"] = _vtf[:, 256:320]
    _ktf = sb["ktok"][:, :].bitcast(F32)
    sb["PCH"] = _ktf[:, 0:16]; sb["QPRE"] = _ktf[:, 16:32]; sb["QSPC"] = _ktf[:, 32:36]
    sb["VSW"] = _ktf[:, 64:128]
    sb["PAC"] = _ktf[:, 36:44]; sb["PBC2"] = _ktf[:, 44:52]
    sb["rns"] = sb["lntmp"][0:4, :]
    sb["hp6"] = _ktf[:, 52:58]
    sb["cstg"] = _ktf[:, 58:61]
    sb["VST"] = [_ktf[:, 128:192], _ktf[:, 192:256]]
    alloc_sec3_small(nc, S, sb, ps)
    sb["hpall"] = sb["lntmp"][:, 0:384]
    S("mbp", [128, 8], F32)
    S("pabst", [128, 32], F32)
    ps["PA"] = ps["PS"]
    ps["PY"] = ps["PKV"]
    ps["PW"] = ps["PKV"][0:64, 512:640]
    ps["PBC"] = ps["P512a"]
    ps["P4"] = ps["P1024"][0:4, 0:512]
    ps["PRN"] = ps["P1024"][:, 512:1024]
    sb["VSP"] = [X[0], X[1], X[2], X[3]]
    sb["PAB"] = _vtf[:, 0:256]
    sb["VINALL"] = sb["ktok"][:, :]
    sb["VIN"] = _vtf[:, 256:320]
    sb["VSW4"] = _vtf[:, 320:384]
    sb["PMA"] = _vtf[:, 384:386]
    sb["MB"] = _vtf[:, 386:394]
    sb["SKP"] = _vtf[:, 394:410]
    sb["G2L"] = SCR[:, 8192:10240]
    sb["YTL"] = SCR[:, 10240:12288]
    sb["VL"] = SCR[:, 12288:14336]
    sb["GL"] = SCR[:, 14336:16384]
    sb["YGT"] = [SCR[:, i * 512:(i + 1) * 512] for i in range(16)]
    ps["PYC"] = ps["PKV"][0:64, 0:512]

    emit_common_init(p, sb)
    emit_sec1(p, sb, ps, io, fused=True)
    p.collective(cc_kv_in.ap()[:, :], cc_kv_out.ap()[:, :], "cc_kv_in", "cc_kv_out")
    p.barrier_all()
    emit_sec2(p, sb, ps, io, fused=True)
    p.collective(cc_halo_in.ap()[:, :], cc_halo_out.ap()[:, :], "cc_halo_in", "cc_halo_out")
    p.barrier_all()
    emit_sec3(p, sb, ps, io, fused=True)
    p.collective(cc_vsp_in.ap()[:, :], cc_vsp_out.ap()[:, :], "cc_vsp_in", "cc_vsp_out")
    p.collective(cc_pab_in.ap()[:, :], cc_pab_out.ap()[:, :], "cc_pab_in", "cc_pab_out")
    p.barrier_all()
    emit_sec4(p, sb, ps, io, fused=True)
    p.finish()
    p.build()
    return nc


def run_fused(inputs):
    nc = _get_nc("fused", build_fused)
    com = host_common(inputs)
    com2 = host_common2(inputs)
    com3 = host_common3(inputs)
    f32 = np.float32
    wout_t = np.ascontiguousarray(np.asarray(inputs["L1_attn_W_out"], f32).T)
    skip_t = np.ascontiguousarray(np.asarray(inputs["L1_attn_skip"], f32).reshape(16, 128).T)
    in_maps = []
    for core in range(NCORE):
        m = {}
        for k in ["wq_t", "wk_t", "wv_t", "wo_t", "l0w_t", "disha", "gsm", "triu"]:
            m[k] = com[k]
        for k in ["tern_t", "l1bias_t", "act_scale", "win_t", "dww", "dwb", "dtb64"]:
            m[k] = com2[k]
        for k in ["wbc_t", "acol_log", "seld", "gs32", "sel4"]:
            m[k] = com3[k]
        m["wout_t"] = wout_t
        m["skip_t"] = skip_t
        m.update(host_core(inputs, core))
        msk = np.zeros((1, 8), f32)
        mpv = np.zeros((1, 8), f32)
        for cci in range(NCORE):
            if cci // 4 == core // 4 and cci < core:
                msk[0, cci] = 1.0
            if cci == core - 1 and cci // 4 == core // 4:
                mpv[0, cci] = 1.0
        m["cmask"] = msk
        m["mprev"] = mpv
        in_maps.append(m)
    res = run_bass_kernel_spmd(nc, in_maps, core_ids=list(range(NCORE)))
    return res.results


def host_common(inputs):
    f32 = np.float32
    c = {}
    c["wq_t"] = np.ascontiguousarray(np.asarray(inputs["L0_attn_Wq"], f32).T)
    c["wk_t"] = np.ascontiguousarray(np.asarray(inputs["L0_attn_Wk"], f32).T)
    c["wv_t"] = np.ascontiguousarray(np.asarray(inputs["L0_attn_Wv"], f32).T)
    c["wo_t"] = np.ascontiguousarray(np.asarray(inputs["L0_attn_Wo"], f32).T)
    c["l0w_t"] = np.ascontiguousarray(np.asarray(inputs["L0_ffn_weight"], f32).T)
    c["disha"] = np.ascontiguousarray(np.asarray(inputs["L0_ffn_disha"], f32))
    c["gsm"] = np.vstack([np.eye(64, dtype=f32), np.eye(64, dtype=f32)])
    c["triu"] = np.triu(np.ones((128, 128), dtype=f32))
    return c


def host_core(inputs, core):
    f32 = np.float32
    b, t0 = core // 4, (core % 4) * TC
    d = {}
    x = np.asarray(inputs["L0_attn_x"], f32)
    d["x"] = np.ascontiguousarray(x[b, t0:t0 + TC, :].T)
    half = 32
    inv_freq = (10000.0 ** (-np.arange(half, dtype=f32) / half)).astype(f32)
    pos = np.arange(t0, t0 + TC, dtype=f32)
    ang = pos[None, :] * inv_freq[:, None]
    cos, sin = np.cos(ang).astype(f32), np.sin(ang).astype(f32)
    d["cc"] = np.ascontiguousarray(np.tile(cos, (4, 1)))
    d["ssx"] = np.ascontiguousarray(np.concatenate([sin, -sin, sin, -sin], axis=0))
    return d


_nc_cache = {}


def _get_nc(key, builder):
    if key not in _nc_cache:
        nc = bass.Bass("TRN2", target_bir_lowering=False)
        builder(nc)
        _nc_cache[key] = nc
    return _nc_cache[key]


def host_common2(inputs):
    f32 = np.float32
    c = {}
    c["wo_t"] = np.ascontiguousarray(np.asarray(inputs["L0_attn_Wo"], f32).T)
    c["l0w_t"] = np.ascontiguousarray(np.asarray(inputs["L0_ffn_weight"], f32).T)
    c["disha"] = np.ascontiguousarray(np.asarray(inputs["L0_ffn_disha"], f32))
    c["gsm"] = np.vstack([np.eye(64, dtype=f32), np.eye(64, dtype=f32)])
    c["tern_t"] = np.ascontiguousarray(np.asarray(inputs["L1_ffn_ternary_weight"], np.int32).T)
    c["l1bias_t"] = np.ascontiguousarray(np.asarray(inputs["L1_ffn_bias"], f32).reshape(8, 128).T)
    c["act_scale"] = np.asarray(inputs["L1_ffn_act_scale"], f32).reshape(1, 1)
    c["win_t"] = np.ascontiguousarray(np.asarray(inputs["L1_attn_W_in"], f32).T)
    c["dww"] = np.ascontiguousarray(np.asarray(inputs["L1_attn_dw_weight"], f32))
    c["dwb"] = np.ascontiguousarray(np.asarray(inputs["L1_attn_dw_bias"], f32).reshape(DI, 1))
    dtb = np.zeros((64, 1), f32)
    dtb[0::2, 0] = np.asarray(inputs["L1_attn_dt_bias"], f32)
    c["dtb64"] = dtb
    return c


def run_launch2(inputs, r1):
    com = host_common2(inputs)
    nc = _get_nc("l2", build_launch2)
    kvall = np.concatenate([np.asarray(r1[c]["kvloc"]) for c in range(NCORE)], axis=0)
    ksall = np.concatenate([np.asarray(r1[c]["ksloc"]) for c in range(NCORE)], axis=0)
    in_maps = []
    for core in range(NCORE):
        m = dict(com)
        m["x1"] = np.asarray(r1[core]["x1"])
        m["qphi"] = np.asarray(r1[core]["qphi"])
        m["num_part"] = np.asarray(r1[core]["num_part"])
        m["den_part"] = np.asarray(r1[core]["den_part"])
        m["kvall"] = kvall
        m["ksall"] = ksall
        msk = np.zeros((1, 8), np.float32)
        for cc in range(NCORE):
            if cc // 4 == core // 4 and cc < core:
                msk[0, cc] = 1.0
        m["cmask"] = msk
        in_maps.append(m)
    res = run_bass_kernel_spmd(nc, in_maps, core_ids=list(range(NCORE)))
    return res.results


def host_common3(inputs):
    f32 = np.float32
    c = {}
    c["wbc_t"] = np.ascontiguousarray(np.asarray(inputs["L1_attn_W_bc"], f32).T)
    c["dww"] = np.ascontiguousarray(np.asarray(inputs["L1_attn_dw_weight"], f32))
    c["dwb"] = np.ascontiguousarray(np.asarray(inputs["L1_attn_dw_bias"], f32).reshape(DI, 1))
    alog = np.asarray(inputs["L1_attn_A_log"], f32)
    acol = np.zeros((128, 8), f32)
    for g in range(8):
        for pp in range(128):
            acol[pp, g] = alog[4 * g + pp // 32, pp % 32]
    c["acol_log"] = acol
    seld = np.zeros((64, 1024), f32)
    selk = np.zeros((64, 1024), f32)
    for g in range(8):
        for pp in range(128):
            h = 4 * g + pp // 32
            seld[2 * h, g * 128 + pp] = 1.0
            selk[2 * h + 1, g * 128 + pp] = 1.0
    c["seld"] = seld
    c["selk"] = selk
    gs32 = np.zeros((128, 4), f32)
    for pp in range(128):
        gs32[pp, pp // 32] = 1.0
    c["gs32"] = gs32
    c["sel4"] = np.ascontiguousarray(gs32.T)
    c["triu"] = np.triu(np.ones((128, 128), dtype=f32))
    return c


def run_launch3(inputs, r2):
    com = host_common3(inputs)
    com.pop("selk", None)
    nc = _get_nc("l3", build_launch3)
    in_maps = []
    for core in range(NCORE):
        m = dict(com)
        m["value_bf"] = np.asarray(r2[core]["value_bf"])
        m["dtkd"] = np.asarray(r2[core]["dtkd"])
        m["halo_first"] = np.asarray(r2[core]["halo_first"])
        if core % 4 == 0:
            m["halo_prev"] = np.zeros((128, 48), np.float32)
        else:
            m["halo_prev"] = np.asarray(r2[core - 1]["halo_last"])
        in_maps.append(m)
    res = run_bass_kernel_spmd(nc, in_maps, core_ids=list(range(NCORE)))
    return res.results


def run_launch1(inputs):
    com = host_common(inputs)
    nc = _get_nc("l1", build_launch1)
    in_maps = []
    for core in range(NCORE):
        m = dict(com)
        del m["wo_t"]
        m.update(host_core(inputs, core))
        in_maps.append(m)
    res = run_bass_kernel_spmd(nc, in_maps, core_ids=list(range(NCORE)))
    return res.results


def kernel_fused(**inputs):
    res = run_fused(inputs)
    out = np.zeros((B, T, D), np.float32)
    for core in range(NCORE):
        b, t0 = core // 4, (core % 4) * TC
        out[b, t0:t0 + TC, :] = np.asarray(res[core]["y"]).T
    return out


def kernel(**inputs):
    r1 = run_launch1(inputs)
    r2 = run_launch2(inputs, r1)
    r3 = run_launch3(inputs, r2)
    r4 = run_launch4(inputs, r2, r3)
    out = np.zeros((B, T, D), np.float32)
    for core in range(NCORE):
        b, t0 = core // 4, (core % 4) * TC
        out[b, t0:t0 + TC, :] = np.asarray(r4[core]["y"]).T
    return out
